# version 03
# baseline (speedup 1.0000x reference)
import numpy as np
from contextlib import ExitStack

import concourse.bass as bass
import concourse.mybir as mybir
from concourse.bass_utils import run_bass_kernel_spmd

F32 = mybir.dt.float32
BF16 = mybir.dt.bfloat16
AF = mybir.ActivationFunctionType
ALU = mybir.AluOpType
AX = mybir.AxisListType.X

_DEBUG = False
_RACE_DETECT = False

P = 128
NCORES = 8
SEQ = 4096
D = 2048
T0, T1, TOWN = 2304, 1792, 1024
NH, HD = 16, 128
NKV = 4
WIN, FRONT = 512, 256
NE, HE, HS = 16, 1024, 4096
NEG = -1.0e30

ENGS = ("sy", "gp", "ve", "sc", "te")
DMASEM = {"sy": "dS", "gp": "dG", "sc": "dA"}
SEMS = ("dS", "dG", "dA", "pT", "pV", "pS", "pP")


def ttiles(total, step=512):
    out, o = [], 0
    while o < total:
        w = min(step, total - o)
        out.append((o, w))
        o += w
    return out


AUTOSER = {"ve": "pV", "sc": "pS", "gp": "pP"}


class Emit:

    def __init__(self, nc, sems):
        self.nc = nc
        self.sems = sems
        self.q = {e: [] for e in ENGS}
        self.cnt = {s: 0 for s in SEMS}
        self.waited = {e: {} for e in ENGS}
        self.last = {e: None for e in ENGS}

    def wait(self, eng, tok):
        if tok is None:
            return
        s, v = tok
        if v <= 0 or self.waited[eng].get(s, 0) >= v:
            return
        self.waited[eng][s] = v
        h = self.sems[s]
        self.q[eng].append(lambda E, h=h, v=v: E.wait_ge(h, v))

    def op(self, eng, fn, waits=(), inc=None):
        for t in waits:
            self.wait(eng, t)
        if eng in AUTOSER:
            self.wait(eng, self.last[eng])
            if inc is None:
                inc = AUTOSER[eng]
        if inc is None:
            self.q[eng].append(fn)
            return None
        amt = 16 if inc in ("dS", "dG", "dA") else 1
        self.cnt[inc] += amt
        h = self.sems[inc]
        self.q[eng].append(lambda E, fn=fn, h=h, amt=amt: fn(E).then_inc(h, amt))
        tok = (inc, self.cnt[inc])
        if eng in AUTOSER and inc not in ("dS", "dG", "dA"):
            self.last[eng] = tok
        return tok

    def dma(self, eng, out, in_, waits=()):
        return self.op(eng, lambda E, o=out, i=in_: E.dma_start(out=o, in_=i),
                       waits=waits, inc=DMASEM[eng])

    def tok(self, sem):
        return (sem, self.cnt[sem])

    def snapshot(self):
        return [(s, c) for s, c in self.cnt.items() if c > 0]

    def barrier(self):
        toks = self.snapshot()
        for e in ENGS:
            for t in toks:
                self.wait(e, t)



def build_program():
    nc = bass.Bass("TRN2", target_bir_lowering=False,
                   detect_race_conditions=_RACE_DETECT)

    def din(name, shape, dt=F32):
        return nc.dram_tensor(name, list(shape), dt, kind="ExternalInput").ap()

    x0 = din("x0", (T0, D))
    m1m = din("m1m", (1536, 1024))
    m1f = din("m1f", (FRONT, FRONT))
    m3 = din("m3", (TOWN, T1))
    qc1 = din("qc1", (P, T1)); qs1 = din("qs1", (P, T1))
    kc1 = din("kc1", (P, T0)); ks1 = din("ks1", (P, T0))
    qc3 = din("qc3", (P, TOWN)); qs3 = din("qs3", (P, TOWN))
    kc3 = din("kc3", (P, T1)); ks3 = din("ks3", (P, T1))
    seli = din("sel", (16, D))
    wqa = din("wqa", (D, D)); wka = din("wka", (D, D))
    wva = din("wva", (D, D)); woa = din("woa", (D, D))
    wq3 = din("wq3", (NH, 16, P, HD))
    wk3 = din("wk3", (NKV, 16, P, HD))
    wv3 = din("wv3", (D, NKV * HD))
    wo3 = din("wo3", (D, D))
    wr = din("wr", (16, P, 16))
    w13 = din("w13", (NE, 8, 2, 16, P, P))
    ws13 = din("ws13", (32, 2, 16, P, P))
    we2 = din("we2", (NE, HE, D))
    ws2 = din("ws2", (HS, D))

    y = nc.dram_tensor("y", [TOWN, D], F32, kind="ExternalOutput").ap()

    kd = "ExternalOutput" if _DEBUG else "Internal"
    q1d = nc.dram_tensor("q1d", [D, T1], BF16, kind=kd).ap()
    k1d = nc.dram_tensor("k1d", [D, T0], BF16, kind=kd).ap()
    v1d = nc.dram_tensor("v1d", [T0, D], BF16, kind=kd).ap()
    aod = nc.dram_tensor("aod", [D, T1], BF16, kind=kd).ap()
    X1d = nc.dram_tensor("X1d", [T1, D], F32, kind=kd).ap()
    X2d = nc.dram_tensor("X2d", [T1, D], F32, kind=kd).ap()
    X3d = nc.dram_tensor("X3d", [TOWN, D], F32, kind=kd).ap()
    cwdbg = nc.dram_tensor("cwdbg", [4, 16, T1], BF16, kind=kd).ap() if _DEBUG else None

    es = ExitStack()
    with es:
        sems = {s: es.enter_context(nc.semaphore(s)) for s in SEMS}
        em = Emit(nc, sems)

        identb = es.enter_context(nc.sbuf_tensor("identb", [P, P], BF16))
        identf = es.enter_context(nc.sbuf_tensor("identf", [P, P], F32))
        stats = es.enter_context(nc.sbuf_tensor("stats", [P, 64], F32))
        selsb = es.enter_context(nc.sbuf_tensor("selsb", [16, D], BF16))
        cwT = es.enter_context(nc.sbuf_tensor("cwT", [16, T1], BF16))
        wrsb = es.enter_context(nc.sbuf_tensor("wrsb", [P, 16 * 16], F32))
        scr16 = es.enter_context(nc.sbuf_tensor("scr16", [P, 2048], BF16))
        scr32 = es.enter_context(nc.sbuf_tensor("scr32", [P, 2048], F32))

        pA = es.enter_context(nc.psum_tensor("pA", [P, 512], F32))
        pB = es.enter_context(nc.psum_tensor("pB", [P, 512], F32))
        pC = es.enter_context(nc.psum_tensor("pC", [P, 512], F32))
        pD = es.enter_context(nc.psum_tensor("pD", [P, 512], F32))
        pE = es.enter_context(nc.psum_tensor("pE", [P, 512], F32))
        pF = es.enter_context(nc.psum_tensor("pF", [P, 512], F32))
        pTb = es.enter_context(nc.psum_tensor("pTb", [P, 1024], BF16))
        pTf = es.enter_context(nc.psum_tensor("pTf", [P, 512], F32))

        def mk_ident(t):
            em.op("gp", lambda E: E.memset(t[:], 0.0))
            return em.op("gp", lambda E: E.affine_select(
                out=t[:], in_=t[:], compare_op=ALU.not_equal,
                fill=1.0, base=0, pattern=[[-1, P]],
                channel_multiplier=1))
        mk_ident(identb)
        tok_id = mk_ident(identf)
        em.dma("gp", selsb[:], seli)
        em.dma("sy", wrsb[:].rearrange("p (c e) -> p c e", e=16),
               wr.rearrange("c p e -> p c e"))
        em.wait("te", tok_id)

        def ln_tile(xin_ap, sqbuf, slot, xln_bf_ap, xin_tok, war=(), xln32_ap=None,
                    sq_war=(), war32=()):
            c0 = slot * 8
            tsq = em.op("sc", lambda E: E.activation(sqbuf, xin_ap, AF.Square),
                        waits=[xin_tok] + list(sq_war), inc="pS")
            em.op("ve", lambda E: E.reduce_sum(stats[:, c0:c0 + 1], xin_ap, axis=AX),
                  waits=[xin_tok])
            tsqr = em.op("ve", lambda E: E.reduce_sum(stats[:, c0 + 1:c0 + 2], sqbuf,
                                                      axis=AX),
                         waits=[tsq], inc="pV")
            em.op("ve", lambda E: E.tensor_scalar_mul(stats[:, c0 + 2:c0 + 3],
                                                      stats[:, c0:c0 + 1], 1.0 / D))
            em.op("ve", lambda E: E.tensor_mul(stats[:, c0 + 3:c0 + 4],
                                               stats[:, c0 + 2:c0 + 3],
                                               stats[:, c0 + 2:c0 + 3]))
            tv = em.op("ve", lambda E: E.scalar_tensor_tensor(
                stats[:, c0 + 4:c0 + 5], stats[:, c0 + 1:c0 + 2], 1.0 / D,
                stats[:, c0 + 3:c0 + 4], ALU.mult, ALU.subtract), inc="pV")
            tve = em.op("ve", lambda E: E.tensor_scalar_add(
                stats[:, c0 + 4:c0 + 5], stats[:, c0 + 4:c0 + 5], 1e-5), inc="pV")
            tsd = em.op("sc", lambda E: E.activation(stats[:, c0 + 5:c0 + 6],
                                                     stats[:, c0 + 4:c0 + 5],
                                                     AF.Sqrt),
                        waits=[tve], inc="pS")
            tr = em.op("ve", lambda E: E.reciprocal(stats[:, c0 + 6:c0 + 7],
                                                    stats[:, c0 + 5:c0 + 6]),
                       waits=[tsd], inc="pV")
            toks = []
            t1 = em.op("ve", lambda E: E.tensor_scalar(
                xln_bf_ap, xin_ap, stats[:, c0 + 2:c0 + 3], stats[:, c0 + 6:c0 + 7],
                ALU.subtract, ALU.mult), waits=[tr] + list(war), inc="pV")
            toks.append(t1)
            if xln32_ap is not None:
                t2 = em.op("ve", lambda E: E.tensor_scalar(
                    xln32_ap, xin_ap, stats[:, c0 + 2:c0 + 3], stats[:, c0 + 6:c0 + 7],
                    ALU.subtract, ALU.mult), waits=list(war32), inc="pV")
                toks.append(t2)
            toks.append(tsqr)
            return toks

        def transpose16(src_sb, src_col0, dst_t, dst_stride, dst_col,
                        src_tok, psum, psum_free, use_bf, width=2048):
            ident = identb[:] if use_bf else identf[:]
            per = 8 if use_bf else 4
            nch = width // P
            ev_tok = None
            for r0 in range(0, nch, per):
                n = min(per, nch - r0)
                last = None
                for i in range(n):
                    c = r0 + i
                    w = [src_tok, psum_free] if i == 0 else []
                    last = em.op("te", lambda E, c=c, i=i, ident=ident: E.transpose(
                        psum[:, i * P:(i + 1) * P],
                        src_sb[:, src_col0 + c * P:src_col0 + (c + 1) * P], ident),
                        waits=w, inc=("pT" if i == n - 1 else None))
                dview = dst_t[:].rearrange("p (c t) -> p c t", t=dst_stride)[
                    :, r0:r0 + n, dst_col:dst_col + P]
                sview = psum[:, 0:n * P].rearrange("p (c t) -> p c t", t=P)
                ev_tok = em.op("sc", lambda E, d=dview, s=sview: E.copy(d, s),
                               waits=[last], inc="pS")
                psum_free = ev_tok
            return ev_tok, psum_free

        s0 = ExitStack()
        xln1 = s0.enter_context(nc.sbuf_tensor("xln1", [P, 16 * T0], BF16))
        wful = s0.enter_context(nc.sbuf_tensor("wful", [P, 16 * D], BF16))
        tabs = s0.enter_context(nc.sbuf_tensor("tabs", [P, 2 * T0 + 2 * T1], BF16))
        xin2 = s0.enter_context(nc.sbuf_tensor("xin2", [P, 2 * D], F32))
        rsc = s0.enter_context(nc.sbuf_tensor("rsc", [P, 2 * 512], F32))
        qkout = s0.enter_context(nc.sbuf_tensor("qkout", [P, 2 * 512], BF16))

        KC0, KS0, QC0, QS0 = 0, T0, 2 * T0, 2 * T0 + T1
        em.dma("gp", tabs[:, KC0:KC0 + T0], kc1)
        em.dma("gp", tabs[:, KS0:KS0 + T0], ks1)
        em.dma("gp", tabs[:, QC0:QC0 + T1], qc1)
        em.dma("gp", tabs[:, QS0:QS0 + T1], qs1)

        def load_wfull(dst, wap, war=()):
            tk = None
            for g in range(4):
                src = wap.rearrange("(c p) o -> p c o", p=P)[:, 4 * g:4 * g + 4, :]
                dvw = dst[:].rearrange("p (c o) -> p c o", o=D)[:, 4 * g:4 * g + 4, :]
                tk = em.dma("gp", dvw, src, waits=(list(war) if g == 0 else []))
            return tk
        twq = load_wfull(wful, wqa)

        xin_free = [[], []]
        scr16_war = []
        sq_war = []
        pTb_free = None
        for t in range(T0 // P):
            sl = t % 2
            xv = xin2[:, sl * D:(sl + 1) * D]
            tin = em.dma("sy", xv, x0[t * P:(t + 1) * P, :], waits=xin_free[sl])
            toks = ln_tile(xv, scr32[:, 0:D], sl, scr16[:, 0:D], tin, war=scr16_war,
                           sq_war=sq_war)
            ev, pTb_free = transpose16(scr16, 0, xln1, T0, t * P,
                                       toks[0], pTb, pTb_free, True)
            xin_free[sl] = [toks[0]]
            scr16_war = [em.tok("pT")]
            sq_war = [toks[-1]]
        em.barrier()

        def rope_tile(ps, tab_t, cc0, ss0, tcol, w, out_ap, waits):
            t1 = rsc[:, 0:512]
            t2 = rsc[:, 512:1024]
            em.op("ve", lambda E: E.tensor_mul(
                t1[:, 0:w], ps[:, 0:w], tab_t[:, cc0 + tcol:cc0 + tcol + w]),
                waits=waits)
            em.op("ve", lambda E: E.tensor_mul(
                t2[0:64, 0:w], ps[64:128, 0:w],
                tab_t[64:128, ss0 + tcol:ss0 + tcol + w]))
            em.op("ve", lambda E: E.tensor_mul(
                t2[64:128, 0:w], ps[0:64, 0:w],
                tab_t[0:64, ss0 + tcol:ss0 + tcol + w]))
            em.op("ve", lambda E: E.tensor_tensor(
                out_ap[0:64, 0:w], t1[0:64, 0:w], t2[0:64, 0:w], ALU.subtract))
            return em.op("ve", lambda E: E.tensor_tensor(
                out_ap[64:128, 0:w], t1[64:128, 0:w], t2[64:128, 0:w], ALU.add),
                inc="pV")

        psum_free = {id(pA): None, id(pB): None}
        out_free = [None, None]
        i = 0
        for (proj, src_off, span, cc0, ss0, dst) in (
                ("q", 512, T1, QC0, QS0, q1d), ("k", 0, T0, KC0, KS0, k1d)):
            if proj == "k":
                load_wfull(wful, wka, war=[em.tok("pT")])
                em.wait("te", em.tok("dG"))
            for h in range(NH):
                for (t0c, w) in ttiles(span):
                    pb = pA if i % 2 == 0 else pB
                    mm = None
                    for dc in range(16):
                        wv = []
                        if dc == 0:
                            wv = [em.tok("dG")]
                            if psum_free[id(pb)]:
                                wv.append(psum_free[id(pb)])
                        mm = em.op("te", lambda E, dc=dc, h=h, t0c=t0c, w=w, pb=pb,
                                   src_off=src_off: E.matmul(
                            pb[:, 0:w],
                            wful[:, dc * D + h * P:dc * D + (h + 1) * P],
                            xln1[:, dc * T0 + src_off + t0c:dc * T0 + src_off + t0c + w],
                            start=(dc == 0), stop=(dc == 15)),
                            waits=wv, inc=("pT" if dc == 15 else None))
                    sl = i % 2
                    ob = qkout[:, sl * 512:(sl + 1) * 512]
                    wv = [mm] + ([out_free[sl]] if out_free[sl] else [])
                    rt = rope_tile(pb, tabs, cc0, ss0, t0c, w, ob, wv)
                    psum_free[id(pb)] = rt
                    out_free[sl] = em.dma(
                        "sc", dst[h * P:(h + 1) * P, t0c:t0c + w], ob[:, 0:w], waits=[rt])
                    i += 1
        load_wfull(wful, wva, war=[em.tok("pT")])
        em.wait("te", em.tok("dG"))
        vfree = [None, None]
        for rt_i in range(T0 // P):
            for dv in range(4):
                k = rt_i * 4 + dv
                pb = pA if k % 2 == 0 else pB
                mm = None
                for dc in range(16):
                    wv = ([psum_free[id(pb)]] if (psum_free[id(pb)] and dc == 0) else [])
                    mm = em.op("te", lambda E, dc=dc, rt_i=rt_i, dv=dv, pb=pb: E.matmul(
                        pb[:],
                        xln1[:, dc * T0 + rt_i * P:dc * T0 + (rt_i + 1) * P],
                        wful[:, dc * D + dv * 512:dc * D + (dv + 1) * 512],
                        start=(dc == 0), stop=(dc == 15)),
                        waits=wv, inc=("pT" if dc == 15 else None))
                sl = k % 2
                ob = qkout[:, sl * 512:(sl + 1) * 512]
                cp = em.op("sc", lambda E, pb=pb, ob=ob: E.copy(ob, pb[:]),
                           waits=[mm] + ([vfree[sl]] if vfree[sl] else []), inc="pS")
                psum_free[id(pb)] = cp
                vfree[sl] = em.dma(
                    "sc", v1d[rt_i * P:(rt_i + 1) * P, dv * 512:(dv + 1) * 512],
                    ob, waits=[cp])
        em.barrier()
        s0.close()

        s2 = ExitStack()
        kwin = s2.enter_context(nc.sbuf_tensor("kwin", [P, 16 * 1024], BF16))
        vwin = s2.enter_context(nc.sbuf_tensor("vwin", [P, 8 * D], BF16))
        ptb = s2.enter_context(nc.sbuf_tensor("ptb", [P, 8 * 512], BF16))
        qh1 = s2.enter_context(nc.sbuf_tensor("qh1", [P, 512], BF16))
        mk1 = s2.enter_context(nc.sbuf_tensor("mk1", [P, 4 * 1024], F32))
        ssb = s2.enter_context(nc.sbuf_tensor("ssb", [P, 1024], F32))
        psb = s2.enter_context(nc.sbuf_tensor("psb", [P, 1024], F32))

        blocks = [
            (0, 512, 0, 1024),
            (512, 512, 512, 1024),
            (1024, 512, 1024, 1024),
            (1536, 256, 2048, 256),
        ]
        aofree = [None, None]
        kv_use = None
        m_use = None
        pC_free = None
        for bi, (qc0, qw, kc0r, kw) in enumerate(blocks):
            nkc = kw // P
            tk = None
            for g in range(2):
                src = k1d.rearrange("(c p) t -> p c t", p=P)[:, 8 * g:8 * g + 8, kc0r:kc0r + kw]
                dvw = kwin[:].rearrange("p (c t) -> p c t", t=1024)[:, 8 * g:8 * g + 8, 0:kw]
                tk = em.dma("sy", dvw, src,
                            waits=([kv_use] if (kv_use and g == 0) else []))
            src = v1d[kc0r:kc0r + kw, :].rearrange("(r p) d -> p r d", p=P)
            dvw = vwin[:].rearrange("p (r d) -> p r d", d=D)[:, 0:nkc, :]
            tv = em.dma("sy", dvw, src)
            if bi < 3:
                msrc = m1m[qc0:qc0 + qw, :].rearrange("(t p) k -> p t k", p=P)
                mdst = mk1[:].rearrange("p (t k) -> p t k", k=1024)[:, 0:qw // P, :]
            else:
                msrc = m1f.rearrange("(t p) k -> p t k", p=P)
                mdst = mk1[:].rearrange("p (t k) -> p t k", k=1024)[:, 0:2, 0:FRONT]
            tm = em.dma("sy", mdst, msrc, waits=([m_use] if m_use else []))
            kv_use = None
            m_use = None
            for h in range(NH):
                tq = em.dma("sy", qh1[:, 0:qw], q1d[h * P:(h + 1) * P, qc0:qc0 + qw],
                            waits=[em.tok("pT")])
                pt_ev = None
                texp_prev = None
                tpn_prev = None
                sbank_free = None
                for qt in range(qw // P):
                    smm = []
                    for kk in range((kw + 511) // 512):
                        wkk = min(512, kw - kk * 512)
                        pb = pA if kk == 0 else pB
                        wv = []
                        if kk == 0:
                            wv = [tk, tq] + ([sbank_free] if sbank_free else [])
                        smm.append(em.op("te", lambda E, qt=qt, kk=kk, wkk=wkk, pb=pb,
                                         h=h: E.matmul(
                            pb[:, 0:wkk], qh1[:, qt * P:(qt + 1) * P],
                            kwin[:, h * 1024 + kk * 512:h * 1024 + kk * 512 + wkk],
                            start=True, stop=True),
                            waits=wv, inc="pT"))
                    nks = (kw + 511) // 512
                    ta = None
                    for kk in range(nks):
                        wkk = min(512, kw - kk * 512)
                        pb = pA if kk == 0 else pB
                        wv = [smm[kk], tm] + ([texp_prev] if (texp_prev and kk == 0) else [])
                        ta = em.op("ve", lambda E, qt=qt, kk=kk, wkk=wkk, pb=pb: E.tensor_tensor(
                            ssb[:, kk * 512:kk * 512 + wkk], pb[:, 0:wkk],
                            mk1[:, qt * 1024 + kk * 512:qt * 1024 + kk * 512 + wkk],
                            ALU.add), waits=wv,
                            inc=("pV" if kk == nks - 1 else None))
                    sbank_free = ta
                    em.op("ve", lambda E, kw=kw: E.reduce_max(
                        stats[:, 32:33], ssb[:, 0:kw], axis=AX))
                    tneg = em.op("ve", lambda E: E.tensor_scalar_mul(
                        stats[:, 33:34], stats[:, 32:33], -1.0), inc="pV")
                    texp = em.op("sc", lambda E, kw=kw: E.activation(
                        psb[:, 0:kw], ssb[:, 0:kw], AF.Exp, bias=stats[:, 33:34]),
                        waits=[tneg] + ([tpn_prev] if tpn_prev else []), inc="pS")
                    texp_prev = texp
                    em.op("ve", lambda E, kw=kw: E.reduce_sum(
                        stats[:, 34:35], psb[:, 0:kw], axis=AX), waits=[texp])
                    trc = em.op("ve", lambda E: E.reciprocal(
                        stats[:, 35:36], stats[:, 34:35]), inc="pV")
                    tpn = em.op("ve", lambda E, kw=kw: E.tensor_scalar(
                        scr16[:, 0:kw], psb[:, 0:kw], stats[:, 35:36], None, ALU.mult),
                        waits=[trc, em.tok("pT")], inc="pV")
                    tpn_prev = tpn
                    for r0 in range(0, nkc, 8):
                        n = min(8, nkc - r0)
                        last = None
                        for icx in range(n):
                            kcix = r0 + icx
                            wv = [tpn] + ([pt_ev] if (pt_ev and icx == 0) else [])
                            last = em.op("te", lambda E, kcix=kcix, icx=icx: E.transpose(
                                pTb[:, icx * P:(icx + 1) * P],
                                scr16[:, kcix * P:(kcix + 1) * P], identb[:]),
                                waits=wv, inc=("pT" if icx == n - 1 else None))
                        dview = ptb[:].rearrange("p (c t) -> p c t", t=512)[
                            :, r0:r0 + n, qt * P:(qt + 1) * P]
                        sview = pTb[:, 0:n * P].rearrange("p (c t) -> p c t", t=P)
                        pt_ev = em.op("sc", lambda E, d=dview, s=sview: E.copy(d, s),
                                      waits=[last], inc="pS")
                mm = None
                for kcix in range(nkc):
                    wv = []
                    if kcix == 0:
                        wv = [pt_ev, tv] + ([pC_free] if pC_free else [])
                    mm = em.op("te", lambda E, kcix=kcix, h=h, qw=qw, nkc=nkc: E.matmul(
                        pC[:, 0:qw],
                        vwin[:, kcix * D + h * P:kcix * D + (h + 1) * P],
                        ptb[:, kcix * 512:kcix * 512 + qw],
                        start=(kcix == 0), stop=(kcix == nkc - 1)),
                        waits=wv, inc=("pT" if kcix == nkc - 1 else None))
                sl = h % 2
                ob = scr16[:, 1024 + sl * 512:1024 + (sl + 1) * 512]
                cp = em.op("sc", lambda E, ob=ob, qw=qw: E.copy(
                    ob[:, 0:qw], pC[:, 0:qw]),
                    waits=[mm] + ([aofree[sl]] if aofree[sl] else []), inc="pS")
                pC_free = cp
                aofree[sl] = em.dma("sc", aod[h * P:(h + 1) * P, qc0:qc0 + qw],
                                    ob[:, 0:qw], waits=[cp])
            kv_use = em.tok("pT")
            m_use = em.tok("pV")
        em.barrier()
        s2.close()

        def wo_residual(wo_ap, ao_dram, ao_sb, res_dram, res_row0, out_dram, rows,
                        wofull, aot, x0t, ysb, x1t):
            tw = load_wfull(wofull, wo_ap)
            aofree = None
            x0free = [None] * 2
            pacc_free = {id(pE): None, id(pF): None}
            ysb_free = None
            pTf_free2 = None
            for (t0c, w) in ttiles(rows, 256):
                nsub = w // P
                ta = None
                if ao_dram is not None:
                    src = ao_dram.rearrange("(c p) t -> p c t", p=P)[:, :, t0c:t0c + w]
                    dvw = aot[:].rearrange("p (c t) -> p c t", t=512)[:, :, 0:w]
                    ta = em.dma("sy", dvw, src, waits=([aofree] if aofree else []))
                tx0 = []
                for sub in range(nsub):
                    tx0.append(em.dma(
                        "sy", x0t[:, sub * D:(sub + 1) * D],
                        res_dram[res_row0 + t0c + sub * P:res_row0 + t0c + (sub + 1) * P, :],
                        waits=([x0free[sub]] if x0free[sub] else [])))
                tt_last = None
                for oc in range(16):
                    pb = pE if oc % 2 == 0 else pF
                    mm = None
                    for ic in range(16):
                        wv = []
                        if ic == 0:
                            wv = [tw] + ([ta] if ta else [])
                            if pacc_free[id(pb)]:
                                wv.append(pacc_free[id(pb)])
                        if ao_dram is not None:
                            rhs = aot[:, ic * 512:ic * 512 + w]
                        else:
                            rhs = ao_sb[:, ic * 1024 + t0c:ic * 1024 + t0c + w]
                        mm = em.op("te", lambda E, ic=ic, oc=oc, pb=pb, rhs=rhs, w=w: E.matmul(
                            pb[:, 0:w],
                            wofull[:, ic * D + oc * P:ic * D + (oc + 1) * P],
                            rhs, start=(ic == 0), stop=(ic == 15)),
                            waits=wv, inc=("pT" if ic == 15 else None))
                    cp = em.op("sc", lambda E, pb=pb, w=w: E.copy(
                        ysb[:, 0:w], pb[:, 0:w]),
                        waits=[mm] + ([ysb_free] if ysb_free else []), inc="pS")
                    pacc_free[id(pb)] = cp
                    tlast_tr = None
                    for sub in range(nsub):
                        wv = [cp] + ([pTf_free2] if (pTf_free2 and sub == 0) else [])
                        tlast_tr = em.op("te", lambda E, sub=sub: E.transpose(
                            pTf[:, sub * P:(sub + 1) * P],
                            ysb[:, sub * P:(sub + 1) * P], identf[:]),
                            waits=wv, inc=("pT" if sub == nsub - 1 else None))
                    ysb_free = tlast_tr
                    for sub in range(nsub):
                        wv = [tlast_tr, tx0[sub]]
                        tt_last = em.op("ve", lambda E, sub=sub, oc=oc: E.tensor_tensor(
                            x1t[:, sub * D + oc * P:sub * D + (oc + 1) * P],
                            pTf[:, sub * P:(sub + 1) * P],
                            x0t[:, sub * D + oc * P:sub * D + (oc + 1) * P],
                            ALU.add), waits=wv,
                            inc=("pV" if sub == nsub - 1 else None))
                    pTf_free2 = tt_last
                for sub in range(nsub):
                    st = em.dma("sc", out_dram[t0c + sub * P:t0c + (sub + 1) * P, :],
                                x1t[:, sub * D:(sub + 1) * D], waits=[tt_last])
                    x0free[sub] = st
                if ao_dram is not None:
                    aofree = em.tok("pT")

        s2b = ExitStack()
        wofl = s2b.enter_context(nc.sbuf_tensor("wofl", [P, 16 * D], BF16))
        aot1 = s2b.enter_context(nc.sbuf_tensor("aot1", [P, 16 * 512], BF16))
        x0t1 = s2b.enter_context(nc.sbuf_tensor("x0t1", [P, 2 * D], F32))
        ysb1 = s2b.enter_context(nc.sbuf_tensor("ysb1", [P, 512], F32))
        x1t1 = s2b.enter_context(nc.sbuf_tensor("x1t1", [P, 2 * D], F32))
        wo_residual(woa, aod, None, x0, 512, X1d, T1, wofl, aot1, x0t1, ysb1, x1t1)
        em.barrier()
        s2b.close()

        def emit_ffn(xin_dram, xout_dram, halves, tag, cw_slot0=0):
            sf = ExitStack()
            xln = sf.enter_context(nc.sbuf_tensor("f_xln" + tag, [P, 16 * 1024], BF16))
            X2H = sf.enter_context(nc.sbuf_tensor("f_X2H" + tag, [P, 16 * 1024], F32))
            g_sb = sf.enter_context(nc.sbuf_tensor("f_g" + tag, [P, 8 * 1024], BF16))
            w13r = sf.enter_context(nc.sbuf_tensor("f_w13" + tag, [P, 2 * 4096], BF16))
            we2r = sf.enter_context(nc.sbuf_tensor("f_we2" + tag, [P, 8 * D], BF16))
            cwbc = sf.enter_context(nc.sbuf_tensor("f_cwbc" + tag, [P, 1024], BF16))
            xinb = sf.enter_context(nc.sbuf_tensor("f_xin" + tag, [P, D], F32))
            x32f = sf.enter_context(nc.sbuf_tensor("f_x32f" + tag, [P, 16 * P], F32))
            t1sb = sf.enter_context(nc.sbuf_tensor("f_t1" + tag, [P, 512], F32))
            tmsb = sf.enter_context(nc.sbuf_tensor("f_tm" + tag, [P, 512], F32))
            xout = xinb

            for (h0, H) in halves:
                tiles = ttiles(H)
                xin_free = []
                scr16_war = []
                scr32_war = []
                rtr_war = []
                pTb_free = None
                pTf_free = None
                for t in range(H // P):
                    sl = t % 2
                    xv = xinb[:, 0:D]
                    tin = em.dma("sy", xv, xin_dram[h0 + t * P:h0 + (t + 1) * P, :],
                                 waits=xin_free)
                    toks = ln_tile(xv, scr32[:, 0:D], sl, scr16[:, 0:D], tin,
                                   war=scr16_war, xln32_ap=scr32[:, 0:D],
                                   sq_war=scr32_war, war32=rtr_war)
                    ev0, pTf_free = transpose16(xinb, 0, X2H, 1024, t * P,
                                                tin, pTf, pTf_free, False)
                    xin_free = [toks[0], em.tok("pT")]
                    ev2, pTb_free = transpose16(scr16, 0, xln, 1024, t * P,
                                                toks[0], pTb, pTb_free, True)
                    scr16_war = [em.tok("pT")]
                    ev3, pTf_free = transpose16(scr32, 0, x32f, P, 0,
                                                toks[1], pTf, pTf_free, False)
                    scr32_war = [em.tok("pT")]
                    mm = None
                    for dc in range(16):
                        mm = em.op("te", lambda E, dc=dc: E.matmul(
                            pD[:, 0:16], x32f[:, dc * P:(dc + 1) * P],
                            wrsb[:, dc * 16:(dc + 1) * 16],
                            start=(dc == 0), stop=(dc == 15)),
                            waits=([ev3] if dc == 0 else []),
                            inc=("pT" if dc == 15 else None))
                    em.op("ve", lambda E: E.reduce_max(stats[:, 40:41], pD[:, 0:16], axis=AX),
                          waits=[mm] + rtr_war)
                    tneg = em.op("ve", lambda E: E.tensor_scalar_mul(
                        stats[:, 41:42], stats[:, 40:41], -1.0), inc="pV")
                    texp = em.op("sc", lambda E: E.activation(
                        scr32[:, 0:16], pD[:, 0:16], AF.Exp, bias=stats[:, 41:42]),
                        waits=[tneg] + scr32_war, inc="pS")
                    rtr_war = [texp]
                    em.op("ve", lambda E: E.reduce_sum(stats[:, 42:43], scr32[:, 0:16], axis=AX),
                          waits=[texp])
                    em.op("ve", lambda E: E.reciprocal(stats[:, 43:44], stats[:, 42:43]))
                    em.op("ve", lambda E: E.tensor_scalar(
                        scr32[:, 0:16], scr32[:, 0:16], stats[:, 43:44], None, ALU.mult))
                    em.op("ve", lambda E: E.tensor_copy(scr32[:, 16:32], scr32[:, 0:16]))
                    em.op("ve", lambda E: E.memset(scr32[:, 32:48], 0.0))
                    for r in range(4):
                        em.op("ve", lambda E: E.reduce_max(stats[:, 44:45],
                                                           scr32[:, 16:32], axis=AX))
                        em.op("ve", lambda E: E.tensor_scalar(
                            scr32[:, 48:64], scr32[:, 16:32], stats[:, 44:45], None,
                            ALU.is_ge))
                        em.op("ve", lambda E: E.tensor_mul(
                            scr32[:, 64:80], scr32[:, 48:64], scr32[:, 0:16]))
                        em.op("ve", lambda E: E.tensor_add(
                            scr32[:, 32:48], scr32[:, 32:48], scr32[:, 64:80]))
                        em.op("ve", lambda E: E.scalar_tensor_tensor(
                            scr32[:, 16:32], scr32[:, 48:64], NEG, scr32[:, 16:32],
                            ALU.mult, ALU.add))
                    em.op("ve", lambda E: E.reduce_sum(stats[:, 45:46], scr32[:, 32:48], axis=AX))
                    em.op("ve", lambda E: E.reciprocal(stats[:, 46:47], stats[:, 45:46]))
                    tcw = em.op("ve", lambda E: E.tensor_scalar(
                        scr32[:, 32:48], scr32[:, 32:48], stats[:, 46:47], None, ALU.mult),
                        inc="pV")
                    ttr = em.op("te", lambda E: E.transpose(
                        pTf[0:16, 0:P], scr32[:, 32:48], identf[:]),
                        waits=[tcw, pTf_free], inc="pT")
                    pTf_free = em.op("sc", lambda E, t=t: E.copy(
                        cwT[0:16, t * P:(t + 1) * P], pTf[0:16, 0:P]),
                        waits=[ttr], inc="pS")
                    scr32_war = [toks[-1], tcw, em.tok("pT")]
                if _DEBUG:
                    em.dma("sc", cwdbg[cw_slot0 + halves.index((h0, H)), :, 0:H],
                           cwT[0:16, 0:H])
                em.barrier()

                units = [("sh", su) for su in range(4)] + [("ex", e) for e in range(NE)]
                w13_free = [[], []]
                we2_free = []
                uv_free = {id(pA): None, id(pB): None, id(pC): None, id(pD): None}
                acc_free = {id(pE): None, id(pF): None}
                g_war = []
                cwbc_war = []
                pTf_cw_free = None
                prev_g = None
                for (kind, ue) in units:
                    tcwb = None
                    if kind == "ex":
                        last_cwb = pTf_cw_free
                        for (t0c, w) in tiles:
                            m1 = em.op("te", lambda E, t0c=t0c, w=w, ue=ue: E.matmul(
                                pTf[:, 0:w], selsb[0:16, ue * P:(ue + 1) * P],
                                cwT[0:16, t0c:t0c + w], start=True, stop=True),
                                waits=([last_cwb] if last_cwb else []),
                                inc="pT")
                            tcwb = em.op("sc", lambda E, t0c=t0c, w=w: E.copy(
                                cwbc[:, t0c:t0c + w], pTf[:, 0:w]),
                                waits=[m1] + (cwbc_war if t0c == 0 else []), inc="pS")
                            last_cwb = tcwb
                    glast = None
                    for hc in range(8):
                        sl = hc % 2
                        if kind == "ex":
                            wsrc = w13[ue, hc].rearrange("u c p h -> p u c h")
                        else:
                            wsrc = ws13[ue * 8 + hc].rearrange("u c p h -> p u c h")
                        wdst = w13r[:, sl * 4096:(sl + 1) * 4096].rearrange(
                            "p (u c h) -> p u c h", u=2, h=P)
                        tw = em.dma("gp", wdst, wsrc, waits=w13_free[sl])
                        for ti, (t0c, w) in enumerate(tiles):
                            pu = pA if ti % 2 == 0 else pC
                            pv = pB if ti % 2 == 0 else pD
                            mmu = mmv = None
                            for dc in range(16):
                                wv = []
                                if dc == 0:
                                    wv = [tw] + ([uv_free[id(pu)]] if uv_free[id(pu)] else [])
                                mmu = em.op("te", lambda E, dc=dc, w=w, t0c=t0c, pu=pu,
                                            sl=sl: E.matmul(
                                    pu[:, 0:w],
                                    w13r[:, sl * 4096 + dc * P:sl * 4096 + (dc + 1) * P],
                                    xln[:, dc * 1024 + t0c:dc * 1024 + t0c + w],
                                    start=(dc == 0), stop=(dc == 15)),
                                    waits=wv, inc=("pT" if dc == 15 else None))
                            for dc in range(16):
                                wv = ([uv_free[id(pv)]] if (uv_free[id(pv)] and dc == 0) else [])
                                mmv = em.op("te", lambda E, dc=dc, w=w, t0c=t0c, pv=pv,
                                            sl=sl: E.matmul(
                                    pv[:, 0:w],
                                    w13r[:, sl * 4096 + 2048 + dc * P:sl * 4096 + 2048 + (dc + 1) * P],
                                    xln[:, dc * 1024 + t0c:dc * 1024 + t0c + w],
                                    start=(dc == 0), stop=(dc == 15)),
                                    waits=wv, inc=("pT" if dc == 15 else None))
                            tsl = em.op("sc", lambda E, pu=pu, w=w: E.activation(
                                t1sb[:, 0:w], pu[:, 0:w], AF.Silu),
                                waits=[mmu] + ([prev_g] if prev_g else []) + g_war,
                                inc="pS")
                            uv_free[id(pu)] = tsl
                            if kind == "ex":
                                ttm = em.op("ve", lambda E, pv=pv, w=w, t0c=t0c: E.tensor_tensor(
                                    tmsb[:, 0:w], pv[:, 0:w], cwbc[:, t0c:t0c + w],
                                    ALU.mult),
                                    waits=[mmv, tcwb] + ([prev_g] if prev_g else []),
                                    inc="pV")
                                uv_free[id(pv)] = ttm
                                glast = em.op("gp", lambda E, hc=hc, t0c=t0c, w=w: E.tensor_tensor(
                                    g_sb[:, hc * 1024 + t0c:hc * 1024 + t0c + w],
                                    t1sb[:, 0:w], tmsb[:, 0:w], ALU.mult),
                                    waits=[tsl, ttm], inc="pP")
                            else:
                                glast = em.op("ve", lambda E, hc=hc, t0c=t0c, w=w,
                                              pv=pv: E.tensor_tensor(
                                    g_sb[:, hc * 1024 + t0c:hc * 1024 + t0c + w],
                                    t1sb[:, 0:w], pv[:, 0:w], ALU.mult),
                                    waits=[tsl, mmv], inc="pV")
                                uv_free[id(pv)] = glast
                            prev_g = glast
                        w13_free[sl] = [em.tok("pT")]
                    g_war = []
                    tw2 = None
                    for g2 in range(2):
                        if kind == "ex":
                            src = we2[ue, g2 * 512:(g2 + 1) * 512, :].rearrange(
                                "(c p) o -> p c o", p=P)
                        else:
                            src = ws2[ue * 1024 + g2 * 512:ue * 1024 + (g2 + 1) * 512, :].rearrange(
                                "(c p) o -> p c o", p=P)
                        dvw = we2r[:].rearrange("p (c o) -> p c o", o=D)[:, g2 * 4:(g2 + 1) * 4, :]
                        tw2 = em.dma("gp", dvw, src, waits=(we2_free if g2 == 0 else []))
                    for (t0c, w) in tiles:
                        for oc in range(16):
                            pb = pE if oc % 2 == 0 else pF
                            mm = None
                            for hc in range(8):
                                wv = []
                                if hc == 0:
                                    wv = [tw2, glast]
                                    if acc_free[id(pb)]:
                                        wv.append(acc_free[id(pb)])
                                mm = em.op("te", lambda E, hc=hc, oc=oc, t0c=t0c, w=w,
                                           pb=pb: E.matmul(
                                    pb[:, 0:w],
                                    we2r[:, hc * D + oc * P:hc * D + (oc + 1) * P],
                                    g_sb[:, hc * 1024 + t0c:hc * 1024 + t0c + w],
                                    start=(hc == 0), stop=(hc == 7)),
                                    waits=wv, inc=("pT" if hc == 7 else None))
                            if kind == "sh":
                                ev = em.op("ve", lambda E, oc=oc, t0c=t0c, w=w,
                                           pb=pb: E.scalar_tensor_tensor(
                                    X2H[:, oc * 1024 + t0c:oc * 1024 + t0c + w],
                                    pb[:, 0:w], 0.2,
                                    X2H[:, oc * 1024 + t0c:oc * 1024 + t0c + w],
                                    ALU.mult, ALU.add), waits=[mm], inc="pV")
                            else:
                                ev = em.op("ve", lambda E, oc=oc, t0c=t0c, w=w,
                                           pb=pb: E.tensor_tensor(
                                    X2H[:, oc * 1024 + t0c:oc * 1024 + t0c + w],
                                    X2H[:, oc * 1024 + t0c:oc * 1024 + t0c + w],
                                    pb[:, 0:w], ALU.add), waits=[mm], inc="pV")
                            acc_free[id(pb)] = ev
                    g_war = [em.tok("pT")]
                    we2_free = [em.tok("pT")]
                    cwbc_war = [em.tok("pV")]
                    pTf_cw_free = em.tok("pS")
                em.barrier()
                pTf_free = None
                xo_free = []
                for t in range(H // P):
                    last_cp = None
                    for r0 in range(0, 16, 4):
                        last = None
                        for i2 in range(4):
                            oc = r0 + i2
                            wv = []
                            if i2 == 0:
                                wv = ([pTf_free] if pTf_free else []) + \
                                     (xo_free if r0 == 0 else [])
                            last = em.op("te", lambda E, oc=oc, i2=i2, t=t: E.transpose(
                                pTf[:, i2 * P:(i2 + 1) * P],
                                X2H[:, oc * 1024 + t * P:oc * 1024 + (t + 1) * P],
                                identf[:]), waits=wv, inc=("pT" if i2 == 3 else None))
                        last_cp = em.op("sc", lambda E, r0=r0: E.copy(
                            xout[:, r0 * P:(r0 + 4) * P],
                            pTf[:, 0:512]), waits=[last], inc="pS")
                        pTf_free = last_cp
                    st = em.dma("sc", xout_dram[h0 + t * P:h0 + (t + 1) * P, :],
                                xout[:, 0:D], waits=[last_cp])
                    xo_free = [st]
                em.barrier()
            sf.close()

        emit_ffn(X1d, X2d, [(0, 1024), (1024, 768)], "2", cw_slot0=0)

        s4keep = ExitStack()
        s4 = ExitStack()
        s4p = ExitStack()
        xln3 = s4keep.enter_context(nc.sbuf_tensor("xln3", [P, 16 * T1], BF16))
        k3 = s4.enter_context(nc.sbuf_tensor("k3", [P, 4 * T1], BF16))
        q3 = s4.enter_context(nc.sbuf_tensor("q3", [P, 16 * 1024], BF16))
        v3 = s4.enter_context(nc.sbuf_tensor("v3", [P, 14 * 512], BF16))
        tab3 = s4p.enter_context(nc.sbuf_tensor("tab3", [P, 2 * TOWN + 2 * T1], BF16))
        xinb3 = s4p.enter_context(nc.sbuf_tensor("xinb3", [P, 2 * D], F32))
        wsl3 = s4p.enter_context(nc.sbuf_tensor("wsl3", [P, 2 * D], BF16))
        wv3sb = s4p.enter_context(nc.sbuf_tensor("wv3sb", [P, 16 * 512], BF16))
        rsc3 = s4p.enter_context(nc.sbuf_tensor("rsc3", [P, 2 * 512], F32))

        C3Q, S3Q, C3K, S3K = 0, TOWN, 2 * TOWN, 2 * TOWN + T1
        em.dma("gp", tab3[:, C3Q:C3Q + TOWN], qc3)
        em.dma("gp", tab3[:, S3Q:S3Q + TOWN], qs3)
        em.dma("gp", tab3[:, C3K:C3K + T1], kc3)
        em.dma("gp", tab3[:, S3K:S3K + T1], ks3)

        xin_free = [[], []]
        scr16_war = []
        sq_war = []
        pTb_free = None
        for t in range(T1 // P):
            sl = t % 2
            xv = xinb3[:, sl * D:(sl + 1) * D]
            tin = em.dma("sy", xv, X2d[t * P:(t + 1) * P, :], waits=xin_free[sl])
            toks = ln_tile(xv, scr32[:, 0:D], sl, scr16[:, 0:D], tin, war=scr16_war,
                           sq_war=sq_war)
            ev, pTb_free = transpose16(scr16, 0, xln3, T1, t * P,
                                       toks[0], pTb, pTb_free, True)
            xin_free[sl] = [toks[0]]
            scr16_war = [em.tok("pT")]
            sq_war = [toks[-1]]
        em.barrier()

        def rope3(ps, cc0, ss0, tcol, w, out_ap, waits):
            t1 = rsc3[:, 0:512]
            t2 = rsc3[:, 512:1024]
            em.op("ve", lambda E: E.tensor_mul(
                t1[:, 0:w], ps[:, 0:w], tab3[:, cc0 + tcol:cc0 + tcol + w]),
                waits=waits)
            em.op("ve", lambda E: E.tensor_mul(
                t2[0:64, 0:w], ps[64:128, 0:w], tab3[64:128, ss0 + tcol:ss0 + tcol + w]))
            em.op("ve", lambda E: E.tensor_mul(
                t2[64:128, 0:w], ps[0:64, 0:w], tab3[0:64, ss0 + tcol:ss0 + tcol + w]))
            em.op("ve", lambda E: E.tensor_tensor(
                out_ap[0:64, 0:w], t1[0:64, 0:w], t2[0:64, 0:w], ALU.subtract))
            return em.op("ve", lambda E: E.tensor_tensor(
                out_ap[64:128, 0:w], t1[64:128, 0:w], t2[64:128, 0:w], ALU.add),
                inc="pV")

        wfree = [[], []]
        pfree = {id(pA): None, id(pB): None}
        for h in range(NH):
            sl = h % 2
            tw = em.dma("gp", wsl3[:, sl * D:(sl + 1) * D].rearrange(
                "p (c h) -> p c h", h=P), wq3[h].rearrange("c p h -> p c h"),
                waits=wfree[sl])
            for (t0c, w) in ttiles(TOWN):
                pb = pA if (t0c // 512) % 2 == 0 else pB
                mm = None
                for dc in range(16):
                    wv = []
                    if dc == 0:
                        wv = [tw] + ([pfree[id(pb)]] if pfree[id(pb)] else [])
                    mm = em.op("te", lambda E, dc=dc, sl=sl, t0c=t0c, w=w, pb=pb: E.matmul(
                        pb[:, 0:w],
                        wsl3[:, sl * D + dc * P:sl * D + (dc + 1) * P],
                        xln3[:, dc * T1 + 512 + t0c:dc * T1 + 512 + t0c + w],
                        start=(dc == 0), stop=(dc == 15)),
                        waits=wv, inc=("pT" if dc == 15 else None))
                rt = rope3(pb, C3Q, S3Q, t0c, w,
                           q3[:, h * 1024 + t0c:h * 1024 + t0c + w], [mm])
                pfree[id(pb)] = rt
            wfree[sl] = [em.tok("pT")]
        for kv in range(NKV):
            sl = kv % 2
            tw = em.dma("gp", wsl3[:, sl * D:(sl + 1) * D].rearrange(
                "p (c h) -> p c h", h=P), wk3[kv].rearrange("c p h -> p c h"),
                waits=wfree[sl])
            for (t0c, w) in ttiles(T1):
                pb = pA if (t0c // 512) % 2 == 0 else pB
                mm = None
                for dc in range(16):
                    wv = []
                    if dc == 0:
                        wv = [tw] + ([pfree[id(pb)]] if pfree[id(pb)] else [])
                    mm = em.op("te", lambda E, dc=dc, sl=sl, t0c=t0c, w=w, pb=pb: E.matmul(
                        pb[:, 0:w],
                        wsl3[:, sl * D + dc * P:sl * D + (dc + 1) * P],
                        xln3[:, dc * T1 + t0c:dc * T1 + t0c + w],
                        start=(dc == 0), stop=(dc == 15)),
                        waits=wv, inc=("pT" if dc == 15 else None))
                rt = rope3(pb, C3K, S3K, t0c, w,
                           k3[:, kv * T1 + t0c:kv * T1 + t0c + w], [mm])
                pfree[id(pb)] = rt
            wfree[sl] = [em.tok("pT")]
        tv3 = None
        for g2 in range(4):
            src = wv3.rearrange("(c p) v -> p c v", p=P)[:, 4 * g2:4 * g2 + 4, :]
            dvw = wv3sb[:].rearrange("p (c v) -> p c v", v=512)[:, 4 * g2:4 * g2 + 4, :]
            tv3 = em.dma("gp", dvw, src)
        v3done = None
        for rt_i in range(T1 // P):
            pb = pA if rt_i % 2 == 0 else pB
            mm = None
            for dc in range(16):
                wv = []
                if dc == 0:
                    wv = [tv3] + ([pfree[id(pb)]] if pfree[id(pb)] else [])
                mm = em.op("te", lambda E, dc=dc, rt_i=rt_i, pb=pb: E.matmul(
                    pb[:],
                    xln3[:, dc * T1 + rt_i * P:dc * T1 + (rt_i + 1) * P],
                    wv3sb[:, dc * 512:(dc + 1) * 512],
                    start=(dc == 0), stop=(dc == 15)),
                    waits=wv, inc=("pT" if dc == 15 else None))
            cp = em.op("sc", lambda E, rt_i=rt_i, pb=pb: E.copy(
                v3[:, rt_i * 512:(rt_i + 1) * 512], pb[:]), waits=[mm], inc="pS")
            pfree[id(pb)] = cp
            v3done = cp
        em.barrier()
        s4p.close()

        s4c = ExitStack()
        m3sb = s4c.enter_context(nc.sbuf_tensor("m3sb", [P, 4 * T1], F32))
        ssb3 = s4c.enter_context(nc.sbuf_tensor("ssb3", [P, T1], F32))
        psb3 = s4c.enter_context(nc.sbuf_tensor("psb3", [P, T1], F32))
        AOC = 8 * 1024
        pt_ar = xln3
        nkc3 = T1 // P

        m_use3 = None
        pE_free = None
        for qg in range(2):
            msrc = m3.rearrange("(t p) k -> p t k", p=P)[:, qg * 4:(qg + 1) * 4, :]
            mdst = m3sb[:].rearrange("p (t k) -> p t k", k=T1)
            tm3 = em.dma("sy", mdst, msrc, waits=([m_use3] if m_use3 else []))
            for h in range(NH):
                kvh = h // 4
                pt_ev = None
                texp_prev = None
                tpn_prev = None
                sbank_free = None
                for qt in range(4):
                    qcol = qg * 512 + qt * P
                    banks = [pA, pB, pC, pD]
                    smm = []
                    for kk, (k0, kwd) in enumerate(ttiles(T1)):
                        pb = banks[kk]
                        wv = []
                        if kk == 0:
                            wv = [tm3] + ([sbank_free] if sbank_free else [])
                        smm.append(em.op("te", lambda E, k0=k0, kwd=kwd, pb=pb, h=h,
                                         qcol=qcol, kvh=kvh: E.matmul(
                            pb[:, 0:kwd],
                            q3[:, h * 1024 + qcol:h * 1024 + qcol + P],
                            k3[:, kvh * T1 + k0:kvh * T1 + k0 + kwd],
                            start=True, stop=True),
                            waits=wv, inc="pT"))
                    ta = None
                    for kk, (k0, kwd) in enumerate(ttiles(T1)):
                        pb = banks[kk]
                        wv = [smm[kk]] + ([texp_prev] if (texp_prev and kk == 0) else [])
                        ta = em.op("ve", lambda E, k0=k0, kwd=kwd, pb=pb, qt=qt: E.tensor_tensor(
                            ssb3[:, k0:k0 + kwd], pb[:, 0:kwd],
                            m3sb[:, qt * T1 + k0:qt * T1 + k0 + kwd], ALU.add),
                            waits=wv, inc=("pV" if kk == 3 else None))
                    sbank_free = ta
                    em.op("ve", lambda E: E.reduce_max(stats[:, 48:49], ssb3[:], axis=AX))
                    tneg = em.op("ve", lambda E: E.tensor_scalar_mul(
                        stats[:, 49:50], stats[:, 48:49], -1.0), inc="pV")
                    texp = em.op("sc", lambda E: E.activation(
                        psb3[:], ssb3[:], AF.Exp, bias=stats[:, 49:50]),
                        waits=[tneg] + ([tpn_prev] if tpn_prev else []), inc="pS")
                    texp_prev = texp
                    em.op("ve", lambda E: E.reduce_sum(stats[:, 50:51], psb3[:], axis=AX),
                          waits=[texp])
                    trc = em.op("ve", lambda E: E.reciprocal(stats[:, 51:52],
                                                             stats[:, 50:51]), inc="pV")
                    tpn = em.op("ve", lambda E: E.tensor_scalar(
                        scr16[:, 0:T1], psb3[:], stats[:, 51:52], None, ALU.mult),
                        waits=[trc, em.tok("pT")], inc="pV")
                    tpn_prev = tpn
                    for r0 in range(0, nkc3, 8):
                        n = min(8, nkc3 - r0)
                        last = None
                        for icx in range(n):
                            kcix = r0 + icx
                            wv = [tpn] + ([pt_ev] if (pt_ev and icx == 0) else [])
                            last = em.op("te", lambda E, kcix=kcix, icx=icx: E.transpose(
                                pTb[:, icx * P:(icx + 1) * P],
                                scr16[:, kcix * P:(kcix + 1) * P], identb[:]),
                                waits=wv, inc=("pT" if icx == n - 1 else None))
                        dview = pt_ar[:].rearrange("p (c t) -> p c t", t=512)[
                            :, r0:r0 + n, qt * P:(qt + 1) * P]
                        sview = pTb[:, 0:n * P].rearrange("p (c t) -> p c t", t=P)
                        pt_ev = em.op("sc", lambda E, d=dview, s=sview: E.copy(d, s),
                                      waits=[last], inc="pS")
                mm = None
                for kcix in range(nkc3):
                    wv = []
                    if kcix == 0:
                        wv = [pt_ev, v3done] + ([pE_free] if pE_free else [])
                    mm = em.op("te", lambda E, kcix=kcix, kvh=kvh, nkc3=nkc3: E.matmul(
                        pE[:],
                        v3[:, kcix * 512 + kvh * P:kcix * 512 + (kvh + 1) * P],
                        pt_ar[:, kcix * 512:(kcix + 1) * 512],
                        start=(kcix == 0), stop=(kcix == nkc3 - 1)),
                        waits=wv, inc=("pT" if kcix == nkc3 - 1 else None))
                pE_free = em.op("sc", lambda E, h=h, qg=qg: E.copy(
                    pt_ar[:, AOC + h * 1024 + qg * 512:AOC + h * 1024 + qg * 512 + 512],
                    pE[:]), waits=[mm], inc="pS")
            m_use3 = em.tok("pV")
        em.barrier()
        s4c.close()
        s4.close()

        s4d = ExitStack()
        wo3f = s4d.enter_context(nc.sbuf_tensor("wo3f", [P, 16 * D], BF16))
        x0t3 = s4d.enter_context(nc.sbuf_tensor("x0t3", [P, 2 * D], F32))
        ysb3 = s4d.enter_context(nc.sbuf_tensor("ysb3", [P, 512], F32))
        x1t3 = s4d.enter_context(nc.sbuf_tensor("x1t3", [P, 2 * D], F32))
        wo_residual(wo3, None, pt_ar[:, AOC:AOC + 16 * 1024], X2d, 512, X3d, TOWN,
                    wo3f, None, x0t3, ysb3, x1t3)
        em.barrier()
        s4d.close()
        s4keep.close()

        emit_ffn(X3d, y, [(0, 1024)], "4", cw_slot0=2)

        em.wait("sc", em.tok("dA"))
        em.wait("sy", em.tok("dS"))
        em.wait("gp", em.tok("dG"))

        def replay(name):
            def run(E):
                for f in em.q[name]:
                    f(E)
            return run

        with nc.Block() as blk:
            blk.sync(replay("sy"))
            blk.gpsimd(replay("gp"))
            blk.vector(replay("ve"))
            blk.scalar(replay("sc"))
            blk.tensor(replay("te"))
    return nc



def _rope_tables(pos, scale):
    inv = 1.0 / (10000.0 ** (np.arange(0, HD, 2, dtype=np.float64) / HD))
    ang = np.asarray(pos, np.float64)[None, :] * inv[:, None]
    c = np.cos(ang)
    s = np.sin(ang)
    C2 = np.concatenate([c, c], 0) * scale
    S2 = np.concatenate([s, s], 0) * scale
    return np.ascontiguousarray(C2, np.float32), np.ascontiguousarray(S2, np.float32)


def _headperm(w, nh):
    perm = np.r_[np.arange(0, 128, 2), np.arange(1, 128, 2)]
    return np.ascontiguousarray(
        w.reshape(w.shape[0], nh, 128)[:, :, perm].reshape(w.shape[0], -1))


def _prep_inputs(kw):
    f32 = np.float32
    x = np.ascontiguousarray(np.asarray(kw["L0_attn_a_x"], f32))
    Wq, Wk, Wv, Wo = (np.asarray(kw["L0_attn_a_" + n], f32)
                      for n in ("Wq", "Wk", "Wv", "Wo"))
    wq, wk, wv, wo = (np.asarray(kw["L0_attn_b_" + n], f32)
                      for n in ("wq", "wk", "wv", "wo"))
    wrt = np.asarray(kw["L0_ffn_w_router"], f32)
    we1, we3, we2 = (np.asarray(kw["L0_ffn_" + n], f32) for n in ("we1", "we3", "we2"))
    ws1, ws3, ws2 = (np.asarray(kw["L0_ffn_" + n], f32) for n in ("ws1", "ws3", "ws2"))

    shared = {
        "wqa": _headperm(np.ascontiguousarray(Wq.T), NH),
        "wka": _headperm(np.ascontiguousarray(Wk.T), NH),
        "wva": np.ascontiguousarray(Wv.T),
        "woa": np.ascontiguousarray(Wo.T),
        "wq3": np.ascontiguousarray(
            _headperm(wq, NH).reshape(16, P, NH, HD).transpose(2, 0, 1, 3)),
        "wk3": np.ascontiguousarray(
            _headperm(wk, NKV).reshape(16, P, NKV, HD).transpose(2, 0, 1, 3)),
        "wv3": np.ascontiguousarray(wv),
        "wo3": np.ascontiguousarray(wo),
        "wr": np.ascontiguousarray(wrt.reshape(16, P, 16)),
        "w13": np.ascontiguousarray(np.stack(
            [we1.reshape(NE, 16, P, 8, P).transpose(0, 3, 1, 2, 4),
             we3.reshape(NE, 16, P, 8, P).transpose(0, 3, 1, 2, 4)], axis=2)),
        "ws13": np.ascontiguousarray(np.stack(
            [ws1.reshape(16, P, 32, P).transpose(2, 0, 1, 3),
             ws3.reshape(16, P, 32, P).transpose(2, 0, 1, 3)], axis=1)),
        "we2": np.ascontiguousarray(we2),
        "ws2": np.ascontiguousarray(ws2),
    }
    sel = np.zeros((16, D), f32)
    for e in range(16):
        sel[e, e * P:(e + 1) * P] = 0.8
    shared["sel"] = sel

    in_maps = []
    for c in range(NCORES):
        b, j = divmod(c, 4)
        q0 = TOWN * j
        pos0 = np.concatenate([np.arange(q0 - 1024, q0 + 1024), np.arange(FRONT)])
        pos1 = pos0[512:]
        posq = np.arange(q0, q0 + TOWN)
        x0c = np.zeros((T0, D), f32)
        lo = q0 - 1024
        s_lo = max(0, lo)
        x0c[(s_lo - lo):2048] = x[b, s_lo:q0 + 1024]
        x0c[2048:] = x[b, :FRONT]

        sc = 1.0 / np.sqrt(HD)
        qc1_, qs1_ = _rope_tables(np.maximum(pos1, 0), sc)
        kc1_, ks1_ = _rope_tables(np.maximum(pos0, 0), 1.0)
        qc3_, qs3_ = _rope_tables(posq, sc)
        kc3_, ks3_ = _rope_tables(np.maximum(pos1, 0), 1.0)

        m1m = np.full((1536, 1024), NEG, f32)
        for n in range(3):
            qp = pos1[n * 512:(n + 1) * 512][:, None]
            kp = (q0 - 1024 + n * 512 + np.arange(1024))[None, :]
            ok = (kp <= qp) & (kp > qp - WIN) & (kp >= 0)
            m1m[n * 512:(n + 1) * 512] = np.where(ok, 0.0, NEG)
        qf = np.arange(FRONT)[:, None]
        kf = np.arange(FRONT)[None, :]
        m1f = np.where(kf <= qf, 0.0, NEG).astype(f32)

        qp = posq[:, None]
        kp_main = (q0 - 512 + np.arange(1536))[None, :]
        band = (kp_main <= qp) & (kp_main > qp - WIN)
        fok = (kp_main < FRONT) & (kp_main <= qp - WIN)
        main_ok = (band | fok) & (kp_main >= 0)
        kp_f = np.arange(FRONT)[None, :]
        bandf = (kp_f <= qp) & (kp_f > qp - WIN)
        fokf = (kp_f < FRONT) & (kp_f <= qp - WIN)
        front_ok = (bandf | fokf) if j > 0 else np.zeros((TOWN, FRONT), bool)
        m3c = np.concatenate([np.where(main_ok, 0.0, NEG),
                              np.where(front_ok, 0.0, NEG)], axis=1).astype(f32)

        im = dict(shared)
        im.update(x0=x0c, m1m=m1m, m1f=m1f, m3=m3c,
                  qc1=qc1_, qs1=qs1_, kc1=kc1_, ks1=ks1_,
                  qc3=qc3_, qs3=qs3_, kc3=kc3_, ks3=ks3_)
        in_maps.append(im)
    return in_maps


def kernel(**kw):
    in_maps = _prep_inputs(kw)
    nc = build_program()
    res = run_bass_kernel_spmd(nc, in_maps, core_ids=list(range(NCORES)))
    out = np.empty((2, SEQ, D), np.float32)
    for c in range(NCORES):
        b, j = divmod(c, 4)
        out[b, TOWN * j:TOWN * (j + 1)] = res.results[c]["y"]
    return out


# version 04
# speedup vs baseline: 1.1076x; 1.1076x over previous
import numpy as np
from contextlib import ExitStack

import concourse.bass as bass
import concourse.mybir as mybir
from concourse.bass_utils import run_bass_kernel_spmd

F32 = mybir.dt.float32
BF16 = mybir.dt.bfloat16
AF = mybir.ActivationFunctionType
ALU = mybir.AluOpType
AX = mybir.AxisListType.X

_DEBUG = False
_RACE_DETECT = False

P = 128
NCORES = 8
SEQ = 4096
D = 2048
T0, T1, TOWN = 2304, 1792, 1024
NH, HD = 16, 128
NKV = 4
WIN, FRONT = 512, 256
NE, HE, HS = 16, 1024, 4096
NEG = -1.0e30

ENGS = ("sy", "gp", "ve", "sc", "te")
DMASEM = {"sy": "dS", "gp": "dG", "sc": "dA"}
SEMS = ("dS", "dG", "dA", "pT", "pV", "pS", "pP")


def ttiles(total, step=512):
    out, o = [], 0
    while o < total:
        w = min(step, total - o)
        out.append((o, w))
        o += w
    return out


AUTOSER = {"ve": "pV", "sc": "pS", "gp": "pP"}


class Emit:

    def __init__(self, nc, sems):
        self.nc = nc
        self.sems = sems
        self.q = {e: [] for e in ENGS}
        self.cnt = {s: 0 for s in SEMS}
        self.waited = {e: {} for e in ENGS}
        self.last = {e: None for e in ENGS}

    def wait(self, eng, tok):
        if tok is None:
            return
        s, v = tok
        if v <= 0 or self.waited[eng].get(s, 0) >= v:
            return
        self.waited[eng][s] = v
        h = self.sems[s]
        self.q[eng].append(lambda E, h=h, v=v: E.wait_ge(h, v))

    def op(self, eng, fn, waits=(), inc=None):
        for t in waits:
            self.wait(eng, t)
        if eng in AUTOSER:
            self.wait(eng, self.last[eng])
            if inc is None:
                inc = AUTOSER[eng]
        if inc is None:
            self.q[eng].append(fn)
            return None
        amt = 16 if inc in ("dS", "dG", "dA") else 1
        self.cnt[inc] += amt
        h = self.sems[inc]
        self.q[eng].append(lambda E, fn=fn, h=h, amt=amt: fn(E).then_inc(h, amt))
        tok = (inc, self.cnt[inc])
        if eng in AUTOSER and inc not in ("dS", "dG", "dA"):
            self.last[eng] = tok
        return tok

    def dma(self, eng, out, in_, waits=()):
        return self.op(eng, lambda E, o=out, i=in_: E.dma_start(out=o, in_=i),
                       waits=waits, inc=DMASEM[eng])

    def tok(self, sem):
        return (sem, self.cnt[sem])

    def snapshot(self):
        return [(s, c) for s, c in self.cnt.items() if c > 0]

    def barrier(self):
        toks = self.snapshot()
        for e in ENGS:
            for t in toks:
                self.wait(e, t)



def build_program():
    nc = bass.Bass("TRN2", target_bir_lowering=False,
                   detect_race_conditions=_RACE_DETECT)

    def din(name, shape, dt=F32):
        return nc.dram_tensor(name, list(shape), dt, kind="ExternalInput").ap()

    x0 = din("x0", (T0, D))
    m1m = din("m1m", (1536, 1024))
    m1f = din("m1f", (FRONT, FRONT))
    m3 = din("m3", (TOWN, T1))
    qc1 = din("qc1", (P, T1)); qs1 = din("qs1", (P, T1))
    kc1 = din("kc1", (P, T0)); ks1 = din("ks1", (P, T0))
    qc3 = din("qc3", (P, TOWN)); qs3 = din("qs3", (P, TOWN))
    kc3 = din("kc3", (P, T1)); ks3 = din("ks3", (P, T1))
    seli = din("sel", (16, D))
    wqa = din("wqa", (D, D)); wka = din("wka", (D, D))
    wva = din("wva", (D, D)); woa = din("woa", (D, D))
    wq3 = din("wq3", (NH, 16, P, HD))
    wk3 = din("wk3", (NKV, 16, P, HD))
    wv3 = din("wv3", (D, NKV * HD))
    wo3 = din("wo3", (D, D))
    wr = din("wr", (16, P, 16))
    w13 = din("w13", (NE, 8, 2, 16, P, P))
    ws13 = din("ws13", (32, 2, 16, P, P))
    we2 = din("we2", (NE, HE, D))
    ws2 = din("ws2", (HS, D))

    w13c = nc.dram_tensor("w13c", [NE, 8, 2, 16, P, P], BF16, kind="Internal").ap()
    ws13c = nc.dram_tensor("ws13c", [32, 2, 16, P, P], BF16, kind="Internal").ap()
    we2c = nc.dram_tensor("we2c", [NE, HE, D], BF16, kind="Internal").ap()
    ws2c = nc.dram_tensor("ws2c", [HS, D], BF16, kind="Internal").ap()

    y = nc.dram_tensor("y", [TOWN, D], F32, kind="ExternalOutput").ap()

    kd = "ExternalOutput" if _DEBUG else "Internal"
    q1d = nc.dram_tensor("q1d", [D, T1], BF16, kind=kd).ap()
    k1d = nc.dram_tensor("k1d", [D, T0], BF16, kind=kd).ap()
    v1d = nc.dram_tensor("v1d", [T0, D], BF16, kind=kd).ap()
    aod = nc.dram_tensor("aod", [D, T1], BF16, kind=kd).ap()
    X1d = nc.dram_tensor("X1d", [T1, D], F32, kind=kd).ap()
    X2d = nc.dram_tensor("X2d", [T1, D], F32, kind=kd).ap()
    X3d = nc.dram_tensor("X3d", [TOWN, D], F32, kind=kd).ap()
    cwdbg = nc.dram_tensor("cwdbg", [4, 16, T1], BF16, kind=kd).ap() if _DEBUG else None

    es = ExitStack()
    with es:
        sems = {s: es.enter_context(nc.semaphore(s)) for s in SEMS}
        em = Emit(nc, sems)

        identb = es.enter_context(nc.sbuf_tensor("identb", [P, P], BF16))
        identf = es.enter_context(nc.sbuf_tensor("identf", [P, P], F32))
        stats = es.enter_context(nc.sbuf_tensor("stats", [P, 64], F32))
        selsb = es.enter_context(nc.sbuf_tensor("selsb", [16, D], BF16))
        cwT = es.enter_context(nc.sbuf_tensor("cwT", [16, T1], BF16))
        wrsb = es.enter_context(nc.sbuf_tensor("wrsb", [P, 16 * 16], F32))
        scr16 = es.enter_context(nc.sbuf_tensor("scr16", [P, 2048], BF16))
        scr32 = es.enter_context(nc.sbuf_tensor("scr32", [P, 2048], F32))

        pA = es.enter_context(nc.psum_tensor("pA", [P, 512], F32))
        pB = es.enter_context(nc.psum_tensor("pB", [P, 512], F32))
        pC = es.enter_context(nc.psum_tensor("pC", [P, 512], F32))
        pD = es.enter_context(nc.psum_tensor("pD", [P, 512], F32))
        pE = es.enter_context(nc.psum_tensor("pE", [P, 512], F32))
        pF = es.enter_context(nc.psum_tensor("pF", [P, 512], F32))
        pTb = es.enter_context(nc.psum_tensor("pTb", [P, 1024], BF16))
        pTf = es.enter_context(nc.psum_tensor("pTf", [P, 512], F32))

        def mk_ident(t):
            em.op("gp", lambda E: E.memset(t[:], 0.0))
            return em.op("gp", lambda E: E.affine_select(
                out=t[:], in_=t[:], compare_op=ALU.not_equal,
                fill=1.0, base=0, pattern=[[-1, P]],
                channel_multiplier=1))
        mk_ident(identb)
        tok_id = mk_ident(identf)
        em.dma("gp", selsb[:], seli)
        em.dma("sy", wrsb[:].rearrange("p (c e) -> p c e", e=16),
               wr.rearrange("c p e -> p c e"))
        em.wait("te", tok_id)

        def ln_tile(xin_ap, sqbuf, slot, xln_bf_ap, xin_tok, war=(), xln32_ap=None,
                    sq_war=(), war32=()):
            c0 = slot * 8
            tsq = em.op("sc", lambda E: E.activation(sqbuf, xin_ap, AF.Square),
                        waits=[xin_tok] + list(sq_war), inc="pS")
            em.op("ve", lambda E: E.reduce_sum(stats[:, c0:c0 + 1], xin_ap, axis=AX),
                  waits=[xin_tok])
            tsqr = em.op("ve", lambda E: E.reduce_sum(stats[:, c0 + 1:c0 + 2], sqbuf,
                                                      axis=AX),
                         waits=[tsq], inc="pV")
            em.op("ve", lambda E: E.tensor_scalar_mul(stats[:, c0 + 2:c0 + 3],
                                                      stats[:, c0:c0 + 1], 1.0 / D))
            em.op("ve", lambda E: E.tensor_mul(stats[:, c0 + 3:c0 + 4],
                                               stats[:, c0 + 2:c0 + 3],
                                               stats[:, c0 + 2:c0 + 3]))
            tv = em.op("ve", lambda E: E.scalar_tensor_tensor(
                stats[:, c0 + 4:c0 + 5], stats[:, c0 + 1:c0 + 2], 1.0 / D,
                stats[:, c0 + 3:c0 + 4], ALU.mult, ALU.subtract), inc="pV")
            tve = em.op("ve", lambda E: E.tensor_scalar_add(
                stats[:, c0 + 4:c0 + 5], stats[:, c0 + 4:c0 + 5], 1e-5), inc="pV")
            tsd = em.op("sc", lambda E: E.activation(stats[:, c0 + 5:c0 + 6],
                                                     stats[:, c0 + 4:c0 + 5],
                                                     AF.Sqrt),
                        waits=[tve], inc="pS")
            tr = em.op("ve", lambda E: E.reciprocal(stats[:, c0 + 6:c0 + 7],
                                                    stats[:, c0 + 5:c0 + 6]),
                       waits=[tsd], inc="pV")
            toks = []
            t1 = em.op("ve", lambda E: E.tensor_scalar(
                xln_bf_ap, xin_ap, stats[:, c0 + 2:c0 + 3], stats[:, c0 + 6:c0 + 7],
                ALU.subtract, ALU.mult), waits=[tr] + list(war), inc="pV")
            toks.append(t1)
            if xln32_ap is not None:
                t2 = em.op("ve", lambda E: E.tensor_scalar(
                    xln32_ap, xin_ap, stats[:, c0 + 2:c0 + 3], stats[:, c0 + 6:c0 + 7],
                    ALU.subtract, ALU.mult), waits=list(war32), inc="pV")
                toks.append(t2)
            toks.append(tsqr)
            return toks

        def transpose16(src_sb, src_col0, dst_t, dst_stride, dst_col,
                        src_tok, psum, psum_free, use_bf, width=2048):
            ident = identb[:] if use_bf else identf[:]
            per = 8 if use_bf else 4
            nch = width // P
            ev_tok = None
            for r0 in range(0, nch, per):
                n = min(per, nch - r0)
                last = None
                for i in range(n):
                    c = r0 + i
                    w = [src_tok, psum_free] if i == 0 else []
                    last = em.op("te", lambda E, c=c, i=i, ident=ident: E.transpose(
                        psum[:, i * P:(i + 1) * P],
                        src_sb[:, src_col0 + c * P:src_col0 + (c + 1) * P], ident),
                        waits=w, inc=("pT" if i == n - 1 else None))
                dview = dst_t[:].rearrange("p (c t) -> p c t", t=dst_stride)[
                    :, r0:r0 + n, dst_col:dst_col + P]
                sview = psum[:, 0:n * P].rearrange("p (c t) -> p c t", t=P)
                ev_tok = em.op("sc", lambda E, d=dview, s=sview: E.copy(d, s),
                               waits=[last], inc="pS")
                psum_free = ev_tok
            return ev_tok, psum_free

        s0 = ExitStack()
        xln1 = s0.enter_context(nc.sbuf_tensor("xln1", [P, 16 * T0], BF16))
        wful = s0.enter_context(nc.sbuf_tensor("wful", [P, 16 * D], BF16))
        tabs = s0.enter_context(nc.sbuf_tensor("tabs", [P, 2 * T0 + 2 * T1], BF16))
        xin2 = s0.enter_context(nc.sbuf_tensor("xin2", [P, 2 * D], F32))
        rsc = s0.enter_context(nc.sbuf_tensor("rsc", [P, 2 * 512], F32))
        qkout = s0.enter_context(nc.sbuf_tensor("qkout", [P, 2 * 512], BF16))

        KC0, KS0, QC0, QS0 = 0, T0, 2 * T0, 2 * T0 + T1
        em.dma("gp", tabs[:, KC0:KC0 + T0], kc1)
        em.dma("gp", tabs[:, KS0:KS0 + T0], ks1)
        em.dma("gp", tabs[:, QC0:QC0 + T1], qc1)
        em.dma("gp", tabs[:, QS0:QS0 + T1], qs1)

        def load_wfull(dst, wap, war=()):
            tk = None
            for g in range(4):
                src = wap.rearrange("(c p) o -> p c o", p=P)[:, 4 * g:4 * g + 4, :]
                dvw = dst[:].rearrange("p (c o) -> p c o", o=D)[:, 4 * g:4 * g + 4, :]
                tk = em.dma("gp", dvw, src, waits=(list(war) if g == 0 else []))
            return tk
        twq = load_wfull(wful, wqa)

        xin_free = [[], []]
        scr16_war = []
        sq_war = []
        pTb_free = None
        for t in range(T0 // P):
            sl = t % 2
            xv = xin2[:, sl * D:(sl + 1) * D]
            tin = em.dma("sy", xv, x0[t * P:(t + 1) * P, :], waits=xin_free[sl])
            toks = ln_tile(xv, scr32[:, 0:D], sl, scr16[:, 0:D], tin, war=scr16_war,
                           sq_war=sq_war)
            ev, pTb_free = transpose16(scr16, 0, xln1, T0, t * P,
                                       toks[0], pTb, pTb_free, True)
            xin_free[sl] = [toks[0]]
            scr16_war = [em.tok("pT")]
            sq_war = [toks[-1]]
        em.barrier()

        def rope_tile(ps, tab_t, cc0, ss0, tcol, w, out_ap, waits):
            t1 = rsc[:, 0:512]
            t2 = rsc[:, 512:1024]
            em.op("ve", lambda E: E.tensor_mul(
                t1[:, 0:w], ps[:, 0:w], tab_t[:, cc0 + tcol:cc0 + tcol + w]),
                waits=waits)
            em.op("ve", lambda E: E.tensor_mul(
                t2[0:64, 0:w], ps[64:128, 0:w],
                tab_t[64:128, ss0 + tcol:ss0 + tcol + w]))
            em.op("ve", lambda E: E.tensor_mul(
                t2[64:128, 0:w], ps[0:64, 0:w],
                tab_t[0:64, ss0 + tcol:ss0 + tcol + w]))
            em.op("ve", lambda E: E.tensor_tensor(
                out_ap[0:64, 0:w], t1[0:64, 0:w], t2[0:64, 0:w], ALU.subtract))
            return em.op("ve", lambda E: E.tensor_tensor(
                out_ap[64:128, 0:w], t1[64:128, 0:w], t2[64:128, 0:w], ALU.add),
                inc="pV")

        psum_free = {id(pA): None, id(pB): None}
        out_free = [None, None]
        i = 0
        for (proj, src_off, span, cc0, ss0, dst) in (
                ("q", 512, T1, QC0, QS0, q1d), ("k", 0, T0, KC0, KS0, k1d)):
            if proj == "k":
                load_wfull(wful, wka, war=[em.tok("pT")])
                em.wait("te", em.tok("dG"))
            for h in range(NH):
                for (t0c, w) in ttiles(span):
                    pb = pA if i % 2 == 0 else pB
                    mm = None
                    for dc in range(16):
                        wv = []
                        if dc == 0:
                            wv = [em.tok("dG")]
                            if psum_free[id(pb)]:
                                wv.append(psum_free[id(pb)])
                        mm = em.op("te", lambda E, dc=dc, h=h, t0c=t0c, w=w, pb=pb,
                                   src_off=src_off: E.matmul(
                            pb[:, 0:w],
                            wful[:, dc * D + h * P:dc * D + (h + 1) * P],
                            xln1[:, dc * T0 + src_off + t0c:dc * T0 + src_off + t0c + w],
                            start=(dc == 0), stop=(dc == 15)),
                            waits=wv, inc=("pT" if dc == 15 else None))
                    sl = i % 2
                    ob = qkout[:, sl * 512:(sl + 1) * 512]
                    wv = [mm] + ([out_free[sl]] if out_free[sl] else [])
                    rt = rope_tile(pb, tabs, cc0, ss0, t0c, w, ob, wv)
                    psum_free[id(pb)] = rt
                    out_free[sl] = em.dma(
                        "sc", dst[h * P:(h + 1) * P, t0c:t0c + w], ob[:, 0:w], waits=[rt])
                    i += 1
        load_wfull(wful, wva, war=[em.tok("pT")])
        em.wait("te", em.tok("dG"))
        vfree = [None, None]
        for rt_i in range(T0 // P):
            for dv in range(4):
                k = rt_i * 4 + dv
                pb = pA if k % 2 == 0 else pB
                mm = None
                for dc in range(16):
                    wv = ([psum_free[id(pb)]] if (psum_free[id(pb)] and dc == 0) else [])
                    mm = em.op("te", lambda E, dc=dc, rt_i=rt_i, dv=dv, pb=pb: E.matmul(
                        pb[:],
                        xln1[:, dc * T0 + rt_i * P:dc * T0 + (rt_i + 1) * P],
                        wful[:, dc * D + dv * 512:dc * D + (dv + 1) * 512],
                        start=(dc == 0), stop=(dc == 15)),
                        waits=wv, inc=("pT" if dc == 15 else None))
                sl = k % 2
                ob = qkout[:, sl * 512:(sl + 1) * 512]
                cp = em.op("sc", lambda E, pb=pb, ob=ob: E.copy(ob, pb[:]),
                           waits=[mm] + ([vfree[sl]] if vfree[sl] else []), inc="pS")
                psum_free[id(pb)] = cp
                vfree[sl] = em.dma(
                    "sc", v1d[rt_i * P:(rt_i + 1) * P, dv * 512:(dv + 1) * 512],
                    ob, waits=[cp])
        em.barrier()
        s0.close()

        sWO = ExitStack()
        wofl = sWO.enter_context(nc.sbuf_tensor("wofl", [P, 16 * D], BF16))
        tw_wo = load_wfull(wofl, woa)

        castk = {}
        castk["sh"] = em.dma("gp", ws13c[0:16], ws13[0:16])
        castk["sh"] = em.dma("gp", ws13c[16:32], ws13[16:32])
        castk["sh2"] = em.dma("gp", ws2c[0:HS // 2, :], ws2[0:HS // 2, :])
        castk["sh2"] = em.dma("gp", ws2c[HS // 2:, :], ws2[HS // 2:, :])
        for e in range(NE):
            castk[("ex", e)] = em.dma("gp", w13c[e], w13[e])
            castk[("ex2", e)] = em.dma("gp", we2c[e], we2[e])

        s2 = ExitStack()
        kwin = s2.enter_context(nc.sbuf_tensor("kwin", [P, 16 * 1024], BF16))
        vwin = s2.enter_context(nc.sbuf_tensor("vwin", [P, 8 * D], BF16))
        ptb = s2.enter_context(nc.sbuf_tensor("ptb", [P, 8 * 512], BF16))
        qh1 = s2.enter_context(nc.sbuf_tensor("qh1", [P, 512], BF16))
        mk1 = s2.enter_context(nc.sbuf_tensor("mk1", [P, 4 * 1024], F32))
        ssb = s2.enter_context(nc.sbuf_tensor("ssb", [P, 1024], F32))
        psb = s2.enter_context(nc.sbuf_tensor("psb", [P, 1024], F32))

        blocks = [
            (0, 512, 0, 1024),
            (512, 512, 512, 1024),
            (1024, 512, 1024, 1024),
            (1536, 256, 2048, 256),
        ]
        aofree = [None, None]
        kv_use = None
        m_use = None
        pC_free = None
        for bi, (qc0, qw, kc0r, kw) in enumerate(blocks):
            nkc = kw // P
            tk = None
            for g in range(2):
                src = k1d.rearrange("(c p) t -> p c t", p=P)[:, 8 * g:8 * g + 8, kc0r:kc0r + kw]
                dvw = kwin[:].rearrange("p (c t) -> p c t", t=1024)[:, 8 * g:8 * g + 8, 0:kw]
                tk = em.dma("sy", dvw, src,
                            waits=([kv_use] if (kv_use and g == 0) else []))
            src = v1d[kc0r:kc0r + kw, :].rearrange("(r p) d -> p r d", p=P)
            dvw = vwin[:].rearrange("p (r d) -> p r d", d=D)[:, 0:nkc, :]
            tv = em.dma("sy", dvw, src)
            if bi < 3:
                msrc = m1m[qc0:qc0 + qw, :].rearrange("(t p) k -> p t k", p=P)
                mdst = mk1[:].rearrange("p (t k) -> p t k", k=1024)[:, 0:qw // P, :]
            else:
                msrc = m1f.rearrange("(t p) k -> p t k", p=P)
                mdst = mk1[:].rearrange("p (t k) -> p t k", k=1024)[:, 0:2, 0:FRONT]
            tm = em.dma("sy", mdst, msrc, waits=([m_use] if m_use else []))
            kv_use = None
            m_use = None
            for h in range(NH):
                tq = em.dma("sy", qh1[:, 0:qw], q1d[h * P:(h + 1) * P, qc0:qc0 + qw],
                            waits=[em.tok("pT")])
                pt_ev = None
                texp_prev = None
                tpn_prev = None
                sbank_free = None
                for qt in range(qw // P):
                    smm = []
                    for kk in range((kw + 511) // 512):
                        wkk = min(512, kw - kk * 512)
                        pb = pA if kk == 0 else pB
                        wv = []
                        if kk == 0:
                            wv = [tk, tq] + ([sbank_free] if sbank_free else [])
                        smm.append(em.op("te", lambda E, qt=qt, kk=kk, wkk=wkk, pb=pb,
                                         h=h: E.matmul(
                            pb[:, 0:wkk], qh1[:, qt * P:(qt + 1) * P],
                            kwin[:, h * 1024 + kk * 512:h * 1024 + kk * 512 + wkk],
                            start=True, stop=True),
                            waits=wv, inc="pT"))
                    nks = (kw + 511) // 512
                    ta = None
                    for kk in range(nks):
                        wkk = min(512, kw - kk * 512)
                        pb = pA if kk == 0 else pB
                        wv = [smm[kk], tm] + ([texp_prev] if (texp_prev and kk == 0) else [])
                        ta = em.op("ve", lambda E, qt=qt, kk=kk, wkk=wkk, pb=pb: E.tensor_tensor(
                            ssb[:, kk * 512:kk * 512 + wkk], pb[:, 0:wkk],
                            mk1[:, qt * 1024 + kk * 512:qt * 1024 + kk * 512 + wkk],
                            ALU.add), waits=wv,
                            inc=("pV" if kk == nks - 1 else None))
                    sbank_free = ta
                    em.op("ve", lambda E, kw=kw: E.reduce_max(
                        stats[:, 32:33], ssb[:, 0:kw], axis=AX))
                    tneg = em.op("ve", lambda E: E.tensor_scalar_mul(
                        stats[:, 33:34], stats[:, 32:33], -1.0), inc="pV")
                    texp = em.op("sc", lambda E, kw=kw: E.activation(
                        psb[:, 0:kw], ssb[:, 0:kw], AF.Exp, bias=stats[:, 33:34]),
                        waits=[tneg] + ([tpn_prev] if tpn_prev else []), inc="pS")
                    texp_prev = texp
                    em.op("ve", lambda E, kw=kw: E.reduce_sum(
                        stats[:, 34:35], psb[:, 0:kw], axis=AX), waits=[texp])
                    trc = em.op("ve", lambda E: E.reciprocal(
                        stats[:, 35:36], stats[:, 34:35]), inc="pV")
                    tpn = em.op("ve", lambda E, kw=kw: E.tensor_scalar(
                        scr16[:, 0:kw], psb[:, 0:kw], stats[:, 35:36], None, ALU.mult),
                        waits=[trc, em.tok("pT")], inc="pV")
                    tpn_prev = tpn
                    for r0 in range(0, nkc, 8):
                        n = min(8, nkc - r0)
                        last = None
                        for icx in range(n):
                            kcix = r0 + icx
                            wv = [tpn] + ([pt_ev] if (pt_ev and icx == 0) else [])
                            last = em.op("te", lambda E, kcix=kcix, icx=icx: E.transpose(
                                pTb[:, icx * P:(icx + 1) * P],
                                scr16[:, kcix * P:(kcix + 1) * P], identb[:]),
                                waits=wv, inc=("pT" if icx == n - 1 else None))
                        dview = ptb[:].rearrange("p (c t) -> p c t", t=512)[
                            :, r0:r0 + n, qt * P:(qt + 1) * P]
                        sview = pTb[:, 0:n * P].rearrange("p (c t) -> p c t", t=P)
                        pt_ev = em.op("sc", lambda E, d=dview, s=sview: E.copy(d, s),
                                      waits=[last], inc="pS")
                mm = None
                for kcix in range(nkc):
                    wv = []
                    if kcix == 0:
                        wv = [pt_ev, tv] + ([pC_free] if pC_free else [])
                    mm = em.op("te", lambda E, kcix=kcix, h=h, qw=qw, nkc=nkc: E.matmul(
                        pC[:, 0:qw],
                        vwin[:, kcix * D + h * P:kcix * D + (h + 1) * P],
                        ptb[:, kcix * 512:kcix * 512 + qw],
                        start=(kcix == 0), stop=(kcix == nkc - 1)),
                        waits=wv, inc=("pT" if kcix == nkc - 1 else None))
                sl = h % 2
                ob = scr16[:, 1024 + sl * 512:1024 + (sl + 1) * 512]
                cp = em.op("sc", lambda E, ob=ob, qw=qw: E.copy(
                    ob[:, 0:qw], pC[:, 0:qw]),
                    waits=[mm] + ([aofree[sl]] if aofree[sl] else []), inc="pS")
                pC_free = cp
                aofree[sl] = em.dma("sc", aod[h * P:(h + 1) * P, qc0:qc0 + qw],
                                    ob[:, 0:qw], waits=[cp])
            kv_use = em.tok("pT")
            m_use = em.tok("pV")
        em.barrier()
        s2.close()

        def wo_residual(wo_ap, ao_dram, ao_sb, res_dram, res_row0, out_dram, rows,
                        wofull, aot, x0t, ysb, x1t, preload_tok=None):
            tw = preload_tok if preload_tok is not None else load_wfull(wofull, wo_ap)
            aofree = None
            x0free = [None] * 2
            pacc_free = {id(pE): None, id(pF): None}
            ysb_free = None
            pTf_free2 = None
            for (t0c, w) in ttiles(rows, 256):
                nsub = w // P
                ta = None
                if ao_dram is not None:
                    src = ao_dram.rearrange("(c p) t -> p c t", p=P)[:, :, t0c:t0c + w]
                    dvw = aot[:].rearrange("p (c t) -> p c t", t=512)[:, :, 0:w]
                    ta = em.dma("sy", dvw, src, waits=([aofree] if aofree else []))
                tx0 = []
                for sub in range(nsub):
                    tx0.append(em.dma(
                        "sy", x0t[:, sub * D:(sub + 1) * D],
                        res_dram[res_row0 + t0c + sub * P:res_row0 + t0c + (sub + 1) * P, :],
                        waits=([x0free[sub]] if x0free[sub] else [])))
                tt_last = None
                for oc in range(16):
                    pb = pE if oc % 2 == 0 else pF
                    mm = None
                    for ic in range(16):
                        wv = []
                        if ic == 0:
                            wv = [tw] + ([ta] if ta else [])
                            if pacc_free[id(pb)]:
                                wv.append(pacc_free[id(pb)])
                        if ao_dram is not None:
                            rhs = aot[:, ic * 512:ic * 512 + w]
                        else:
                            rhs = ao_sb[:, ic * 1024 + t0c:ic * 1024 + t0c + w]
                        mm = em.op("te", lambda E, ic=ic, oc=oc, pb=pb, rhs=rhs, w=w: E.matmul(
                            pb[:, 0:w],
                            wofull[:, ic * D + oc * P:ic * D + (oc + 1) * P],
                            rhs, start=(ic == 0), stop=(ic == 15)),
                            waits=wv, inc=("pT" if ic == 15 else None))
                    cp = em.op("sc", lambda E, pb=pb, w=w: E.copy(
                        ysb[:, 0:w], pb[:, 0:w]),
                        waits=[mm] + ([ysb_free] if ysb_free else []), inc="pS")
                    pacc_free[id(pb)] = cp
                    tlast_tr = None
                    for sub in range(nsub):
                        wv = [cp] + ([pTf_free2] if (pTf_free2 and sub == 0) else [])
                        tlast_tr = em.op("te", lambda E, sub=sub: E.transpose(
                            pTf[:, sub * P:(sub + 1) * P],
                            ysb[:, sub * P:(sub + 1) * P], identf[:]),
                            waits=wv, inc=("pT" if sub == nsub - 1 else None))
                    ysb_free = tlast_tr
                    for sub in range(nsub):
                        wv = [tlast_tr, tx0[sub]]
                        tt_last = em.op("ve", lambda E, sub=sub, oc=oc: E.tensor_tensor(
                            x1t[:, sub * D + oc * P:sub * D + (oc + 1) * P],
                            pTf[:, sub * P:(sub + 1) * P],
                            x0t[:, sub * D + oc * P:sub * D + (oc + 1) * P],
                            ALU.add), waits=wv,
                            inc=("pV" if sub == nsub - 1 else None))
                    pTf_free2 = tt_last
                for sub in range(nsub):
                    st = em.dma("sc", out_dram[t0c + sub * P:t0c + (sub + 1) * P, :],
                                x1t[:, sub * D:(sub + 1) * D], waits=[tt_last])
                    x0free[sub] = st
                if ao_dram is not None:
                    aofree = em.tok("pT")

        s2b = ExitStack()
        aot1 = s2b.enter_context(nc.sbuf_tensor("aot1", [P, 16 * 512], BF16))
        x0t1 = s2b.enter_context(nc.sbuf_tensor("x0t1", [P, 2 * D], F32))
        ysb1 = s2b.enter_context(nc.sbuf_tensor("ysb1", [P, 512], F32))
        x1t1 = s2b.enter_context(nc.sbuf_tensor("x1t1", [P, 2 * D], F32))
        wo_residual(woa, aod, None, x0, 512, X1d, T1, wofl, aot1, x0t1, ysb1, x1t1,
                    preload_tok=tw_wo)
        em.barrier()
        s2b.close()
        sWO.close()

        def emit_ffn(xin_dram, xout_dram, halves, tag, cw_slot0=0):
            sf = ExitStack()
            xln = sf.enter_context(nc.sbuf_tensor("f_xln" + tag, [P, 16 * 1024], BF16))
            X2H = sf.enter_context(nc.sbuf_tensor("f_X2H" + tag, [P, 16 * 1024], F32))
            g_sb = sf.enter_context(nc.sbuf_tensor("f_g" + tag, [P, 8 * 1024], BF16))
            w13r = sf.enter_context(nc.sbuf_tensor("f_w13" + tag, [P, 2 * 4096], BF16))
            we2r = sf.enter_context(nc.sbuf_tensor("f_we2" + tag, [P, 8 * D], BF16))
            cwbc = sf.enter_context(nc.sbuf_tensor("f_cwbc" + tag, [P, 1024], BF16))
            xinb = sf.enter_context(nc.sbuf_tensor("f_xin" + tag, [P, D], F32))
            x32f = sf.enter_context(nc.sbuf_tensor("f_x32f" + tag, [P, 16 * P], F32))
            t1sb = sf.enter_context(nc.sbuf_tensor("f_t1" + tag, [P, 512], F32))
            tmsb = sf.enter_context(nc.sbuf_tensor("f_tm" + tag, [P, 512], F32))
            xout = xinb

            for (h0, H) in halves:
                tiles = ttiles(H)
                xin_free = []
                scr16_war = []
                scr32_war = []
                rtr_war = []
                pTb_free = None
                pTf_free = None
                for t in range(H // P):
                    sl = t % 2
                    xv = xinb[:, 0:D]
                    tin = em.dma("sy", xv, xin_dram[h0 + t * P:h0 + (t + 1) * P, :],
                                 waits=xin_free)
                    toks = ln_tile(xv, scr32[:, 0:D], sl, scr16[:, 0:D], tin,
                                   war=scr16_war, xln32_ap=scr32[:, 0:D],
                                   sq_war=scr32_war, war32=rtr_war)
                    ev0, pTf_free = transpose16(xinb, 0, X2H, 1024, t * P,
                                                tin, pTf, pTf_free, False)
                    xin_free = [toks[0], em.tok("pT")]
                    ev2, pTb_free = transpose16(scr16, 0, xln, 1024, t * P,
                                                toks[0], pTb, pTb_free, True)
                    scr16_war = [em.tok("pT")]
                    ev3, pTf_free = transpose16(scr32, 0, x32f, P, 0,
                                                toks[1], pTf, pTf_free, False)
                    scr32_war = [em.tok("pT")]
                    mm = None
                    for dc in range(16):
                        mm = em.op("te", lambda E, dc=dc: E.matmul(
                            pD[:, 0:16], x32f[:, dc * P:(dc + 1) * P],
                            wrsb[:, dc * 16:(dc + 1) * 16],
                            start=(dc == 0), stop=(dc == 15)),
                            waits=([ev3] if dc == 0 else []),
                            inc=("pT" if dc == 15 else None))
                    em.op("ve", lambda E: E.reduce_max(stats[:, 40:41], pD[:, 0:16], axis=AX),
                          waits=[mm] + rtr_war)
                    tneg = em.op("ve", lambda E: E.tensor_scalar_mul(
                        stats[:, 41:42], stats[:, 40:41], -1.0), inc="pV")
                    texp = em.op("sc", lambda E: E.activation(
                        scr32[:, 0:16], pD[:, 0:16], AF.Exp, bias=stats[:, 41:42]),
                        waits=[tneg] + scr32_war, inc="pS")
                    rtr_war = [texp]
                    em.op("ve", lambda E: E.reduce_sum(stats[:, 42:43], scr32[:, 0:16], axis=AX),
                          waits=[texp])
                    em.op("ve", lambda E: E.reciprocal(stats[:, 43:44], stats[:, 42:43]))
                    em.op("ve", lambda E: E.tensor_scalar(
                        scr32[:, 0:16], scr32[:, 0:16], stats[:, 43:44], None, ALU.mult))
                    em.op("ve", lambda E: E.tensor_copy(scr32[:, 16:32], scr32[:, 0:16]))
                    em.op("ve", lambda E: E.memset(scr32[:, 32:48], 0.0))
                    for r in range(4):
                        em.op("ve", lambda E: E.reduce_max(stats[:, 44:45],
                                                           scr32[:, 16:32], axis=AX))
                        em.op("ve", lambda E: E.tensor_scalar(
                            scr32[:, 48:64], scr32[:, 16:32], stats[:, 44:45], None,
                            ALU.is_ge))
                        em.op("ve", lambda E: E.tensor_mul(
                            scr32[:, 64:80], scr32[:, 48:64], scr32[:, 0:16]))
                        em.op("ve", lambda E: E.tensor_add(
                            scr32[:, 32:48], scr32[:, 32:48], scr32[:, 64:80]))
                        em.op("ve", lambda E: E.scalar_tensor_tensor(
                            scr32[:, 16:32], scr32[:, 48:64], NEG, scr32[:, 16:32],
                            ALU.mult, ALU.add))
                    em.op("ve", lambda E: E.reduce_sum(stats[:, 45:46], scr32[:, 32:48], axis=AX))
                    em.op("ve", lambda E: E.reciprocal(stats[:, 46:47], stats[:, 45:46]))
                    tcw = em.op("ve", lambda E: E.tensor_scalar(
                        scr32[:, 32:48], scr32[:, 32:48], stats[:, 46:47], None, ALU.mult),
                        inc="pV")
                    ttr = em.op("te", lambda E: E.transpose(
                        pTf[0:16, 0:P], scr32[:, 32:48], identf[:]),
                        waits=[tcw, pTf_free], inc="pT")
                    pTf_free = em.op("sc", lambda E, t=t: E.copy(
                        cwT[0:16, t * P:(t + 1) * P], pTf[0:16, 0:P]),
                        waits=[ttr], inc="pS")
                    scr32_war = [toks[-1], tcw, em.tok("pT")]
                if _DEBUG:
                    em.dma("sc", cwdbg[cw_slot0 + halves.index((h0, H)), :, 0:H],
                           cwT[0:16, 0:H])
                em.barrier()

                units = [("sh", su) for su in range(4)] + [("ex", e) for e in range(NE)]
                w13_free = [[], []]
                we2_free = []
                uv_free = {id(pA): None, id(pB): None, id(pC): None, id(pD): None}
                acc_free = {id(pE): None, id(pF): None}
                g_war = []
                cwbc_war = []
                pTf_cw_free = None
                prev_g = None
                for (kind, ue) in units:
                    tcwb = None
                    if kind == "ex":
                        last_cwb = pTf_cw_free
                        for (t0c, w) in tiles:
                            m1 = em.op("te", lambda E, t0c=t0c, w=w, ue=ue: E.matmul(
                                pTf[:, 0:w], selsb[0:16, ue * P:(ue + 1) * P],
                                cwT[0:16, t0c:t0c + w], start=True, stop=True),
                                waits=([last_cwb] if last_cwb else []),
                                inc="pT")
                            tcwb = em.op("sc", lambda E, t0c=t0c, w=w: E.copy(
                                cwbc[:, t0c:t0c + w], pTf[:, 0:w]),
                                waits=[m1] + (cwbc_war if t0c == 0 else []), inc="pS")
                            last_cwb = tcwb
                    glast = None
                    for hc in range(8):
                        sl = hc % 2
                        if kind == "ex":
                            wsrc = w13c[ue, hc].rearrange("u c p h -> p u c h")
                            ctok = castk[("ex", ue)]
                        else:
                            wsrc = ws13c[ue * 8 + hc].rearrange("u c p h -> p u c h")
                            ctok = castk["sh"]
                        wdst = w13r[:, sl * 4096:(sl + 1) * 4096].rearrange(
                            "p (u c h) -> p u c h", u=2, h=P)
                        tw = em.dma("sy" if hc % 2 == 0 else "sc", wdst, wsrc,
                                    waits=list(w13_free[sl]) + [ctok])
                        for ti, (t0c, w) in enumerate(tiles):
                            pu = pA if ti % 2 == 0 else pC
                            pv = pB if ti % 2 == 0 else pD
                            mmu = mmv = None
                            for dc in range(16):
                                wv = []
                                if dc == 0:
                                    wv = [tw] + ([uv_free[id(pu)]] if uv_free[id(pu)] else [])
                                mmu = em.op("te", lambda E, dc=dc, w=w, t0c=t0c, pu=pu,
                                            sl=sl: E.matmul(
                                    pu[:, 0:w],
                                    w13r[:, sl * 4096 + dc * P:sl * 4096 + (dc + 1) * P],
                                    xln[:, dc * 1024 + t0c:dc * 1024 + t0c + w],
                                    start=(dc == 0), stop=(dc == 15)),
                                    waits=wv, inc=("pT" if dc == 15 else None))
                            for dc in range(16):
                                wv = ([uv_free[id(pv)]] if (uv_free[id(pv)] and dc == 0) else [])
                                mmv = em.op("te", lambda E, dc=dc, w=w, t0c=t0c, pv=pv,
                                            sl=sl: E.matmul(
                                    pv[:, 0:w],
                                    w13r[:, sl * 4096 + 2048 + dc * P:sl * 4096 + 2048 + (dc + 1) * P],
                                    xln[:, dc * 1024 + t0c:dc * 1024 + t0c + w],
                                    start=(dc == 0), stop=(dc == 15)),
                                    waits=wv, inc=("pT" if dc == 15 else None))
                            tsl = em.op("sc", lambda E, pu=pu, w=w: E.activation(
                                t1sb[:, 0:w], pu[:, 0:w], AF.Silu),
                                waits=[mmu] + ([prev_g] if prev_g else []) + g_war,
                                inc="pS")
                            uv_free[id(pu)] = tsl
                            if kind == "ex":
                                ttm = em.op("ve", lambda E, pv=pv, w=w, t0c=t0c: E.tensor_tensor(
                                    tmsb[:, 0:w], pv[:, 0:w], cwbc[:, t0c:t0c + w],
                                    ALU.mult),
                                    waits=[mmv, tcwb] + ([prev_g] if prev_g else []),
                                    inc="pV")
                                uv_free[id(pv)] = ttm
                                glast = em.op("gp", lambda E, hc=hc, t0c=t0c, w=w: E.tensor_tensor(
                                    g_sb[:, hc * 1024 + t0c:hc * 1024 + t0c + w],
                                    t1sb[:, 0:w], tmsb[:, 0:w], ALU.mult),
                                    waits=[tsl, ttm], inc="pP")
                            else:
                                glast = em.op("ve", lambda E, hc=hc, t0c=t0c, w=w,
                                              pv=pv: E.tensor_tensor(
                                    g_sb[:, hc * 1024 + t0c:hc * 1024 + t0c + w],
                                    t1sb[:, 0:w], pv[:, 0:w], ALU.mult),
                                    waits=[tsl, mmv], inc="pV")
                                uv_free[id(pv)] = glast
                            prev_g = glast
                        w13_free[sl] = [em.tok("pT")]
                    g_war = []
                    tw2s = []
                    for g2 in range(2):
                        if kind == "ex":
                            src = we2c[ue, g2 * 512:(g2 + 1) * 512, :].rearrange(
                                "(c p) o -> p c o", p=P)
                            ctok2 = castk[("ex2", ue)]
                        else:
                            src = ws2c[ue * 1024 + g2 * 512:ue * 1024 + (g2 + 1) * 512, :].rearrange(
                                "(c p) o -> p c o", p=P)
                            ctok2 = castk["sh2"]
                        dvw = we2r[:].rearrange("p (c o) -> p c o", o=D)[:, g2 * 4:(g2 + 1) * 4, :]
                        tw2s.append(em.dma("sc" if g2 == 0 else "sy", dvw, src,
                                           waits=list(we2_free) + [ctok2]))
                    for (t0c, w) in tiles:
                        for oc in range(16):
                            pb = pE if oc % 2 == 0 else pF
                            mm = None
                            for hc in range(8):
                                wv = []
                                if hc == 0:
                                    wv = tw2s + [glast]
                                    if acc_free[id(pb)]:
                                        wv.append(acc_free[id(pb)])
                                mm = em.op("te", lambda E, hc=hc, oc=oc, t0c=t0c, w=w,
                                           pb=pb: E.matmul(
                                    pb[:, 0:w],
                                    we2r[:, hc * D + oc * P:hc * D + (oc + 1) * P],
                                    g_sb[:, hc * 1024 + t0c:hc * 1024 + t0c + w],
                                    start=(hc == 0), stop=(hc == 7)),
                                    waits=wv, inc=("pT" if hc == 7 else None))
                            if kind == "sh":
                                ev = em.op("ve", lambda E, oc=oc, t0c=t0c, w=w,
                                           pb=pb: E.scalar_tensor_tensor(
                                    X2H[:, oc * 1024 + t0c:oc * 1024 + t0c + w],
                                    pb[:, 0:w], 0.2,
                                    X2H[:, oc * 1024 + t0c:oc * 1024 + t0c + w],
                                    ALU.mult, ALU.add), waits=[mm], inc="pV")
                            else:
                                ev = em.op("ve", lambda E, oc=oc, t0c=t0c, w=w,
                                           pb=pb: E.tensor_tensor(
                                    X2H[:, oc * 1024 + t0c:oc * 1024 + t0c + w],
                                    X2H[:, oc * 1024 + t0c:oc * 1024 + t0c + w],
                                    pb[:, 0:w], ALU.add), waits=[mm], inc="pV")
                            acc_free[id(pb)] = ev
                    g_war = [em.tok("pT")]
                    we2_free = [em.tok("pT")]
                    cwbc_war = [em.tok("pV")]
                    pTf_cw_free = em.tok("pS")
                em.barrier()
                pTf_free = None
                xo_free = []
                for t in range(H // P):
                    last_cp = None
                    for r0 in range(0, 16, 4):
                        last = None
                        for i2 in range(4):
                            oc = r0 + i2
                            wv = []
                            if i2 == 0:
                                wv = ([pTf_free] if pTf_free else []) + \
                                     (xo_free if r0 == 0 else [])
                            last = em.op("te", lambda E, oc=oc, i2=i2, t=t: E.transpose(
                                pTf[:, i2 * P:(i2 + 1) * P],
                                X2H[:, oc * 1024 + t * P:oc * 1024 + (t + 1) * P],
                                identf[:]), waits=wv, inc=("pT" if i2 == 3 else None))
                        last_cp = em.op("sc", lambda E, r0=r0: E.copy(
                            xout[:, r0 * P:(r0 + 4) * P],
                            pTf[:, 0:512]), waits=[last], inc="pS")
                        pTf_free = last_cp
                    st = em.dma("sc", xout_dram[h0 + t * P:h0 + (t + 1) * P, :],
                                xout[:, 0:D], waits=[last_cp])
                    xo_free = [st]
                em.barrier()
            sf.close()

        emit_ffn(X1d, X2d, [(0, 1024), (1024, 768)], "2", cw_slot0=0)

        s4keep = ExitStack()
        s4 = ExitStack()
        s4p = ExitStack()
        xln3 = s4keep.enter_context(nc.sbuf_tensor("xln3", [P, 16 * T1], BF16))
        k3 = s4.enter_context(nc.sbuf_tensor("k3", [P, 4 * T1], BF16))
        q3 = s4.enter_context(nc.sbuf_tensor("q3", [P, 16 * 1024], BF16))
        v3 = s4.enter_context(nc.sbuf_tensor("v3", [P, 14 * 512], BF16))
        tab3 = s4p.enter_context(nc.sbuf_tensor("tab3", [P, 2 * TOWN + 2 * T1], BF16))
        xinb3 = s4p.enter_context(nc.sbuf_tensor("xinb3", [P, 2 * D], F32))
        wsl3 = s4p.enter_context(nc.sbuf_tensor("wsl3", [P, 2 * D], BF16))
        wv3sb = s4p.enter_context(nc.sbuf_tensor("wv3sb", [P, 16 * 512], BF16))
        rsc3 = s4p.enter_context(nc.sbuf_tensor("rsc3", [P, 2 * 512], F32))

        C3Q, S3Q, C3K, S3K = 0, TOWN, 2 * TOWN, 2 * TOWN + T1
        em.dma("gp", tab3[:, C3Q:C3Q + TOWN], qc3)
        em.dma("gp", tab3[:, S3Q:S3Q + TOWN], qs3)
        em.dma("gp", tab3[:, C3K:C3K + T1], kc3)
        em.dma("gp", tab3[:, S3K:S3K + T1], ks3)

        xin_free = [[], []]
        scr16_war = []
        sq_war = []
        pTb_free = None
        for t in range(T1 // P):
            sl = t % 2
            xv = xinb3[:, sl * D:(sl + 1) * D]
            tin = em.dma("sy", xv, X2d[t * P:(t + 1) * P, :], waits=xin_free[sl])
            toks = ln_tile(xv, scr32[:, 0:D], sl, scr16[:, 0:D], tin, war=scr16_war,
                           sq_war=sq_war)
            ev, pTb_free = transpose16(scr16, 0, xln3, T1, t * P,
                                       toks[0], pTb, pTb_free, True)
            xin_free[sl] = [toks[0]]
            scr16_war = [em.tok("pT")]
            sq_war = [toks[-1]]
        em.barrier()

        def rope3(ps, cc0, ss0, tcol, w, out_ap, waits):
            t1 = rsc3[:, 0:512]
            t2 = rsc3[:, 512:1024]
            em.op("ve", lambda E: E.tensor_mul(
                t1[:, 0:w], ps[:, 0:w], tab3[:, cc0 + tcol:cc0 + tcol + w]),
                waits=waits)
            em.op("ve", lambda E: E.tensor_mul(
                t2[0:64, 0:w], ps[64:128, 0:w], tab3[64:128, ss0 + tcol:ss0 + tcol + w]))
            em.op("ve", lambda E: E.tensor_mul(
                t2[64:128, 0:w], ps[0:64, 0:w], tab3[0:64, ss0 + tcol:ss0 + tcol + w]))
            em.op("ve", lambda E: E.tensor_tensor(
                out_ap[0:64, 0:w], t1[0:64, 0:w], t2[0:64, 0:w], ALU.subtract))
            return em.op("ve", lambda E: E.tensor_tensor(
                out_ap[64:128, 0:w], t1[64:128, 0:w], t2[64:128, 0:w], ALU.add),
                inc="pV")

        wfree = [[], []]
        pfree = {id(pA): None, id(pB): None}
        for h in range(NH):
            sl = h % 2
            tw = em.dma("gp", wsl3[:, sl * D:(sl + 1) * D].rearrange(
                "p (c h) -> p c h", h=P), wq3[h].rearrange("c p h -> p c h"),
                waits=wfree[sl])
            for (t0c, w) in ttiles(TOWN):
                pb = pA if (t0c // 512) % 2 == 0 else pB
                mm = None
                for dc in range(16):
                    wv = []
                    if dc == 0:
                        wv = [tw] + ([pfree[id(pb)]] if pfree[id(pb)] else [])
                    mm = em.op("te", lambda E, dc=dc, sl=sl, t0c=t0c, w=w, pb=pb: E.matmul(
                        pb[:, 0:w],
                        wsl3[:, sl * D + dc * P:sl * D + (dc + 1) * P],
                        xln3[:, dc * T1 + 512 + t0c:dc * T1 + 512 + t0c + w],
                        start=(dc == 0), stop=(dc == 15)),
                        waits=wv, inc=("pT" if dc == 15 else None))
                rt = rope3(pb, C3Q, S3Q, t0c, w,
                           q3[:, h * 1024 + t0c:h * 1024 + t0c + w], [mm])
                pfree[id(pb)] = rt
            wfree[sl] = [em.tok("pT")]
        for kv in range(NKV):
            sl = kv % 2
            tw = em.dma("gp", wsl3[:, sl * D:(sl + 1) * D].rearrange(
                "p (c h) -> p c h", h=P), wk3[kv].rearrange("c p h -> p c h"),
                waits=wfree[sl])
            for (t0c, w) in ttiles(T1):
                pb = pA if (t0c // 512) % 2 == 0 else pB
                mm = None
                for dc in range(16):
                    wv = []
                    if dc == 0:
                        wv = [tw] + ([pfree[id(pb)]] if pfree[id(pb)] else [])
                    mm = em.op("te", lambda E, dc=dc, sl=sl, t0c=t0c, w=w, pb=pb: E.matmul(
                        pb[:, 0:w],
                        wsl3[:, sl * D + dc * P:sl * D + (dc + 1) * P],
                        xln3[:, dc * T1 + t0c:dc * T1 + t0c + w],
                        start=(dc == 0), stop=(dc == 15)),
                        waits=wv, inc=("pT" if dc == 15 else None))
                rt = rope3(pb, C3K, S3K, t0c, w,
                           k3[:, kv * T1 + t0c:kv * T1 + t0c + w], [mm])
                pfree[id(pb)] = rt
            wfree[sl] = [em.tok("pT")]
        tv3 = None
        for g2 in range(4):
            src = wv3.rearrange("(c p) v -> p c v", p=P)[:, 4 * g2:4 * g2 + 4, :]
            dvw = wv3sb[:].rearrange("p (c v) -> p c v", v=512)[:, 4 * g2:4 * g2 + 4, :]
            tv3 = em.dma("gp", dvw, src)
        v3done = None
        for rt_i in range(T1 // P):
            pb = pA if rt_i % 2 == 0 else pB
            mm = None
            for dc in range(16):
                wv = []
                if dc == 0:
                    wv = [tv3] + ([pfree[id(pb)]] if pfree[id(pb)] else [])
                mm = em.op("te", lambda E, dc=dc, rt_i=rt_i, pb=pb: E.matmul(
                    pb[:],
                    xln3[:, dc * T1 + rt_i * P:dc * T1 + (rt_i + 1) * P],
                    wv3sb[:, dc * 512:(dc + 1) * 512],
                    start=(dc == 0), stop=(dc == 15)),
                    waits=wv, inc=("pT" if dc == 15 else None))
            cp = em.op("sc", lambda E, rt_i=rt_i, pb=pb: E.copy(
                v3[:, rt_i * 512:(rt_i + 1) * 512], pb[:]), waits=[mm], inc="pS")
            pfree[id(pb)] = cp
            v3done = cp
        em.barrier()
        s4p.close()

        s4c = ExitStack()
        m3sb = s4c.enter_context(nc.sbuf_tensor("m3sb", [P, 4 * T1], F32))
        ssb3 = s4c.enter_context(nc.sbuf_tensor("ssb3", [P, T1], F32))
        psb3 = s4c.enter_context(nc.sbuf_tensor("psb3", [P, T1], F32))
        AOC = 8 * 1024
        pt_ar = xln3
        nkc3 = T1 // P

        m_use3 = None
        pE_free = None
        for qg in range(2):
            msrc = m3.rearrange("(t p) k -> p t k", p=P)[:, qg * 4:(qg + 1) * 4, :]
            mdst = m3sb[:].rearrange("p (t k) -> p t k", k=T1)
            tm3 = em.dma("sy", mdst, msrc, waits=([m_use3] if m_use3 else []))
            for h in range(NH):
                kvh = h // 4
                pt_ev = None
                texp_prev = None
                tpn_prev = None
                sbank_free = None
                for qt in range(4):
                    qcol = qg * 512 + qt * P
                    banks = [pA, pB, pC, pD]
                    smm = []
                    for kk, (k0, kwd) in enumerate(ttiles(T1)):
                        pb = banks[kk]
                        wv = []
                        if kk == 0:
                            wv = [tm3] + ([sbank_free] if sbank_free else [])
                        smm.append(em.op("te", lambda E, k0=k0, kwd=kwd, pb=pb, h=h,
                                         qcol=qcol, kvh=kvh: E.matmul(
                            pb[:, 0:kwd],
                            q3[:, h * 1024 + qcol:h * 1024 + qcol + P],
                            k3[:, kvh * T1 + k0:kvh * T1 + k0 + kwd],
                            start=True, stop=True),
                            waits=wv, inc="pT"))
                    ta = None
                    for kk, (k0, kwd) in enumerate(ttiles(T1)):
                        pb = banks[kk]
                        wv = [smm[kk]] + ([texp_prev] if (texp_prev and kk == 0) else [])
                        ta = em.op("ve", lambda E, k0=k0, kwd=kwd, pb=pb, qt=qt: E.tensor_tensor(
                            ssb3[:, k0:k0 + kwd], pb[:, 0:kwd],
                            m3sb[:, qt * T1 + k0:qt * T1 + k0 + kwd], ALU.add),
                            waits=wv, inc=("pV" if kk == 3 else None))
                    sbank_free = ta
                    em.op("ve", lambda E: E.reduce_max(stats[:, 48:49], ssb3[:], axis=AX))
                    tneg = em.op("ve", lambda E: E.tensor_scalar_mul(
                        stats[:, 49:50], stats[:, 48:49], -1.0), inc="pV")
                    texp = em.op("sc", lambda E: E.activation(
                        psb3[:], ssb3[:], AF.Exp, bias=stats[:, 49:50]),
                        waits=[tneg] + ([tpn_prev] if tpn_prev else []), inc="pS")
                    texp_prev = texp
                    em.op("ve", lambda E: E.reduce_sum(stats[:, 50:51], psb3[:], axis=AX),
                          waits=[texp])
                    trc = em.op("ve", lambda E: E.reciprocal(stats[:, 51:52],
                                                             stats[:, 50:51]), inc="pV")
                    tpn = em.op("ve", lambda E: E.tensor_scalar(
                        scr16[:, 0:T1], psb3[:], stats[:, 51:52], None, ALU.mult),
                        waits=[trc, em.tok("pT")], inc="pV")
                    tpn_prev = tpn
                    for r0 in range(0, nkc3, 8):
                        n = min(8, nkc3 - r0)
                        last = None
                        for icx in range(n):
                            kcix = r0 + icx
                            wv = [tpn] + ([pt_ev] if (pt_ev and icx == 0) else [])
                            last = em.op("te", lambda E, kcix=kcix, icx=icx: E.transpose(
                                pTb[:, icx * P:(icx + 1) * P],
                                scr16[:, kcix * P:(kcix + 1) * P], identb[:]),
                                waits=wv, inc=("pT" if icx == n - 1 else None))
                        dview = pt_ar[:].rearrange("p (c t) -> p c t", t=512)[
                            :, r0:r0 + n, qt * P:(qt + 1) * P]
                        sview = pTb[:, 0:n * P].rearrange("p (c t) -> p c t", t=P)
                        pt_ev = em.op("sc", lambda E, d=dview, s=sview: E.copy(d, s),
                                      waits=[last], inc="pS")
                mm = None
                for kcix in range(nkc3):
                    wv = []
                    if kcix == 0:
                        wv = [pt_ev, v3done] + ([pE_free] if pE_free else [])
                    mm = em.op("te", lambda E, kcix=kcix, kvh=kvh, nkc3=nkc3: E.matmul(
                        pE[:],
                        v3[:, kcix * 512 + kvh * P:kcix * 512 + (kvh + 1) * P],
                        pt_ar[:, kcix * 512:(kcix + 1) * 512],
                        start=(kcix == 0), stop=(kcix == nkc3 - 1)),
                        waits=wv, inc=("pT" if kcix == nkc3 - 1 else None))
                pE_free = em.op("sc", lambda E, h=h, qg=qg: E.copy(
                    pt_ar[:, AOC + h * 1024 + qg * 512:AOC + h * 1024 + qg * 512 + 512],
                    pE[:]), waits=[mm], inc="pS")
            m_use3 = em.tok("pV")
        em.barrier()
        s4c.close()
        s4.close()

        s4d = ExitStack()
        wo3f = s4d.enter_context(nc.sbuf_tensor("wo3f", [P, 16 * D], BF16))
        x0t3 = s4d.enter_context(nc.sbuf_tensor("x0t3", [P, 2 * D], F32))
        ysb3 = s4d.enter_context(nc.sbuf_tensor("ysb3", [P, 512], F32))
        x1t3 = s4d.enter_context(nc.sbuf_tensor("x1t3", [P, 2 * D], F32))
        wo_residual(wo3, None, pt_ar[:, AOC:AOC + 16 * 1024], X2d, 512, X3d, TOWN,
                    wo3f, None, x0t3, ysb3, x1t3)
        em.barrier()
        s4d.close()
        s4keep.close()

        emit_ffn(X3d, y, [(0, 1024)], "4", cw_slot0=2)

        em.wait("sc", em.tok("dA"))
        em.wait("sy", em.tok("dS"))
        em.wait("gp", em.tok("dG"))

        def replay(name):
            def run(E):
                for f in em.q[name]:
                    f(E)
            return run

        with nc.Block() as blk:
            blk.sync(replay("sy"))
            blk.gpsimd(replay("gp"))
            blk.vector(replay("ve"))
            blk.scalar(replay("sc"))
            blk.tensor(replay("te"))
    return nc



def _rope_tables(pos, scale):
    inv = 1.0 / (10000.0 ** (np.arange(0, HD, 2, dtype=np.float64) / HD))
    ang = np.asarray(pos, np.float64)[None, :] * inv[:, None]
    c = np.cos(ang)
    s = np.sin(ang)
    C2 = np.concatenate([c, c], 0) * scale
    S2 = np.concatenate([s, s], 0) * scale
    return np.ascontiguousarray(C2, np.float32), np.ascontiguousarray(S2, np.float32)


def _headperm(w, nh):
    perm = np.r_[np.arange(0, 128, 2), np.arange(1, 128, 2)]
    return np.ascontiguousarray(
        w.reshape(w.shape[0], nh, 128)[:, :, perm].reshape(w.shape[0], -1))


def _prep_inputs(kw):
    f32 = np.float32
    x = np.ascontiguousarray(np.asarray(kw["L0_attn_a_x"], f32))
    Wq, Wk, Wv, Wo = (np.asarray(kw["L0_attn_a_" + n], f32)
                      for n in ("Wq", "Wk", "Wv", "Wo"))
    wq, wk, wv, wo = (np.asarray(kw["L0_attn_b_" + n], f32)
                      for n in ("wq", "wk", "wv", "wo"))
    wrt = np.asarray(kw["L0_ffn_w_router"], f32)
    we1, we3, we2 = (np.asarray(kw["L0_ffn_" + n], f32) for n in ("we1", "we3", "we2"))
    ws1, ws3, ws2 = (np.asarray(kw["L0_ffn_" + n], f32) for n in ("ws1", "ws3", "ws2"))

    shared = {
        "wqa": _headperm(np.ascontiguousarray(Wq.T), NH),
        "wka": _headperm(np.ascontiguousarray(Wk.T), NH),
        "wva": np.ascontiguousarray(Wv.T),
        "woa": np.ascontiguousarray(Wo.T),
        "wq3": np.ascontiguousarray(
            _headperm(wq, NH).reshape(16, P, NH, HD).transpose(2, 0, 1, 3)),
        "wk3": np.ascontiguousarray(
            _headperm(wk, NKV).reshape(16, P, NKV, HD).transpose(2, 0, 1, 3)),
        "wv3": np.ascontiguousarray(wv),
        "wo3": np.ascontiguousarray(wo),
        "wr": np.ascontiguousarray(wrt.reshape(16, P, 16)),
        "w13": np.ascontiguousarray(np.stack(
            [we1.reshape(NE, 16, P, 8, P).transpose(0, 3, 1, 2, 4),
             we3.reshape(NE, 16, P, 8, P).transpose(0, 3, 1, 2, 4)], axis=2)),
        "ws13": np.ascontiguousarray(np.stack(
            [ws1.reshape(16, P, 32, P).transpose(2, 0, 1, 3),
             ws3.reshape(16, P, 32, P).transpose(2, 0, 1, 3)], axis=1)),
        "we2": np.ascontiguousarray(we2),
        "ws2": np.ascontiguousarray(ws2),
    }
    sel = np.zeros((16, D), f32)
    for e in range(16):
        sel[e, e * P:(e + 1) * P] = 0.8
    shared["sel"] = sel

    in_maps = []
    for c in range(NCORES):
        b, j = divmod(c, 4)
        q0 = TOWN * j
        pos0 = np.concatenate([np.arange(q0 - 1024, q0 + 1024), np.arange(FRONT)])
        pos1 = pos0[512:]
        posq = np.arange(q0, q0 + TOWN)
        x0c = np.zeros((T0, D), f32)
        lo = q0 - 1024
        s_lo = max(0, lo)
        x0c[(s_lo - lo):2048] = x[b, s_lo:q0 + 1024]
        x0c[2048:] = x[b, :FRONT]

        sc = 1.0 / np.sqrt(HD)
        qc1_, qs1_ = _rope_tables(np.maximum(pos1, 0), sc)
        kc1_, ks1_ = _rope_tables(np.maximum(pos0, 0), 1.0)
        qc3_, qs3_ = _rope_tables(posq, sc)
        kc3_, ks3_ = _rope_tables(np.maximum(pos1, 0), 1.0)

        m1m = np.full((1536, 1024), NEG, f32)
        for n in range(3):
            qp = pos1[n * 512:(n + 1) * 512][:, None]
            kp = (q0 - 1024 + n * 512 + np.arange(1024))[None, :]
            ok = (kp <= qp) & (kp > qp - WIN) & (kp >= 0)
            m1m[n * 512:(n + 1) * 512] = np.where(ok, 0.0, NEG)
        qf = np.arange(FRONT)[:, None]
        kf = np.arange(FRONT)[None, :]
        m1f = np.where(kf <= qf, 0.0, NEG).astype(f32)

        qp = posq[:, None]
        kp_main = (q0 - 512 + np.arange(1536))[None, :]
        band = (kp_main <= qp) & (kp_main > qp - WIN)
        fok = (kp_main < FRONT) & (kp_main <= qp - WIN)
        main_ok = (band | fok) & (kp_main >= 0)
        kp_f = np.arange(FRONT)[None, :]
        bandf = (kp_f <= qp) & (kp_f > qp - WIN)
        fokf = (kp_f < FRONT) & (kp_f <= qp - WIN)
        front_ok = (bandf | fokf) if j > 0 else np.zeros((TOWN, FRONT), bool)
        m3c = np.concatenate([np.where(main_ok, 0.0, NEG),
                              np.where(front_ok, 0.0, NEG)], axis=1).astype(f32)

        im = dict(shared)
        im.update(x0=x0c, m1m=m1m, m1f=m1f, m3=m3c,
                  qc1=qc1_, qs1=qs1_, kc1=kc1_, ks1=ks1_,
                  qc3=qc3_, qs3=qs3_, kc3=kc3_, ks3=ks3_)
        in_maps.append(im)
    return in_maps


def kernel(**kw):
    in_maps = _prep_inputs(kw)
    nc = build_program()
    res = run_bass_kernel_spmd(nc, in_maps, core_ids=list(range(NCORES)))
    out = np.empty((2, SEQ, D), np.float32)
    for c in range(NCORES):
        b, j = divmod(c, 4)
        out[b, TOWN * j:TOWN * (j + 1)] = res.results[c]["y"]
    return out


# version 05
# speedup vs baseline: 1.1322x; 1.0223x over previous
import numpy as np
from contextlib import ExitStack

import concourse.bass as bass
import concourse.mybir as mybir
from concourse.bass_utils import run_bass_kernel_spmd

F32 = mybir.dt.float32
BF16 = mybir.dt.bfloat16
AF = mybir.ActivationFunctionType
ALU = mybir.AluOpType
AX = mybir.AxisListType.X

_DEBUG = False
_RACE_DETECT = False

P = 128
NCORES = 8
SEQ = 4096
D = 2048
T0, T1, TOWN = 2304, 1792, 1024
NH, HD = 16, 128
NKV = 4
WIN, FRONT = 512, 256
NE, HE, HS = 16, 1024, 4096
NEG = -1.0e30

ENGS = ("sy", "gp", "ve", "sc", "te")
DMASEM = {"sy": "dS", "gp": "dG", "sc": "dA"}
SEMS = ("dS", "dG", "dA", "pT", "pV", "pS", "pP")


def ttiles(total, step=512):
    out, o = [], 0
    while o < total:
        w = min(step, total - o)
        out.append((o, w))
        o += w
    return out


AUTOSER = {"ve": "pV", "sc": "pS", "gp": "pP"}


class Emit:

    def __init__(self, nc, sems):
        self.nc = nc
        self.sems = sems
        self.q = {e: [] for e in ENGS}
        self.cnt = {s: 0 for s in SEMS}
        self.waited = {e: {} for e in ENGS}
        self.last = {e: None for e in ENGS}

    def wait(self, eng, tok):
        if tok is None:
            return
        s, v = tok
        if v <= 0 or self.waited[eng].get(s, 0) >= v:
            return
        self.waited[eng][s] = v
        h = self.sems[s]
        self.q[eng].append(lambda E, h=h, v=v: E.wait_ge(h, v))

    def op(self, eng, fn, waits=(), inc=None):
        for t in waits:
            self.wait(eng, t)
        if eng in AUTOSER:
            self.wait(eng, self.last[eng])
            if inc is None:
                inc = AUTOSER[eng]
        if inc is None:
            self.q[eng].append(fn)
            return None
        amt = 16 if inc in ("dS", "dG", "dA") else 1
        self.cnt[inc] += amt
        h = self.sems[inc]
        self.q[eng].append(lambda E, fn=fn, h=h, amt=amt: fn(E).then_inc(h, amt))
        tok = (inc, self.cnt[inc])
        if eng in AUTOSER and inc not in ("dS", "dG", "dA"):
            self.last[eng] = tok
        return tok

    def dma(self, eng, out, in_, waits=()):
        return self.op(eng, lambda E, o=out, i=in_: E.dma_start(out=o, in_=i),
                       waits=waits, inc=DMASEM[eng])

    def tok(self, sem):
        return (sem, self.cnt[sem])

    def snapshot(self):
        return [(s, c) for s, c in self.cnt.items() if c > 0]

    def barrier(self):
        toks = self.snapshot()
        for e in ENGS:
            for t in toks:
                self.wait(e, t)



def build_program():
    nc = bass.Bass("TRN2", target_bir_lowering=False,
                   detect_race_conditions=_RACE_DETECT)

    def din(name, shape, dt=F32):
        return nc.dram_tensor(name, list(shape), dt, kind="ExternalInput").ap()

    x0 = din("x0", (T0, D))
    m1m = din("m1m", (1536, 1024))
    m1f = din("m1f", (FRONT, FRONT))
    m3 = din("m3", (TOWN, T1))
    qc1 = din("qc1", (P, T1)); qs1 = din("qs1", (P, T1))
    kc1 = din("kc1", (P, T0)); ks1 = din("ks1", (P, T0))
    qc3 = din("qc3", (P, TOWN)); qs3 = din("qs3", (P, TOWN))
    kc3 = din("kc3", (P, T1)); ks3 = din("ks3", (P, T1))
    seli = din("sel", (16, D))
    wqa = din("wqa", (D, D)); wka = din("wka", (D, D))
    wva = din("wva", (D, D)); woa = din("woa", (D, D))
    wq3 = din("wq3", (NH, 16, P, HD))
    wk3 = din("wk3", (NKV, 16, P, HD))
    wv3 = din("wv3", (D, NKV * HD))
    wo3 = din("wo3", (D, D))
    wr = din("wr", (16, P, 16))
    w13 = din("w13", (NE, 8, 2, 16, P, P))
    ws13 = din("ws13", (32, 2, 16, P, P))
    we2 = din("we2", (NE, HE, D))
    ws2 = din("ws2", (HS, D))

    w13c = nc.dram_tensor("w13c", [NE, 8, 2, 16, P, P], BF16, kind="Internal").ap()
    ws13c = nc.dram_tensor("ws13c", [32, 2, 16, P, P], BF16, kind="Internal").ap()
    we2c = nc.dram_tensor("we2c", [NE, HE, D], BF16, kind="Internal").ap()
    ws2c = nc.dram_tensor("ws2c", [HS, D], BF16, kind="Internal").ap()

    y = nc.dram_tensor("y", [TOWN, D], F32, kind="ExternalOutput").ap()

    kd = "ExternalOutput" if _DEBUG else "Internal"
    q1d = nc.dram_tensor("q1d", [D, T1], BF16, kind=kd).ap()
    k1d = nc.dram_tensor("k1d", [D, T0], BF16, kind=kd).ap()
    v1d = nc.dram_tensor("v1d", [T0, D], BF16, kind=kd).ap()
    aod = nc.dram_tensor("aod", [D, T1], BF16, kind=kd).ap()
    X1d = nc.dram_tensor("X1d", [T1, D], F32, kind=kd).ap()
    X2d = nc.dram_tensor("X2d", [T1, D], F32, kind=kd).ap()
    X3d = nc.dram_tensor("X3d", [TOWN, D], F32, kind=kd).ap()
    cwdbg = nc.dram_tensor("cwdbg", [4, 16, T1], BF16, kind=kd).ap() if _DEBUG else None

    es = ExitStack()
    with es:
        sems = {s: es.enter_context(nc.semaphore(s)) for s in SEMS}
        em = Emit(nc, sems)

        identb = es.enter_context(nc.sbuf_tensor("identb", [P, P], BF16))
        identf = es.enter_context(nc.sbuf_tensor("identf", [P, P], F32))
        stats = es.enter_context(nc.sbuf_tensor("stats", [P, 64], F32))
        selsb = es.enter_context(nc.sbuf_tensor("selsb", [16, D], BF16))
        cwT = es.enter_context(nc.sbuf_tensor("cwT", [16, T1], BF16))
        wrsb = es.enter_context(nc.sbuf_tensor("wrsb", [P, 16 * 16], F32))
        scr16 = es.enter_context(nc.sbuf_tensor("scr16", [P, 2048], BF16))
        scr32 = es.enter_context(nc.sbuf_tensor("scr32", [P, 2048], F32))

        pA = es.enter_context(nc.psum_tensor("pA", [P, 512], F32))
        pB = es.enter_context(nc.psum_tensor("pB", [P, 512], F32))
        pC = es.enter_context(nc.psum_tensor("pC", [P, 512], F32))
        pD = es.enter_context(nc.psum_tensor("pD", [P, 512], F32))
        pE = es.enter_context(nc.psum_tensor("pE", [P, 512], F32))
        pF = es.enter_context(nc.psum_tensor("pF", [P, 512], F32))
        pTb = es.enter_context(nc.psum_tensor("pTb", [P, 1024], BF16))
        pTf = es.enter_context(nc.psum_tensor("pTf", [P, 512], F32))

        def mk_ident(t):
            em.op("gp", lambda E: E.memset(t[:], 0.0))
            return em.op("gp", lambda E: E.affine_select(
                out=t[:], in_=t[:], compare_op=ALU.not_equal,
                fill=1.0, base=0, pattern=[[-1, P]],
                channel_multiplier=1))
        mk_ident(identb)
        tok_id = mk_ident(identf)
        em.dma("gp", selsb[:], seli)
        em.dma("sy", wrsb[:].rearrange("p (c e) -> p c e", e=16),
               wr.rearrange("c p e -> p c e"))
        em.wait("te", tok_id)

        def ln_tile(xin_ap, sqbuf, slot, xln_bf_ap, xin_tok, war=(), xln32_ap=None,
                    sq_war=(), war32=()):
            c0 = slot * 8
            tsq = em.op("sc", lambda E: E.activation(sqbuf, xin_ap, AF.Square),
                        waits=[xin_tok] + list(sq_war), inc="pS")
            em.op("ve", lambda E: E.reduce_sum(stats[:, c0:c0 + 1], xin_ap, axis=AX),
                  waits=[xin_tok])
            tsqr = em.op("ve", lambda E: E.reduce_sum(stats[:, c0 + 1:c0 + 2], sqbuf,
                                                      axis=AX),
                         waits=[tsq], inc="pV")
            em.op("ve", lambda E: E.tensor_scalar_mul(stats[:, c0 + 2:c0 + 3],
                                                      stats[:, c0:c0 + 1], 1.0 / D))
            em.op("ve", lambda E: E.tensor_mul(stats[:, c0 + 3:c0 + 4],
                                               stats[:, c0 + 2:c0 + 3],
                                               stats[:, c0 + 2:c0 + 3]))
            tv = em.op("ve", lambda E: E.scalar_tensor_tensor(
                stats[:, c0 + 4:c0 + 5], stats[:, c0 + 1:c0 + 2], 1.0 / D,
                stats[:, c0 + 3:c0 + 4], ALU.mult, ALU.subtract), inc="pV")
            tve = em.op("ve", lambda E: E.tensor_scalar_add(
                stats[:, c0 + 4:c0 + 5], stats[:, c0 + 4:c0 + 5], 1e-5), inc="pV")
            tsd = em.op("sc", lambda E: E.activation(stats[:, c0 + 5:c0 + 6],
                                                     stats[:, c0 + 4:c0 + 5],
                                                     AF.Sqrt),
                        waits=[tve], inc="pS")
            tr = em.op("ve", lambda E: E.reciprocal(stats[:, c0 + 6:c0 + 7],
                                                    stats[:, c0 + 5:c0 + 6]),
                       waits=[tsd], inc="pV")
            toks = []
            t1 = em.op("ve", lambda E: E.tensor_scalar(
                xln_bf_ap, xin_ap, stats[:, c0 + 2:c0 + 3], stats[:, c0 + 6:c0 + 7],
                ALU.subtract, ALU.mult), waits=[tr] + list(war), inc="pV")
            toks.append(t1)
            if xln32_ap is not None:
                t2 = em.op("ve", lambda E: E.tensor_scalar(
                    xln32_ap, xin_ap, stats[:, c0 + 2:c0 + 3], stats[:, c0 + 6:c0 + 7],
                    ALU.subtract, ALU.mult), waits=list(war32), inc="pV")
                toks.append(t2)
            toks.append(tsqr)
            return toks

        def transpose16(src_sb, src_col0, dst_t, dst_stride, dst_col,
                        src_tok, psum, psum_free, use_bf, width=2048):
            ident = identb[:] if use_bf else identf[:]
            per = 8 if use_bf else 4
            nch = width // P
            ev_tok = None
            for r0 in range(0, nch, per):
                n = min(per, nch - r0)
                last = None
                for i in range(n):
                    c = r0 + i
                    w = [src_tok, psum_free] if i == 0 else []
                    last = em.op("te", lambda E, c=c, i=i, ident=ident: E.transpose(
                        psum[:, i * P:(i + 1) * P],
                        src_sb[:, src_col0 + c * P:src_col0 + (c + 1) * P], ident),
                        waits=w, inc=("pT" if i == n - 1 else None))
                dview = dst_t[:].rearrange("p (c t) -> p c t", t=dst_stride)[
                    :, r0:r0 + n, dst_col:dst_col + P]
                sview = psum[:, 0:n * P].rearrange("p (c t) -> p c t", t=P)
                ev_tok = em.op("sc", lambda E, d=dview, s=sview: E.copy(d, s),
                               waits=[last], inc="pS")
                psum_free = ev_tok
            return ev_tok, psum_free

        s0 = ExitStack()
        xln1 = s0.enter_context(nc.sbuf_tensor("xln1", [P, 16 * T0], BF16))
        wful = s0.enter_context(nc.sbuf_tensor("wful", [P, 16 * D], BF16))
        tabs = s0.enter_context(nc.sbuf_tensor("tabs", [P, 2 * T0 + 2 * T1], BF16))
        xin2 = s0.enter_context(nc.sbuf_tensor("xin2", [P, 2 * D], F32))
        rsc = s0.enter_context(nc.sbuf_tensor("rsc", [P, 2 * 512], F32))
        qkout = s0.enter_context(nc.sbuf_tensor("qkout", [P, 2 * 512], BF16))

        KC0, KS0, QC0, QS0 = 0, T0, 2 * T0, 2 * T0 + T1
        em.dma("gp", tabs[:, KC0:KC0 + T0], kc1)
        em.dma("gp", tabs[:, KS0:KS0 + T0], ks1)
        em.dma("gp", tabs[:, QC0:QC0 + T1], qc1)
        em.dma("gp", tabs[:, QS0:QS0 + T1], qs1)

        def load_wfull(dst, wap, war=()):
            tk = None
            for g in range(4):
                src = wap.rearrange("(c p) o -> p c o", p=P)[:, 4 * g:4 * g + 4, :]
                dvw = dst[:].rearrange("p (c o) -> p c o", o=D)[:, 4 * g:4 * g + 4, :]
                tk = em.dma("gp", dvw, src, waits=(list(war) if g == 0 else []))
            return tk
        twq = load_wfull(wful, wqa)

        xin_free = [[], []]
        scr16_war = []
        sq_war = []
        pTb_free = None
        for t in range(T0 // P):
            sl = t % 2
            xv = xin2[:, sl * D:(sl + 1) * D]
            tin = em.dma("sy", xv, x0[t * P:(t + 1) * P, :], waits=xin_free[sl])
            toks = ln_tile(xv, scr32[:, 0:D], sl, scr16[:, 0:D], tin, war=scr16_war,
                           sq_war=sq_war)
            ev, pTb_free = transpose16(scr16, 0, xln1, T0, t * P,
                                       toks[0], pTb, pTb_free, True)
            xin_free[sl] = [toks[0]]
            scr16_war = [em.tok("pT")]
            sq_war = [toks[-1]]
        em.barrier()

        def rope_tile(ps, tab_t, cc0, ss0, tcol, w, out_ap, waits):
            t1 = rsc[:, 0:512]
            t2 = rsc[:, 512:1024]
            em.op("ve", lambda E: E.tensor_mul(
                t1[:, 0:w], ps[:, 0:w], tab_t[:, cc0 + tcol:cc0 + tcol + w]),
                waits=waits)
            em.op("ve", lambda E: E.tensor_mul(
                t2[0:64, 0:w], ps[64:128, 0:w],
                tab_t[64:128, ss0 + tcol:ss0 + tcol + w]))
            em.op("ve", lambda E: E.tensor_mul(
                t2[64:128, 0:w], ps[0:64, 0:w],
                tab_t[0:64, ss0 + tcol:ss0 + tcol + w]))
            em.op("ve", lambda E: E.tensor_tensor(
                out_ap[0:64, 0:w], t1[0:64, 0:w], t2[0:64, 0:w], ALU.subtract))
            return em.op("ve", lambda E: E.tensor_tensor(
                out_ap[64:128, 0:w], t1[64:128, 0:w], t2[64:128, 0:w], ALU.add),
                inc="pV")

        psum_free = {id(pA): None, id(pB): None}
        out_free = [None, None]
        i = 0
        for (proj, src_off, span, cc0, ss0, dst) in (
                ("q", 512, T1, QC0, QS0, q1d), ("k", 0, T0, KC0, KS0, k1d)):
            if proj == "k":
                load_wfull(wful, wka, war=[em.tok("pT")])
                em.wait("te", em.tok("dG"))
            for h in range(NH):
                for (t0c, w) in ttiles(span):
                    pb = pA if i % 2 == 0 else pB
                    mm = None
                    for dc in range(16):
                        wv = []
                        if dc == 0:
                            wv = [em.tok("dG")]
                            if psum_free[id(pb)]:
                                wv.append(psum_free[id(pb)])
                        mm = em.op("te", lambda E, dc=dc, h=h, t0c=t0c, w=w, pb=pb,
                                   src_off=src_off: E.matmul(
                            pb[:, 0:w],
                            wful[:, dc * D + h * P:dc * D + (h + 1) * P],
                            xln1[:, dc * T0 + src_off + t0c:dc * T0 + src_off + t0c + w],
                            start=(dc == 0), stop=(dc == 15)),
                            waits=wv, inc=("pT" if dc == 15 else None))
                    sl = i % 2
                    ob = qkout[:, sl * 512:(sl + 1) * 512]
                    wv = [mm] + ([out_free[sl]] if out_free[sl] else [])
                    rt = rope_tile(pb, tabs, cc0, ss0, t0c, w, ob, wv)
                    psum_free[id(pb)] = rt
                    out_free[sl] = em.dma(
                        "sc", dst[h * P:(h + 1) * P, t0c:t0c + w], ob[:, 0:w], waits=[rt])
                    i += 1
        load_wfull(wful, wva, war=[em.tok("pT")])
        em.wait("te", em.tok("dG"))
        vfree = [None, None]
        for rt_i in range(T0 // P):
            for dv in range(4):
                k = rt_i * 4 + dv
                pb = pA if k % 2 == 0 else pB
                mm = None
                for dc in range(16):
                    wv = ([psum_free[id(pb)]] if (psum_free[id(pb)] and dc == 0) else [])
                    mm = em.op("te", lambda E, dc=dc, rt_i=rt_i, dv=dv, pb=pb: E.matmul(
                        pb[:],
                        xln1[:, dc * T0 + rt_i * P:dc * T0 + (rt_i + 1) * P],
                        wful[:, dc * D + dv * 512:dc * D + (dv + 1) * 512],
                        start=(dc == 0), stop=(dc == 15)),
                        waits=wv, inc=("pT" if dc == 15 else None))
                sl = k % 2
                ob = qkout[:, sl * 512:(sl + 1) * 512]
                cp = em.op("sc", lambda E, pb=pb, ob=ob: E.copy(ob, pb[:]),
                           waits=[mm] + ([vfree[sl]] if vfree[sl] else []), inc="pS")
                psum_free[id(pb)] = cp
                vfree[sl] = em.dma(
                    "sc", v1d[rt_i * P:(rt_i + 1) * P, dv * 512:(dv + 1) * 512],
                    ob, waits=[cp])
        em.barrier()
        s0.close()

        sWO = ExitStack()
        wofl = sWO.enter_context(nc.sbuf_tensor("wofl", [P, 16 * D], BF16))
        tw_wo = load_wfull(wofl, woa)

        castk = {}
        shared_cast_chunks = [
            ("sh", ws13c[0:16], ws13[0:16]),
            ("sh", ws13c[16:32], ws13[16:32]),
            ("sh2", ws2c[0:HS // 2, :], ws2[0:HS // 2, :]),
            ("sh2", ws2c[HS // 2:, :], ws2[HS // 2:, :]),
        ]

        s2 = ExitStack()
        kwin = s2.enter_context(nc.sbuf_tensor("kwin", [P, 16 * 1024], BF16))
        vwin = s2.enter_context(nc.sbuf_tensor("vwin", [P, 8 * D], BF16))
        ptb = s2.enter_context(nc.sbuf_tensor("ptb", [P, 8 * 512], BF16))
        qh1 = s2.enter_context(nc.sbuf_tensor("qh1", [P, 512], BF16))
        mk1 = s2.enter_context(nc.sbuf_tensor("mk1", [P, 4 * 1024], F32))
        ssb = s2.enter_context(nc.sbuf_tensor("ssb", [P, 1024], F32))
        psb = s2.enter_context(nc.sbuf_tensor("psb", [P, 1024], F32))

        blocks = [
            (0, 512, 0, 1024),
            (512, 512, 512, 1024),
            (1024, 512, 1024, 1024),
            (1536, 256, 2048, 256),
        ]
        aofree = [None, None]
        kv_use = None
        m_use = None
        pC_free = None
        for bi, (qc0, qw, kc0r, kw) in enumerate(blocks):
            nkc = kw // P
            tk = None
            for g in range(2):
                src = k1d.rearrange("(c p) t -> p c t", p=P)[:, 8 * g:8 * g + 8, kc0r:kc0r + kw]
                dvw = kwin[:].rearrange("p (c t) -> p c t", t=1024)[:, 8 * g:8 * g + 8, 0:kw]
                tk = em.dma("sy", dvw, src,
                            waits=([kv_use] if (kv_use and g == 0) else []))
            src = v1d[kc0r:kc0r + kw, :].rearrange("(r p) d -> p r d", p=P)
            dvw = vwin[:].rearrange("p (r d) -> p r d", d=D)[:, 0:nkc, :]
            tv = em.dma("sy", dvw, src)
            if bi < 3:
                msrc = m1m[qc0:qc0 + qw, :].rearrange("(t p) k -> p t k", p=P)
                mdst = mk1[:].rearrange("p (t k) -> p t k", k=1024)[:, 0:qw // P, :]
            else:
                msrc = m1f.rearrange("(t p) k -> p t k", p=P)
                mdst = mk1[:].rearrange("p (t k) -> p t k", k=1024)[:, 0:2, 0:FRONT]
            tm = em.dma("sy", mdst, msrc, waits=([m_use] if m_use else []))
            kv_use = None
            m_use = None
            for h in range(NH):
                tq = em.dma("sy", qh1[:, 0:qw], q1d[h * P:(h + 1) * P, qc0:qc0 + qw],
                            waits=[em.tok("pT")])
                pt_ev = None
                texp_prev = None
                tpn_prev = None
                sbank_free = None
                for qt in range(qw // P):
                    smm = []
                    for kk in range((kw + 511) // 512):
                        wkk = min(512, kw - kk * 512)
                        pb = pA if kk == 0 else pB
                        wv = []
                        if kk == 0:
                            wv = [tk, tq] + ([sbank_free] if sbank_free else [])
                        smm.append(em.op("te", lambda E, qt=qt, kk=kk, wkk=wkk, pb=pb,
                                         h=h: E.matmul(
                            pb[:, 0:wkk], qh1[:, qt * P:(qt + 1) * P],
                            kwin[:, h * 1024 + kk * 512:h * 1024 + kk * 512 + wkk],
                            start=True, stop=True),
                            waits=wv, inc="pT"))
                    nks = (kw + 511) // 512
                    ta = None
                    for kk in range(nks):
                        wkk = min(512, kw - kk * 512)
                        pb = pA if kk == 0 else pB
                        wv = [smm[kk], tm] + ([texp_prev] if (texp_prev and kk == 0) else [])
                        ta = em.op("ve", lambda E, qt=qt, kk=kk, wkk=wkk, pb=pb: E.tensor_tensor(
                            ssb[:, kk * 512:kk * 512 + wkk], pb[:, 0:wkk],
                            mk1[:, qt * 1024 + kk * 512:qt * 1024 + kk * 512 + wkk],
                            ALU.add), waits=wv,
                            inc=("pV" if kk == nks - 1 else None))
                    sbank_free = ta
                    em.op("ve", lambda E, kw=kw: E.reduce_max(
                        stats[:, 32:33], ssb[:, 0:kw], axis=AX))
                    tneg = em.op("ve", lambda E: E.tensor_scalar_mul(
                        stats[:, 33:34], stats[:, 32:33], -1.0), inc="pV")
                    texp = em.op("sc", lambda E, kw=kw: E.activation(
                        psb[:, 0:kw], ssb[:, 0:kw], AF.Exp, bias=stats[:, 33:34]),
                        waits=[tneg] + ([tpn_prev] if tpn_prev else []), inc="pS")
                    texp_prev = texp
                    em.op("ve", lambda E, kw=kw: E.reduce_sum(
                        stats[:, 34:35], psb[:, 0:kw], axis=AX), waits=[texp])
                    trc = em.op("ve", lambda E: E.reciprocal(
                        stats[:, 35:36], stats[:, 34:35]), inc="pV")
                    tpn = em.op("ve", lambda E, kw=kw: E.tensor_scalar(
                        scr16[:, 0:kw], psb[:, 0:kw], stats[:, 35:36], None, ALU.mult),
                        waits=[trc, em.tok("pT")], inc="pV")
                    tpn_prev = tpn
                    for r0 in range(0, nkc, 8):
                        n = min(8, nkc - r0)
                        last = None
                        for icx in range(n):
                            kcix = r0 + icx
                            wv = [tpn] + ([pt_ev] if (pt_ev and icx == 0) else [])
                            last = em.op("te", lambda E, kcix=kcix, icx=icx: E.transpose(
                                pTb[:, icx * P:(icx + 1) * P],
                                scr16[:, kcix * P:(kcix + 1) * P], identb[:]),
                                waits=wv, inc=("pT" if icx == n - 1 else None))
                        dview = ptb[:].rearrange("p (c t) -> p c t", t=512)[
                            :, r0:r0 + n, qt * P:(qt + 1) * P]
                        sview = pTb[:, 0:n * P].rearrange("p (c t) -> p c t", t=P)
                        pt_ev = em.op("sc", lambda E, d=dview, s=sview: E.copy(d, s),
                                      waits=[last], inc="pS")
                mm = None
                for kcix in range(nkc):
                    wv = []
                    if kcix == 0:
                        wv = [pt_ev, tv] + ([pC_free] if pC_free else [])
                    mm = em.op("te", lambda E, kcix=kcix, h=h, qw=qw, nkc=nkc: E.matmul(
                        pC[:, 0:qw],
                        vwin[:, kcix * D + h * P:kcix * D + (h + 1) * P],
                        ptb[:, kcix * 512:kcix * 512 + qw],
                        start=(kcix == 0), stop=(kcix == nkc - 1)),
                        waits=wv, inc=("pT" if kcix == nkc - 1 else None))
                sl = h % 2
                ob = scr16[:, 1024 + sl * 512:1024 + (sl + 1) * 512]
                cp = em.op("sc", lambda E, ob=ob, qw=qw: E.copy(
                    ob[:, 0:qw], pC[:, 0:qw]),
                    waits=[mm] + ([aofree[sl]] if aofree[sl] else []), inc="pS")
                pC_free = cp
                aofree[sl] = em.dma("sc", aod[h * P:(h + 1) * P, qc0:qc0 + qw],
                                    ob[:, 0:qw], waits=[cp])
            kv_use = em.tok("pT")
            m_use = em.tok("pV")
            if bi < len(shared_cast_chunks):
                key, cdst, csrc = shared_cast_chunks[bi]
                castk[key] = em.dma("gp", cdst, csrc, waits=[em.tok("pT")])
        em.barrier()
        s2.close()

        def wo_residual(wo_ap, ao_dram, ao_sb, res_dram, res_row0, out_dram, rows,
                        wofull, aot, x0t, ysb, x1t, preload_tok=None):
            tw = preload_tok if preload_tok is not None else load_wfull(wofull, wo_ap)
            aofree = None
            x0free = [None] * 2
            pacc_free = {id(pE): None, id(pF): None}
            ysb_free = None
            pTf_free2 = None
            for (t0c, w) in ttiles(rows, 256):
                nsub = w // P
                ta = None
                if ao_dram is not None:
                    src = ao_dram.rearrange("(c p) t -> p c t", p=P)[:, :, t0c:t0c + w]
                    dvw = aot[:].rearrange("p (c t) -> p c t", t=512)[:, :, 0:w]
                    ta = em.dma("sy", dvw, src, waits=([aofree] if aofree else []))
                tx0 = []
                for sub in range(nsub):
                    tx0.append(em.dma(
                        "sy", x0t[:, sub * D:(sub + 1) * D],
                        res_dram[res_row0 + t0c + sub * P:res_row0 + t0c + (sub + 1) * P, :],
                        waits=([x0free[sub]] if x0free[sub] else [])))
                tt_last = None
                for oc in range(16):
                    pb = pE if oc % 2 == 0 else pF
                    mm = None
                    for ic in range(16):
                        wv = []
                        if ic == 0:
                            wv = [tw] + ([ta] if ta else [])
                            if pacc_free[id(pb)]:
                                wv.append(pacc_free[id(pb)])
                        if ao_dram is not None:
                            rhs = aot[:, ic * 512:ic * 512 + w]
                        else:
                            rhs = ao_sb[:, ic * 1024 + t0c:ic * 1024 + t0c + w]
                        mm = em.op("te", lambda E, ic=ic, oc=oc, pb=pb, rhs=rhs, w=w: E.matmul(
                            pb[:, 0:w],
                            wofull[:, ic * D + oc * P:ic * D + (oc + 1) * P],
                            rhs, start=(ic == 0), stop=(ic == 15)),
                            waits=wv, inc=("pT" if ic == 15 else None))
                    cp = em.op("sc", lambda E, pb=pb, w=w: E.copy(
                        ysb[:, 0:w], pb[:, 0:w]),
                        waits=[mm] + ([ysb_free] if ysb_free else []), inc="pS")
                    pacc_free[id(pb)] = cp
                    tlast_tr = None
                    for sub in range(nsub):
                        wv = [cp] + ([pTf_free2] if (pTf_free2 and sub == 0) else [])
                        tlast_tr = em.op("te", lambda E, sub=sub: E.transpose(
                            pTf[:, sub * P:(sub + 1) * P],
                            ysb[:, sub * P:(sub + 1) * P], identf[:]),
                            waits=wv, inc=("pT" if sub == nsub - 1 else None))
                    ysb_free = tlast_tr
                    for sub in range(nsub):
                        wv = [tlast_tr, tx0[sub]]
                        tt_last = em.op("ve", lambda E, sub=sub, oc=oc: E.tensor_tensor(
                            x1t[:, sub * D + oc * P:sub * D + (oc + 1) * P],
                            pTf[:, sub * P:(sub + 1) * P],
                            x0t[:, sub * D + oc * P:sub * D + (oc + 1) * P],
                            ALU.add), waits=wv,
                            inc=("pV" if sub == nsub - 1 else None))
                    pTf_free2 = tt_last
                for sub in range(nsub):
                    st = em.dma("sc", out_dram[t0c + sub * P:t0c + (sub + 1) * P, :],
                                x1t[:, sub * D:(sub + 1) * D], waits=[tt_last])
                    x0free[sub] = st
                if ao_dram is not None:
                    aofree = em.tok("pT")

        s2b = ExitStack()
        aot1 = s2b.enter_context(nc.sbuf_tensor("aot1", [P, 16 * 512], BF16))
        x0t1 = s2b.enter_context(nc.sbuf_tensor("x0t1", [P, 2 * D], F32))
        ysb1 = s2b.enter_context(nc.sbuf_tensor("ysb1", [P, 512], F32))
        x1t1 = s2b.enter_context(nc.sbuf_tensor("x1t1", [P, 2 * D], F32))
        wo_residual(woa, aod, None, x0, 512, X1d, T1, wofl, aot1, x0t1, ysb1, x1t1,
                    preload_tok=tw_wo)
        em.barrier()
        s2b.close()
        sWO.close()

        def emit_ffn(xin_dram, xout_dram, halves, tag, cw_slot0=0, do_casts=False):
            sf = ExitStack()
            xln = sf.enter_context(nc.sbuf_tensor("f_xln" + tag, [P, 16 * 1024], BF16))
            X2H = sf.enter_context(nc.sbuf_tensor("f_X2H" + tag, [P, 16 * 1024], F32))
            g_sb = sf.enter_context(nc.sbuf_tensor("f_g" + tag, [P, 8 * 1024], BF16))
            w13r = sf.enter_context(nc.sbuf_tensor("f_w13" + tag, [P, 2 * 4096], BF16))
            we2r = sf.enter_context(nc.sbuf_tensor("f_we2" + tag, [P, 8 * D], BF16))
            cwbc = sf.enter_context(nc.sbuf_tensor("f_cwbc" + tag, [P, 1024], BF16))
            xinb = sf.enter_context(nc.sbuf_tensor("f_xin" + tag, [P, D], F32))
            x32f = sf.enter_context(nc.sbuf_tensor("f_x32f" + tag, [P, 16 * P], F32))
            t1sb = sf.enter_context(nc.sbuf_tensor("f_t1" + tag, [P, 512], F32))
            tmsb = sf.enter_context(nc.sbuf_tensor("f_tm" + tag, [P, 512], F32))
            xout = xinb

            for (h0, H) in halves:
                tiles = ttiles(H)
                xin_free = []
                scr16_war = []
                scr32_war = []
                rtr_war = []
                pTb_free = None
                pTf_free = None
                for t in range(H // P):
                    sl = t % 2
                    xv = xinb[:, 0:D]
                    tin = em.dma("sy", xv, xin_dram[h0 + t * P:h0 + (t + 1) * P, :],
                                 waits=xin_free)
                    toks = ln_tile(xv, scr32[:, 0:D], sl, scr16[:, 0:D], tin,
                                   war=scr16_war, xln32_ap=scr32[:, 0:D],
                                   sq_war=scr32_war, war32=rtr_war)
                    ev0, pTf_free = transpose16(xinb, 0, X2H, 1024, t * P,
                                                tin, pTf, pTf_free, False)
                    xin_free = [toks[0], em.tok("pT")]
                    ev2, pTb_free = transpose16(scr16, 0, xln, 1024, t * P,
                                                toks[0], pTb, pTb_free, True)
                    scr16_war = [em.tok("pT")]
                    ev3, pTf_free = transpose16(scr32, 0, x32f, P, 0,
                                                toks[1], pTf, pTf_free, False)
                    scr32_war = [em.tok("pT")]
                    mm = None
                    for dc in range(16):
                        mm = em.op("te", lambda E, dc=dc: E.matmul(
                            pD[:, 0:16], x32f[:, dc * P:(dc + 1) * P],
                            wrsb[:, dc * 16:(dc + 1) * 16],
                            start=(dc == 0), stop=(dc == 15)),
                            waits=([ev3] if dc == 0 else []),
                            inc=("pT" if dc == 15 else None))
                    em.op("ve", lambda E: E.reduce_max(stats[:, 40:41], pD[:, 0:16], axis=AX),
                          waits=[mm] + rtr_war)
                    tneg = em.op("ve", lambda E: E.tensor_scalar_mul(
                        stats[:, 41:42], stats[:, 40:41], -1.0), inc="pV")
                    texp = em.op("sc", lambda E: E.activation(
                        scr32[:, 0:16], pD[:, 0:16], AF.Exp, bias=stats[:, 41:42]),
                        waits=[tneg] + scr32_war, inc="pS")
                    rtr_war = [texp]
                    em.op("ve", lambda E: E.reduce_sum(stats[:, 42:43], scr32[:, 0:16], axis=AX),
                          waits=[texp])
                    em.op("ve", lambda E: E.reciprocal(stats[:, 43:44], stats[:, 42:43]))
                    em.op("ve", lambda E: E.tensor_scalar(
                        scr32[:, 0:16], scr32[:, 0:16], stats[:, 43:44], None, ALU.mult))
                    em.op("ve", lambda E: E.tensor_copy(scr32[:, 16:32], scr32[:, 0:16]))
                    em.op("ve", lambda E: E.memset(scr32[:, 32:48], 0.0))
                    for r in range(4):
                        em.op("ve", lambda E: E.reduce_max(stats[:, 44:45],
                                                           scr32[:, 16:32], axis=AX))
                        em.op("ve", lambda E: E.tensor_scalar(
                            scr32[:, 48:64], scr32[:, 16:32], stats[:, 44:45], None,
                            ALU.is_ge))
                        em.op("ve", lambda E: E.tensor_mul(
                            scr32[:, 64:80], scr32[:, 48:64], scr32[:, 0:16]))
                        em.op("ve", lambda E: E.tensor_add(
                            scr32[:, 32:48], scr32[:, 32:48], scr32[:, 64:80]))
                        em.op("ve", lambda E: E.scalar_tensor_tensor(
                            scr32[:, 16:32], scr32[:, 48:64], NEG, scr32[:, 16:32],
                            ALU.mult, ALU.add))
                    em.op("ve", lambda E: E.reduce_sum(stats[:, 45:46], scr32[:, 32:48], axis=AX))
                    em.op("ve", lambda E: E.reciprocal(stats[:, 46:47], stats[:, 45:46]))
                    tcw = em.op("ve", lambda E: E.tensor_scalar(
                        scr32[:, 32:48], scr32[:, 32:48], stats[:, 46:47], None, ALU.mult),
                        inc="pV")
                    ttr = em.op("te", lambda E: E.transpose(
                        pTf[0:16, 0:P], scr32[:, 32:48], identf[:]),
                        waits=[tcw, pTf_free], inc="pT")
                    pTf_free = em.op("sc", lambda E, t=t: E.copy(
                        cwT[0:16, t * P:(t + 1) * P], pTf[0:16, 0:P]),
                        waits=[ttr], inc="pS")
                    scr32_war = [toks[-1], tcw, em.tok("pT")]
                if _DEBUG:
                    em.dma("sc", cwdbg[cw_slot0 + halves.index((h0, H)), :, 0:H],
                           cwT[0:16, 0:H])
                em.barrier()

                units = [("sh", su) for su in range(4)] + [("ex", e) for e in range(NE)]
                w13_free = [[], []]
                we2_free = []
                uv_free = {id(pA): None, id(pB): None, id(pC): None, id(pD): None}
                acc_free = {id(pE): None, id(pF): None}
                g_war = []
                cwbc_war = []
                pTf_cw_free = None
                prev_g = None
                for uidx, (kind, ue) in enumerate(units):
                    if do_casts and h0 == halves[0][0] and 2 <= uidx <= 17:
                        ec = uidx - 2
                        castk[("ex", ec)] = em.dma("gp", w13c[ec], w13[ec],
                                                   waits=[em.tok("pT")])
                        castk[("ex2", ec)] = em.dma("gp", we2c[ec], we2[ec])
                    tcwb = None
                    if kind == "ex":
                        last_cwb = pTf_cw_free
                        for (t0c, w) in tiles:
                            m1 = em.op("te", lambda E, t0c=t0c, w=w, ue=ue: E.matmul(
                                pTf[:, 0:w], selsb[0:16, ue * P:(ue + 1) * P],
                                cwT[0:16, t0c:t0c + w], start=True, stop=True),
                                waits=([last_cwb] if last_cwb else []),
                                inc="pT")
                            tcwb = em.op("sc", lambda E, t0c=t0c, w=w: E.copy(
                                cwbc[:, t0c:t0c + w], pTf[:, 0:w]),
                                waits=[m1] + (cwbc_war if t0c == 0 else []), inc="pS")
                            last_cwb = tcwb
                    glast = None
                    for hc in range(8):
                        sl = hc % 2
                        if kind == "ex":
                            wsrc = w13c[ue, hc].rearrange("u c p h -> p u c h")
                            ctok = castk[("ex", ue)]
                        else:
                            wsrc = ws13c[ue * 8 + hc].rearrange("u c p h -> p u c h")
                            ctok = castk["sh"]
                        wdst = w13r[:, sl * 4096:(sl + 1) * 4096].rearrange(
                            "p (u c h) -> p u c h", u=2, h=P)
                        tw = em.dma("sy" if hc % 2 == 0 else "sc", wdst, wsrc,
                                    waits=list(w13_free[sl]) + [ctok])
                        for ti, (t0c, w) in enumerate(tiles):
                            pu = pA if ti % 2 == 0 else pC
                            pv = pB if ti % 2 == 0 else pD
                            mmu = mmv = None
                            for dc in range(16):
                                wv = []
                                if dc == 0:
                                    wv = [tw] + ([uv_free[id(pu)]] if uv_free[id(pu)] else [])
                                mmu = em.op("te", lambda E, dc=dc, w=w, t0c=t0c, pu=pu,
                                            sl=sl: E.matmul(
                                    pu[:, 0:w],
                                    w13r[:, sl * 4096 + dc * P:sl * 4096 + (dc + 1) * P],
                                    xln[:, dc * 1024 + t0c:dc * 1024 + t0c + w],
                                    start=(dc == 0), stop=(dc == 15)),
                                    waits=wv, inc=("pT" if dc == 15 else None))
                            for dc in range(16):
                                wv = ([uv_free[id(pv)]] if (uv_free[id(pv)] and dc == 0) else [])
                                mmv = em.op("te", lambda E, dc=dc, w=w, t0c=t0c, pv=pv,
                                            sl=sl: E.matmul(
                                    pv[:, 0:w],
                                    w13r[:, sl * 4096 + 2048 + dc * P:sl * 4096 + 2048 + (dc + 1) * P],
                                    xln[:, dc * 1024 + t0c:dc * 1024 + t0c + w],
                                    start=(dc == 0), stop=(dc == 15)),
                                    waits=wv, inc=("pT" if dc == 15 else None))
                            tsl = em.op("sc", lambda E, pu=pu, w=w: E.activation(
                                t1sb[:, 0:w], pu[:, 0:w], AF.Silu),
                                waits=[mmu] + ([prev_g] if prev_g else []) + g_war,
                                inc="pS")
                            uv_free[id(pu)] = tsl
                            if kind == "ex":
                                ttm = em.op("ve", lambda E, pv=pv, w=w, t0c=t0c: E.tensor_tensor(
                                    tmsb[:, 0:w], pv[:, 0:w], cwbc[:, t0c:t0c + w],
                                    ALU.mult),
                                    waits=[mmv, tcwb] + ([prev_g] if prev_g else []),
                                    inc="pV")
                                uv_free[id(pv)] = ttm
                                glast = em.op("gp", lambda E, hc=hc, t0c=t0c, w=w: E.tensor_tensor(
                                    g_sb[:, hc * 1024 + t0c:hc * 1024 + t0c + w],
                                    t1sb[:, 0:w], tmsb[:, 0:w], ALU.mult),
                                    waits=[tsl, ttm], inc="pP")
                            else:
                                glast = em.op("ve", lambda E, hc=hc, t0c=t0c, w=w,
                                              pv=pv: E.tensor_tensor(
                                    g_sb[:, hc * 1024 + t0c:hc * 1024 + t0c + w],
                                    t1sb[:, 0:w], pv[:, 0:w], ALU.mult),
                                    waits=[tsl, mmv], inc="pV")
                                uv_free[id(pv)] = glast
                            prev_g = glast
                        w13_free[sl] = [em.tok("pT")]
                    g_war = []
                    tw2s = []
                    for g2 in range(2):
                        if kind == "ex":
                            src = we2c[ue, g2 * 512:(g2 + 1) * 512, :].rearrange(
                                "(c p) o -> p c o", p=P)
                            ctok2 = castk[("ex2", ue)]
                        else:
                            src = ws2c[ue * 1024 + g2 * 512:ue * 1024 + (g2 + 1) * 512, :].rearrange(
                                "(c p) o -> p c o", p=P)
                            ctok2 = castk["sh2"]
                        dvw = we2r[:].rearrange("p (c o) -> p c o", o=D)[:, g2 * 4:(g2 + 1) * 4, :]
                        tw2s.append(em.dma("sc" if g2 == 0 else "sy", dvw, src,
                                           waits=list(we2_free) + [ctok2]))
                    for (t0c, w) in tiles:
                        for oc in range(16):
                            pb = pE if oc % 2 == 0 else pF
                            mm = None
                            for hc in range(8):
                                wv = []
                                if hc == 0:
                                    wv = tw2s + [glast]
                                    if acc_free[id(pb)]:
                                        wv.append(acc_free[id(pb)])
                                mm = em.op("te", lambda E, hc=hc, oc=oc, t0c=t0c, w=w,
                                           pb=pb: E.matmul(
                                    pb[:, 0:w],
                                    we2r[:, hc * D + oc * P:hc * D + (oc + 1) * P],
                                    g_sb[:, hc * 1024 + t0c:hc * 1024 + t0c + w],
                                    start=(hc == 0), stop=(hc == 7)),
                                    waits=wv, inc=("pT" if hc == 7 else None))
                            if kind == "sh":
                                ev = em.op("ve", lambda E, oc=oc, t0c=t0c, w=w,
                                           pb=pb: E.scalar_tensor_tensor(
                                    X2H[:, oc * 1024 + t0c:oc * 1024 + t0c + w],
                                    pb[:, 0:w], 0.2,
                                    X2H[:, oc * 1024 + t0c:oc * 1024 + t0c + w],
                                    ALU.mult, ALU.add), waits=[mm], inc="pV")
                            else:
                                ev = em.op("ve", lambda E, oc=oc, t0c=t0c, w=w,
                                           pb=pb: E.tensor_tensor(
                                    X2H[:, oc * 1024 + t0c:oc * 1024 + t0c + w],
                                    X2H[:, oc * 1024 + t0c:oc * 1024 + t0c + w],
                                    pb[:, 0:w], ALU.add), waits=[mm], inc="pV")
                            acc_free[id(pb)] = ev
                    g_war = [em.tok("pT")]
                    we2_free = [em.tok("pT")]
                    cwbc_war = [em.tok("pV")]
                    pTf_cw_free = em.tok("pS")
                em.barrier()
                pTf_free = None
                xo_free = []
                for t in range(H // P):
                    last_cp = None
                    for r0 in range(0, 16, 4):
                        last = None
                        for i2 in range(4):
                            oc = r0 + i2
                            wv = []
                            if i2 == 0:
                                wv = ([pTf_free] if pTf_free else []) + \
                                     (xo_free if r0 == 0 else [])
                            last = em.op("te", lambda E, oc=oc, i2=i2, t=t: E.transpose(
                                pTf[:, i2 * P:(i2 + 1) * P],
                                X2H[:, oc * 1024 + t * P:oc * 1024 + (t + 1) * P],
                                identf[:]), waits=wv, inc=("pT" if i2 == 3 else None))
                        last_cp = em.op("sc", lambda E, r0=r0: E.copy(
                            xout[:, r0 * P:(r0 + 4) * P],
                            pTf[:, 0:512]), waits=[last], inc="pS")
                        pTf_free = last_cp
                    st = em.dma("sc", xout_dram[h0 + t * P:h0 + (t + 1) * P, :],
                                xout[:, 0:D], waits=[last_cp])
                    xo_free = [st]
                em.barrier()
            sf.close()

        emit_ffn(X1d, X2d, [(0, 1024), (1024, 768)], "2", cw_slot0=0, do_casts=True)

        s4keep = ExitStack()
        s4 = ExitStack()
        s4p = ExitStack()
        xln3 = s4keep.enter_context(nc.sbuf_tensor("xln3", [P, 16 * T1], BF16))
        k3 = s4.enter_context(nc.sbuf_tensor("k3", [P, 4 * T1], BF16))
        q3 = s4.enter_context(nc.sbuf_tensor("q3", [P, 16 * 1024], BF16))
        v3 = s4.enter_context(nc.sbuf_tensor("v3", [P, 14 * 512], BF16))
        tab3 = s4p.enter_context(nc.sbuf_tensor("tab3", [P, 2 * TOWN + 2 * T1], BF16))
        xinb3 = s4p.enter_context(nc.sbuf_tensor("xinb3", [P, 2 * D], F32))
        wsl3 = s4p.enter_context(nc.sbuf_tensor("wsl3", [P, 2 * D], BF16))
        wv3sb = s4p.enter_context(nc.sbuf_tensor("wv3sb", [P, 16 * 512], BF16))
        rsc3 = s4p.enter_context(nc.sbuf_tensor("rsc3", [P, 2 * 512], F32))

        C3Q, S3Q, C3K, S3K = 0, TOWN, 2 * TOWN, 2 * TOWN + T1
        em.dma("gp", tab3[:, C3Q:C3Q + TOWN], qc3)
        em.dma("gp", tab3[:, S3Q:S3Q + TOWN], qs3)
        em.dma("gp", tab3[:, C3K:C3K + T1], kc3)
        em.dma("gp", tab3[:, S3K:S3K + T1], ks3)

        xin_free = [[], []]
        scr16_war = []
        sq_war = []
        pTb_free = None
        for t in range(T1 // P):
            sl = t % 2
            xv = xinb3[:, sl * D:(sl + 1) * D]
            tin = em.dma("sy", xv, X2d[t * P:(t + 1) * P, :], waits=xin_free[sl])
            toks = ln_tile(xv, scr32[:, 0:D], sl, scr16[:, 0:D], tin, war=scr16_war,
                           sq_war=sq_war)
            ev, pTb_free = transpose16(scr16, 0, xln3, T1, t * P,
                                       toks[0], pTb, pTb_free, True)
            xin_free[sl] = [toks[0]]
            scr16_war = [em.tok("pT")]
            sq_war = [toks[-1]]
        em.barrier()

        def rope3(ps, cc0, ss0, tcol, w, out_ap, waits):
            t1 = rsc3[:, 0:512]
            t2 = rsc3[:, 512:1024]
            em.op("ve", lambda E: E.tensor_mul(
                t1[:, 0:w], ps[:, 0:w], tab3[:, cc0 + tcol:cc0 + tcol + w]),
                waits=waits)
            em.op("ve", lambda E: E.tensor_mul(
                t2[0:64, 0:w], ps[64:128, 0:w], tab3[64:128, ss0 + tcol:ss0 + tcol + w]))
            em.op("ve", lambda E: E.tensor_mul(
                t2[64:128, 0:w], ps[0:64, 0:w], tab3[0:64, ss0 + tcol:ss0 + tcol + w]))
            em.op("ve", lambda E: E.tensor_tensor(
                out_ap[0:64, 0:w], t1[0:64, 0:w], t2[0:64, 0:w], ALU.subtract))
            return em.op("ve", lambda E: E.tensor_tensor(
                out_ap[64:128, 0:w], t1[64:128, 0:w], t2[64:128, 0:w], ALU.add),
                inc="pV")

        wfree = [[], []]
        pfree = {id(pA): None, id(pB): None}
        for h in range(NH):
            sl = h % 2
            tw = em.dma("gp", wsl3[:, sl * D:(sl + 1) * D].rearrange(
                "p (c h) -> p c h", h=P), wq3[h].rearrange("c p h -> p c h"),
                waits=wfree[sl])
            for (t0c, w) in ttiles(TOWN):
                pb = pA if (t0c // 512) % 2 == 0 else pB
                mm = None
                for dc in range(16):
                    wv = []
                    if dc == 0:
                        wv = [tw] + ([pfree[id(pb)]] if pfree[id(pb)] else [])
                    mm = em.op("te", lambda E, dc=dc, sl=sl, t0c=t0c, w=w, pb=pb: E.matmul(
                        pb[:, 0:w],
                        wsl3[:, sl * D + dc * P:sl * D + (dc + 1) * P],
                        xln3[:, dc * T1 + 512 + t0c:dc * T1 + 512 + t0c + w],
                        start=(dc == 0), stop=(dc == 15)),
                        waits=wv, inc=("pT" if dc == 15 else None))
                rt = rope3(pb, C3Q, S3Q, t0c, w,
                           q3[:, h * 1024 + t0c:h * 1024 + t0c + w], [mm])
                pfree[id(pb)] = rt
            wfree[sl] = [em.tok("pT")]
        for kv in range(NKV):
            sl = kv % 2
            tw = em.dma("gp", wsl3[:, sl * D:(sl + 1) * D].rearrange(
                "p (c h) -> p c h", h=P), wk3[kv].rearrange("c p h -> p c h"),
                waits=wfree[sl])
            for (t0c, w) in ttiles(T1):
                pb = pA if (t0c // 512) % 2 == 0 else pB
                mm = None
                for dc in range(16):
                    wv = []
                    if dc == 0:
                        wv = [tw] + ([pfree[id(pb)]] if pfree[id(pb)] else [])
                    mm = em.op("te", lambda E, dc=dc, sl=sl, t0c=t0c, w=w, pb=pb: E.matmul(
                        pb[:, 0:w],
                        wsl3[:, sl * D + dc * P:sl * D + (dc + 1) * P],
                        xln3[:, dc * T1 + t0c:dc * T1 + t0c + w],
                        start=(dc == 0), stop=(dc == 15)),
                        waits=wv, inc=("pT" if dc == 15 else None))
                rt = rope3(pb, C3K, S3K, t0c, w,
                           k3[:, kv * T1 + t0c:kv * T1 + t0c + w], [mm])
                pfree[id(pb)] = rt
            wfree[sl] = [em.tok("pT")]
        tv3 = None
        for g2 in range(4):
            src = wv3.rearrange("(c p) v -> p c v", p=P)[:, 4 * g2:4 * g2 + 4, :]
            dvw = wv3sb[:].rearrange("p (c v) -> p c v", v=512)[:, 4 * g2:4 * g2 + 4, :]
            tv3 = em.dma("gp", dvw, src)
        v3done = None
        for rt_i in range(T1 // P):
            pb = pA if rt_i % 2 == 0 else pB
            mm = None
            for dc in range(16):
                wv = []
                if dc == 0:
                    wv = [tv3] + ([pfree[id(pb)]] if pfree[id(pb)] else [])
                mm = em.op("te", lambda E, dc=dc, rt_i=rt_i, pb=pb: E.matmul(
                    pb[:],
                    xln3[:, dc * T1 + rt_i * P:dc * T1 + (rt_i + 1) * P],
                    wv3sb[:, dc * 512:(dc + 1) * 512],
                    start=(dc == 0), stop=(dc == 15)),
                    waits=wv, inc=("pT" if dc == 15 else None))
            cp = em.op("sc", lambda E, rt_i=rt_i, pb=pb: E.copy(
                v3[:, rt_i * 512:(rt_i + 1) * 512], pb[:]), waits=[mm], inc="pS")
            pfree[id(pb)] = cp
            v3done = cp
        em.barrier()
        s4p.close()

        s4c = ExitStack()
        m3sb = s4c.enter_context(nc.sbuf_tensor("m3sb", [P, 4 * T1], F32))
        ssb3 = s4c.enter_context(nc.sbuf_tensor("ssb3", [P, T1], F32))
        psb3 = s4c.enter_context(nc.sbuf_tensor("psb3", [P, T1], F32))
        AOC = 8 * 1024
        pt_ar = xln3
        nkc3 = T1 // P

        m_use3 = None
        pE_free = None
        for qg in range(2):
            msrc = m3.rearrange("(t p) k -> p t k", p=P)[:, qg * 4:(qg + 1) * 4, :]
            mdst = m3sb[:].rearrange("p (t k) -> p t k", k=T1)
            tm3 = em.dma("sy", mdst, msrc, waits=([m_use3] if m_use3 else []))
            for h in range(NH):
                kvh = h // 4
                pt_ev = None
                texp_prev = None
                tpn_prev = None
                sbank_free = None
                for qt in range(4):
                    qcol = qg * 512 + qt * P
                    banks = [pA, pB, pC, pD]
                    smm = []
                    for kk, (k0, kwd) in enumerate(ttiles(T1)):
                        pb = banks[kk]
                        wv = []
                        if kk == 0:
                            wv = [tm3] + ([sbank_free] if sbank_free else [])
                        smm.append(em.op("te", lambda E, k0=k0, kwd=kwd, pb=pb, h=h,
                                         qcol=qcol, kvh=kvh: E.matmul(
                            pb[:, 0:kwd],
                            q3[:, h * 1024 + qcol:h * 1024 + qcol + P],
                            k3[:, kvh * T1 + k0:kvh * T1 + k0 + kwd],
                            start=True, stop=True),
                            waits=wv, inc="pT"))
                    ta = None
                    for kk, (k0, kwd) in enumerate(ttiles(T1)):
                        pb = banks[kk]
                        wv = [smm[kk]] + ([texp_prev] if (texp_prev and kk == 0) else [])
                        ta = em.op("ve", lambda E, k0=k0, kwd=kwd, pb=pb, qt=qt: E.tensor_tensor(
                            ssb3[:, k0:k0 + kwd], pb[:, 0:kwd],
                            m3sb[:, qt * T1 + k0:qt * T1 + k0 + kwd], ALU.add),
                            waits=wv, inc=("pV" if kk == 3 else None))
                    sbank_free = ta
                    em.op("ve", lambda E: E.reduce_max(stats[:, 48:49], ssb3[:], axis=AX))
                    tneg = em.op("ve", lambda E: E.tensor_scalar_mul(
                        stats[:, 49:50], stats[:, 48:49], -1.0), inc="pV")
                    texp = em.op("sc", lambda E: E.activation(
                        psb3[:], ssb3[:], AF.Exp, bias=stats[:, 49:50]),
                        waits=[tneg] + ([tpn_prev] if tpn_prev else []), inc="pS")
                    texp_prev = texp
                    em.op("ve", lambda E: E.reduce_sum(stats[:, 50:51], psb3[:], axis=AX),
                          waits=[texp])
                    trc = em.op("ve", lambda E: E.reciprocal(stats[:, 51:52],
                                                             stats[:, 50:51]), inc="pV")
                    tpn = em.op("ve", lambda E: E.tensor_scalar(
                        scr16[:, 0:T1], psb3[:], stats[:, 51:52], None, ALU.mult),
                        waits=[trc, em.tok("pT")], inc="pV")
                    tpn_prev = tpn
                    for r0 in range(0, nkc3, 8):
                        n = min(8, nkc3 - r0)
                        last = None
                        for icx in range(n):
                            kcix = r0 + icx
                            wv = [tpn] + ([pt_ev] if (pt_ev and icx == 0) else [])
                            last = em.op("te", lambda E, kcix=kcix, icx=icx: E.transpose(
                                pTb[:, icx * P:(icx + 1) * P],
                                scr16[:, kcix * P:(kcix + 1) * P], identb[:]),
                                waits=wv, inc=("pT" if icx == n - 1 else None))
                        dview = pt_ar[:].rearrange("p (c t) -> p c t", t=512)[
                            :, r0:r0 + n, qt * P:(qt + 1) * P]
                        sview = pTb[:, 0:n * P].rearrange("p (c t) -> p c t", t=P)
                        pt_ev = em.op("sc", lambda E, d=dview, s=sview: E.copy(d, s),
                                      waits=[last], inc="pS")
                mm = None
                for kcix in range(nkc3):
                    wv = []
                    if kcix == 0:
                        wv = [pt_ev, v3done] + ([pE_free] if pE_free else [])
                    mm = em.op("te", lambda E, kcix=kcix, kvh=kvh, nkc3=nkc3: E.matmul(
                        pE[:],
                        v3[:, kcix * 512 + kvh * P:kcix * 512 + (kvh + 1) * P],
                        pt_ar[:, kcix * 512:(kcix + 1) * 512],
                        start=(kcix == 0), stop=(kcix == nkc3 - 1)),
                        waits=wv, inc=("pT" if kcix == nkc3 - 1 else None))
                pE_free = em.op("sc", lambda E, h=h, qg=qg: E.copy(
                    pt_ar[:, AOC + h * 1024 + qg * 512:AOC + h * 1024 + qg * 512 + 512],
                    pE[:]), waits=[mm], inc="pS")
            m_use3 = em.tok("pV")
        em.barrier()
        s4c.close()
        s4.close()

        s4d = ExitStack()
        wo3f = s4d.enter_context(nc.sbuf_tensor("wo3f", [P, 16 * D], BF16))
        x0t3 = s4d.enter_context(nc.sbuf_tensor("x0t3", [P, 2 * D], F32))
        ysb3 = s4d.enter_context(nc.sbuf_tensor("ysb3", [P, 512], F32))
        x1t3 = s4d.enter_context(nc.sbuf_tensor("x1t3", [P, 2 * D], F32))
        wo_residual(wo3, None, pt_ar[:, AOC:AOC + 16 * 1024], X2d, 512, X3d, TOWN,
                    wo3f, None, x0t3, ysb3, x1t3)
        em.barrier()
        s4d.close()
        s4keep.close()

        emit_ffn(X3d, y, [(0, 1024)], "4", cw_slot0=2)

        em.wait("sc", em.tok("dA"))
        em.wait("sy", em.tok("dS"))
        em.wait("gp", em.tok("dG"))

        def replay(name):
            def run(E):
                for f in em.q[name]:
                    f(E)
            return run

        with nc.Block() as blk:
            blk.sync(replay("sy"))
            blk.gpsimd(replay("gp"))
            blk.vector(replay("ve"))
            blk.scalar(replay("sc"))
            blk.tensor(replay("te"))
    return nc



def _rope_tables(pos, scale):
    inv = 1.0 / (10000.0 ** (np.arange(0, HD, 2, dtype=np.float64) / HD))
    ang = np.asarray(pos, np.float64)[None, :] * inv[:, None]
    c = np.cos(ang)
    s = np.sin(ang)
    C2 = np.concatenate([c, c], 0) * scale
    S2 = np.concatenate([s, s], 0) * scale
    return np.ascontiguousarray(C2, np.float32), np.ascontiguousarray(S2, np.float32)


def _headperm(w, nh):
    perm = np.r_[np.arange(0, 128, 2), np.arange(1, 128, 2)]
    return np.ascontiguousarray(
        w.reshape(w.shape[0], nh, 128)[:, :, perm].reshape(w.shape[0], -1))


def _prep_inputs(kw):
    f32 = np.float32
    x = np.ascontiguousarray(np.asarray(kw["L0_attn_a_x"], f32))
    Wq, Wk, Wv, Wo = (np.asarray(kw["L0_attn_a_" + n], f32)
                      for n in ("Wq", "Wk", "Wv", "Wo"))
    wq, wk, wv, wo = (np.asarray(kw["L0_attn_b_" + n], f32)
                      for n in ("wq", "wk", "wv", "wo"))
    wrt = np.asarray(kw["L0_ffn_w_router"], f32)
    we1, we3, we2 = (np.asarray(kw["L0_ffn_" + n], f32) for n in ("we1", "we3", "we2"))
    ws1, ws3, ws2 = (np.asarray(kw["L0_ffn_" + n], f32) for n in ("ws1", "ws3", "ws2"))

    shared = {
        "wqa": _headperm(np.ascontiguousarray(Wq.T), NH),
        "wka": _headperm(np.ascontiguousarray(Wk.T), NH),
        "wva": np.ascontiguousarray(Wv.T),
        "woa": np.ascontiguousarray(Wo.T),
        "wq3": np.ascontiguousarray(
            _headperm(wq, NH).reshape(16, P, NH, HD).transpose(2, 0, 1, 3)),
        "wk3": np.ascontiguousarray(
            _headperm(wk, NKV).reshape(16, P, NKV, HD).transpose(2, 0, 1, 3)),
        "wv3": np.ascontiguousarray(wv),
        "wo3": np.ascontiguousarray(wo),
        "wr": np.ascontiguousarray(wrt.reshape(16, P, 16)),
        "w13": np.ascontiguousarray(np.stack(
            [we1.reshape(NE, 16, P, 8, P).transpose(0, 3, 1, 2, 4),
             we3.reshape(NE, 16, P, 8, P).transpose(0, 3, 1, 2, 4)], axis=2)),
        "ws13": np.ascontiguousarray(np.stack(
            [ws1.reshape(16, P, 32, P).transpose(2, 0, 1, 3),
             ws3.reshape(16, P, 32, P).transpose(2, 0, 1, 3)], axis=1)),
        "we2": np.ascontiguousarray(we2),
        "ws2": np.ascontiguousarray(ws2),
    }
    sel = np.zeros((16, D), f32)
    for e in range(16):
        sel[e, e * P:(e + 1) * P] = 0.8
    shared["sel"] = sel

    in_maps = []
    for c in range(NCORES):
        b, j = divmod(c, 4)
        q0 = TOWN * j
        pos0 = np.concatenate([np.arange(q0 - 1024, q0 + 1024), np.arange(FRONT)])
        pos1 = pos0[512:]
        posq = np.arange(q0, q0 + TOWN)
        x0c = np.zeros((T0, D), f32)
        lo = q0 - 1024
        s_lo = max(0, lo)
        x0c[(s_lo - lo):2048] = x[b, s_lo:q0 + 1024]
        x0c[2048:] = x[b, :FRONT]

        sc = 1.0 / np.sqrt(HD)
        qc1_, qs1_ = _rope_tables(np.maximum(pos1, 0), sc)
        kc1_, ks1_ = _rope_tables(np.maximum(pos0, 0), 1.0)
        qc3_, qs3_ = _rope_tables(posq, sc)
        kc3_, ks3_ = _rope_tables(np.maximum(pos1, 0), 1.0)

        m1m = np.full((1536, 1024), NEG, f32)
        for n in range(3):
            qp = pos1[n * 512:(n + 1) * 512][:, None]
            kp = (q0 - 1024 + n * 512 + np.arange(1024))[None, :]
            ok = (kp <= qp) & (kp > qp - WIN) & (kp >= 0)
            m1m[n * 512:(n + 1) * 512] = np.where(ok, 0.0, NEG)
        qf = np.arange(FRONT)[:, None]
        kf = np.arange(FRONT)[None, :]
        m1f = np.where(kf <= qf, 0.0, NEG).astype(f32)

        qp = posq[:, None]
        kp_main = (q0 - 512 + np.arange(1536))[None, :]
        band = (kp_main <= qp) & (kp_main > qp - WIN)
        fok = (kp_main < FRONT) & (kp_main <= qp - WIN)
        main_ok = (band | fok) & (kp_main >= 0)
        kp_f = np.arange(FRONT)[None, :]
        bandf = (kp_f <= qp) & (kp_f > qp - WIN)
        fokf = (kp_f < FRONT) & (kp_f <= qp - WIN)
        front_ok = (bandf | fokf) if j > 0 else np.zeros((TOWN, FRONT), bool)
        m3c = np.concatenate([np.where(main_ok, 0.0, NEG),
                              np.where(front_ok, 0.0, NEG)], axis=1).astype(f32)

        im = dict(shared)
        im.update(x0=x0c, m1m=m1m, m1f=m1f, m3=m3c,
                  qc1=qc1_, qs1=qs1_, kc1=kc1_, ks1=ks1_,
                  qc3=qc3_, qs3=qs3_, kc3=kc3_, ks3=ks3_)
        in_maps.append(im)
    return in_maps


def kernel(**kw):
    in_maps = _prep_inputs(kw)
    nc = build_program()
    res = run_bass_kernel_spmd(nc, in_maps, core_ids=list(range(NCORES)))
    out = np.empty((2, SEQ, D), np.float32)
    for c in range(NCORES):
        b, j = divmod(c, 4)
        out[b, TOWN * j:TOWN * (j + 1)] = res.results[c]["y"]
    return out


# version 06 (final)
# speedup vs baseline: 1.1686x; 1.0321x over previous
import numpy as np
from contextlib import ExitStack

import concourse.bass as bass
import concourse.mybir as mybir
from concourse.bass_utils import run_bass_kernel_spmd

F32 = mybir.dt.float32
BF16 = mybir.dt.bfloat16
AF = mybir.ActivationFunctionType
ALU = mybir.AluOpType
AX = mybir.AxisListType.X

_DEBUG = False
_RACE_DETECT = False

P = 128
NCORES = 8
SEQ = 4096
D = 2048
T0, T1, TOWN = 2304, 1792, 1024
NH, HD = 16, 128
NKV = 4
WIN, FRONT = 512, 256
NE, HE, HS = 16, 1024, 4096
NEG = -1.0e30

ENGS = ("sy", "gp", "ve", "sc", "te")
DMASEM = {"sy": "dS", "gp": "dG", "sc": "dA"}
SEMS = ("dS", "dG", "dA", "pT", "pV", "pS", "pP")


def ttiles(total, step=512):
    out, o = [], 0
    while o < total:
        w = min(step, total - o)
        out.append((o, w))
        o += w
    return out


AUTOSER = {"ve": "pV", "sc": "pS", "gp": "pP"}


class Emit:

    def __init__(self, nc, sems):
        self.nc = nc
        self.sems = sems
        self.q = {e: [] for e in ENGS}
        self.cnt = {s: 0 for s in SEMS}
        self.waited = {e: {} for e in ENGS}
        self.last = {e: None for e in ENGS}

    def wait(self, eng, tok):
        if tok is None:
            return
        s, v = tok
        if v <= 0 or self.waited[eng].get(s, 0) >= v:
            return
        self.waited[eng][s] = v
        h = self.sems[s]
        self.q[eng].append(lambda E, h=h, v=v: E.wait_ge(h, v))

    def op(self, eng, fn, waits=(), inc=None):
        for t in waits:
            self.wait(eng, t)
        if eng in AUTOSER:
            self.wait(eng, self.last[eng])
            if inc is None:
                inc = AUTOSER[eng]
        if inc is None:
            self.q[eng].append(fn)
            return None
        amt = 16 if inc in ("dS", "dG", "dA") else 1
        self.cnt[inc] += amt
        h = self.sems[inc]
        self.q[eng].append(lambda E, fn=fn, h=h, amt=amt: fn(E).then_inc(h, amt))
        tok = (inc, self.cnt[inc])
        if eng in AUTOSER and inc not in ("dS", "dG", "dA"):
            self.last[eng] = tok
        return tok

    def dma(self, eng, out, in_, waits=()):
        return self.op(eng, lambda E, o=out, i=in_: E.dma_start(out=o, in_=i),
                       waits=waits, inc=DMASEM[eng])

    def tok(self, sem):
        return (sem, self.cnt[sem])

    def snapshot(self):
        return [(s, c) for s, c in self.cnt.items() if c > 0]

    def barrier(self):
        toks = self.snapshot()
        for e in ENGS:
            for t in toks:
                self.wait(e, t)



def build_program():
    nc = bass.Bass("TRN2", target_bir_lowering=False,
                   detect_race_conditions=_RACE_DETECT)

    def din(name, shape, dt=F32):
        return nc.dram_tensor(name, list(shape), dt, kind="ExternalInput").ap()

    x0 = din("x0", (T0, D))
    m1m = din("m1m", (1536, 1024))
    m1f = din("m1f", (FRONT, FRONT))
    m3 = din("m3", (TOWN, T1))
    qc1 = din("qc1", (P, T1)); qs1 = din("qs1", (P, T1))
    kc1 = din("kc1", (P, T0)); ks1 = din("ks1", (P, T0))
    qc3 = din("qc3", (P, TOWN)); qs3 = din("qs3", (P, TOWN))
    kc3 = din("kc3", (P, T1)); ks3 = din("ks3", (P, T1))
    seli = din("sel", (16, D))
    wqa = din("wqa", (D, D)); wka = din("wka", (D, D))
    wva = din("wva", (D, D)); woa = din("woa", (D, D))
    wq3 = din("wq3", (NH, 16, P, HD))
    wk3 = din("wk3", (NKV, 16, P, HD))
    wv3 = din("wv3", (D, NKV * HD))
    wo3 = din("wo3", (D, D))
    wr = din("wr", (16, P, 16))
    w13 = din("w13", (NE, 8, 2, 16, P, P))
    ws13 = din("ws13", (32, 2, 16, P, P))
    we2 = din("we2", (NE, HE, D))
    ws2 = din("ws2", (HS, D))

    w13c = nc.dram_tensor("w13c", [NE, 8, 2, 16, P, P], BF16, kind="Internal").ap()
    ws13c = nc.dram_tensor("ws13c", [32, 2, 16, P, P], BF16, kind="Internal").ap()
    we2c = nc.dram_tensor("we2c", [NE, HE, D], BF16, kind="Internal").ap()
    ws2c = nc.dram_tensor("ws2c", [HS, D], BF16, kind="Internal").ap()

    y = nc.dram_tensor("y", [TOWN, D], F32, kind="ExternalOutput").ap()

    kd = "ExternalOutput" if _DEBUG else "Internal"
    q1d = nc.dram_tensor("q1d", [D, T1], BF16, kind=kd).ap()
    k1d = nc.dram_tensor("k1d", [D, T0], BF16, kind=kd).ap()
    v1d = nc.dram_tensor("v1d", [T0, D], BF16, kind=kd).ap()
    aod = nc.dram_tensor("aod", [D, T1], BF16, kind=kd).ap()
    X1d = nc.dram_tensor("X1d", [T1, D], F32, kind=kd).ap()
    X2d = nc.dram_tensor("X2d", [T1, D], F32, kind=kd).ap()
    X3d = nc.dram_tensor("X3d", [TOWN, D], F32, kind=kd).ap()
    cwdbg = nc.dram_tensor("cwdbg", [4, 16, T1], BF16, kind=kd).ap() if _DEBUG else None

    es = ExitStack()
    with es:
        sems = {s: es.enter_context(nc.semaphore(s)) for s in SEMS}
        em = Emit(nc, sems)

        identb = es.enter_context(nc.sbuf_tensor("identb", [P, P], BF16))
        identf = es.enter_context(nc.sbuf_tensor("identf", [P, P], F32))
        stats = es.enter_context(nc.sbuf_tensor("stats", [P, 64], F32))
        selsb = es.enter_context(nc.sbuf_tensor("selsb", [16, D], BF16))
        cwT = es.enter_context(nc.sbuf_tensor("cwT", [16, T1], BF16))
        wrsb = es.enter_context(nc.sbuf_tensor("wrsb", [P, 16 * 16], F32))
        scr16 = es.enter_context(nc.sbuf_tensor("scr16", [P, 2048], BF16))
        scr32 = es.enter_context(nc.sbuf_tensor("scr32", [P, 2048], F32))

        pA = es.enter_context(nc.psum_tensor("pA", [P, 512], F32))
        pB = es.enter_context(nc.psum_tensor("pB", [P, 512], F32))
        pC = es.enter_context(nc.psum_tensor("pC", [P, 512], F32))
        pD = es.enter_context(nc.psum_tensor("pD", [P, 512], F32))
        pE = es.enter_context(nc.psum_tensor("pE", [P, 512], F32))
        pF = es.enter_context(nc.psum_tensor("pF", [P, 512], F32))
        pTb = es.enter_context(nc.psum_tensor("pTb", [P, 1024], BF16))
        pTf = es.enter_context(nc.psum_tensor("pTf", [P, 512], F32))

        def mk_ident(t):
            em.op("gp", lambda E: E.memset(t[:], 0.0))
            return em.op("gp", lambda E: E.affine_select(
                out=t[:], in_=t[:], compare_op=ALU.not_equal,
                fill=1.0, base=0, pattern=[[-1, P]],
                channel_multiplier=1))
        mk_ident(identb)
        tok_id = mk_ident(identf)
        em.dma("gp", selsb[:], seli)
        em.dma("sy", wrsb[:].rearrange("p (c e) -> p c e", e=16),
               wr.rearrange("c p e -> p c e"))
        em.wait("te", tok_id)

        def ln_tile(xin_ap, sqbuf, slot, xln_bf_ap, xin_tok, war=(), xln32_ap=None,
                    sq_war=(), war32=()):
            c0 = slot * 8
            tsq = em.op("sc", lambda E: E.activation(sqbuf, xin_ap, AF.Square),
                        waits=[xin_tok] + list(sq_war), inc="pS")
            em.op("ve", lambda E: E.reduce_sum(stats[:, c0:c0 + 1], xin_ap, axis=AX),
                  waits=[xin_tok])
            tsqr = em.op("ve", lambda E: E.reduce_sum(stats[:, c0 + 1:c0 + 2], sqbuf,
                                                      axis=AX),
                         waits=[tsq], inc="pV")
            em.op("ve", lambda E: E.tensor_scalar_mul(stats[:, c0 + 2:c0 + 3],
                                                      stats[:, c0:c0 + 1], 1.0 / D))
            em.op("ve", lambda E: E.tensor_mul(stats[:, c0 + 3:c0 + 4],
                                               stats[:, c0 + 2:c0 + 3],
                                               stats[:, c0 + 2:c0 + 3]))
            tv = em.op("ve", lambda E: E.scalar_tensor_tensor(
                stats[:, c0 + 4:c0 + 5], stats[:, c0 + 1:c0 + 2], 1.0 / D,
                stats[:, c0 + 3:c0 + 4], ALU.mult, ALU.subtract), inc="pV")
            tve = em.op("ve", lambda E: E.tensor_scalar_add(
                stats[:, c0 + 4:c0 + 5], stats[:, c0 + 4:c0 + 5], 1e-5), inc="pV")
            tsd = em.op("sc", lambda E: E.activation(stats[:, c0 + 5:c0 + 6],
                                                     stats[:, c0 + 4:c0 + 5],
                                                     AF.Sqrt),
                        waits=[tve], inc="pS")
            tr = em.op("ve", lambda E: E.reciprocal(stats[:, c0 + 6:c0 + 7],
                                                    stats[:, c0 + 5:c0 + 6]),
                       waits=[tsd], inc="pV")
            toks = []
            t1 = em.op("ve", lambda E: E.tensor_scalar(
                xln_bf_ap, xin_ap, stats[:, c0 + 2:c0 + 3], stats[:, c0 + 6:c0 + 7],
                ALU.subtract, ALU.mult), waits=[tr] + list(war), inc="pV")
            toks.append(t1)
            if xln32_ap is not None:
                t2 = em.op("ve", lambda E: E.tensor_scalar(
                    xln32_ap, xin_ap, stats[:, c0 + 2:c0 + 3], stats[:, c0 + 6:c0 + 7],
                    ALU.subtract, ALU.mult), waits=list(war32), inc="pV")
                toks.append(t2)
            toks.append(tsqr)
            return toks

        def transpose16(src_sb, src_col0, dst_t, dst_stride, dst_col,
                        src_tok, psum, psum_free, use_bf, width=2048):
            ident = identb[:] if use_bf else identf[:]
            per = 8 if use_bf else 4
            nch = width // P
            ev_tok = None
            for r0 in range(0, nch, per):
                n = min(per, nch - r0)
                last = None
                for i in range(n):
                    c = r0 + i
                    w = [src_tok, psum_free] if i == 0 else []
                    last = em.op("te", lambda E, c=c, i=i, ident=ident: E.transpose(
                        psum[:, i * P:(i + 1) * P],
                        src_sb[:, src_col0 + c * P:src_col0 + (c + 1) * P], ident),
                        waits=w, inc=("pT" if i == n - 1 else None))
                dview = dst_t[:].rearrange("p (c t) -> p c t", t=dst_stride)[
                    :, r0:r0 + n, dst_col:dst_col + P]
                sview = psum[:, 0:n * P].rearrange("p (c t) -> p c t", t=P)
                ev_tok = em.op("sc", lambda E, d=dview, s=sview: E.copy(d, s),
                               waits=[last], inc="pS")
                psum_free = ev_tok
            return ev_tok, psum_free

        s0 = ExitStack()
        xln1 = s0.enter_context(nc.sbuf_tensor("xln1", [P, 16 * T0], BF16))
        wful = s0.enter_context(nc.sbuf_tensor("wful", [P, 16 * D], BF16))
        tabs = s0.enter_context(nc.sbuf_tensor("tabs", [P, 2 * T0 + 2 * T1], BF16))
        xin2 = s0.enter_context(nc.sbuf_tensor("xin2", [P, 2 * D], F32))
        rsc = s0.enter_context(nc.sbuf_tensor("rsc", [P, 2 * 512], F32))
        qkout = s0.enter_context(nc.sbuf_tensor("qkout", [P, 2 * 512], BF16))

        KC0, KS0, QC0, QS0 = 0, T0, 2 * T0, 2 * T0 + T1
        em.dma("gp", tabs[:, KC0:KC0 + T0], kc1)
        em.dma("gp", tabs[:, KS0:KS0 + T0], ks1)
        em.dma("gp", tabs[:, QC0:QC0 + T1], qc1)
        em.dma("gp", tabs[:, QS0:QS0 + T1], qs1)

        def load_wfull(dst, wap, war=()):
            tk = None
            for g in range(4):
                src = wap.rearrange("(c p) o -> p c o", p=P)[:, 4 * g:4 * g + 4, :]
                dvw = dst[:].rearrange("p (c o) -> p c o", o=D)[:, 4 * g:4 * g + 4, :]
                tk = em.dma("gp", dvw, src, waits=(list(war) if g == 0 else []))
            return tk
        twq = load_wfull(wful, wqa)

        xin_free = [[], []]
        scr16_war = []
        sq_war = []
        pTb_free = None
        for t in range(T0 // P):
            sl = t % 2
            xv = xin2[:, sl * D:(sl + 1) * D]
            tin = em.dma("sy", xv, x0[t * P:(t + 1) * P, :], waits=xin_free[sl])
            toks = ln_tile(xv, scr32[:, 0:D], sl, scr16[:, 0:D], tin, war=scr16_war,
                           sq_war=sq_war)
            ev, pTb_free = transpose16(scr16, 0, xln1, T0, t * P,
                                       toks[0], pTb, pTb_free, True)
            xin_free[sl] = [toks[0]]
            scr16_war = [em.tok("pT")]
            sq_war = [toks[-1]]
        em.barrier()

        def rope_tile(ps, tab_t, cc0, ss0, tcol, w, out_ap, waits):
            t1 = rsc[:, 0:512]
            t2 = rsc[:, 512:1024]
            em.op("ve", lambda E: E.tensor_mul(
                t1[:, 0:w], ps[:, 0:w], tab_t[:, cc0 + tcol:cc0 + tcol + w]),
                waits=waits)
            em.op("ve", lambda E: E.tensor_mul(
                t2[0:64, 0:w], ps[64:128, 0:w],
                tab_t[64:128, ss0 + tcol:ss0 + tcol + w]))
            em.op("ve", lambda E: E.tensor_mul(
                t2[64:128, 0:w], ps[0:64, 0:w],
                tab_t[0:64, ss0 + tcol:ss0 + tcol + w]))
            em.op("ve", lambda E: E.tensor_tensor(
                out_ap[0:64, 0:w], t1[0:64, 0:w], t2[0:64, 0:w], ALU.subtract))
            return em.op("ve", lambda E: E.tensor_tensor(
                out_ap[64:128, 0:w], t1[64:128, 0:w], t2[64:128, 0:w], ALU.add),
                inc="pV")

        psum_free = {id(pA): None, id(pB): None}
        out_free = [None, None]
        i = 0
        for (proj, src_off, span, cc0, ss0, dst) in (
                ("q", 512, T1, QC0, QS0, q1d), ("k", 0, T0, KC0, KS0, k1d)):
            if proj == "k":
                load_wfull(wful, wka, war=[em.tok("pT")])
                em.wait("te", em.tok("dG"))
            for h in range(NH):
                for (t0c, w) in ttiles(span):
                    pb = pA if i % 2 == 0 else pB
                    mm = None
                    for dc in range(16):
                        wv = []
                        if dc == 0:
                            wv = [em.tok("dG")]
                            if psum_free[id(pb)]:
                                wv.append(psum_free[id(pb)])
                        mm = em.op("te", lambda E, dc=dc, h=h, t0c=t0c, w=w, pb=pb,
                                   src_off=src_off: E.matmul(
                            pb[:, 0:w],
                            wful[:, dc * D + h * P:dc * D + (h + 1) * P],
                            xln1[:, dc * T0 + src_off + t0c:dc * T0 + src_off + t0c + w],
                            start=(dc == 0), stop=(dc == 15)),
                            waits=wv, inc=("pT" if dc == 15 else None))
                    sl = i % 2
                    ob = qkout[:, sl * 512:(sl + 1) * 512]
                    wv = [mm] + ([out_free[sl]] if out_free[sl] else [])
                    rt = rope_tile(pb, tabs, cc0, ss0, t0c, w, ob, wv)
                    psum_free[id(pb)] = rt
                    out_free[sl] = em.dma(
                        "sc", dst[h * P:(h + 1) * P, t0c:t0c + w], ob[:, 0:w], waits=[rt])
                    i += 1
        load_wfull(wful, wva, war=[em.tok("pT")])
        em.wait("te", em.tok("dG"))
        vfree = [None, None]
        for rt_i in range(T0 // P):
            for dv in range(4):
                k = rt_i * 4 + dv
                pb = pA if k % 2 == 0 else pB
                mm = None
                for dc in range(16):
                    wv = ([psum_free[id(pb)]] if (psum_free[id(pb)] and dc == 0) else [])
                    mm = em.op("te", lambda E, dc=dc, rt_i=rt_i, dv=dv, pb=pb: E.matmul(
                        pb[:],
                        xln1[:, dc * T0 + rt_i * P:dc * T0 + (rt_i + 1) * P],
                        wful[:, dc * D + dv * 512:dc * D + (dv + 1) * 512],
                        start=(dc == 0), stop=(dc == 15)),
                        waits=wv, inc=("pT" if dc == 15 else None))
                sl = k % 2
                ob = qkout[:, sl * 512:(sl + 1) * 512]
                cp = em.op("sc", lambda E, pb=pb, ob=ob: E.copy(ob, pb[:]),
                           waits=[mm] + ([vfree[sl]] if vfree[sl] else []), inc="pS")
                psum_free[id(pb)] = cp
                vfree[sl] = em.dma(
                    "sc", v1d[rt_i * P:(rt_i + 1) * P, dv * 512:(dv + 1) * 512],
                    ob, waits=[cp])
        em.barrier()
        s0.close()

        sWO = ExitStack()
        wofl = sWO.enter_context(nc.sbuf_tensor("wofl", [P, 16 * D], BF16))
        tw_wo = load_wfull(wofl, woa)

        castk = {}
        cast_chunks = [
            ("sh", ws13c[0:16], ws13[0:16]),
            ("sh", ws13c[16:32], ws13[16:32]),
            ("sh2", ws2c[0:HS // 2, :], ws2[0:HS // 2, :]),
            ("sh2", ws2c[HS // 2:, :], ws2[HS // 2:, :]),
        ]
        for e in range(NE):
            cast_chunks.append((("ex", e), w13c[e], w13[e]))
            cast_chunks.append((("ex2", e), we2c[e], we2[e]))
        cast_iter = iter(cast_chunks)

        s2 = ExitStack()
        kwin = s2.enter_context(nc.sbuf_tensor("kwin", [P, 16 * 1024], BF16))
        vwin = s2.enter_context(nc.sbuf_tensor("vwin", [P, 8 * D], BF16))
        ptb = s2.enter_context(nc.sbuf_tensor("ptb", [P, 8 * 512], BF16))
        qh1 = s2.enter_context(nc.sbuf_tensor("qh1", [P, 512], BF16))
        mk1 = s2.enter_context(nc.sbuf_tensor("mk1", [P, 4 * 1024], F32))
        ssb = s2.enter_context(nc.sbuf_tensor("ssb", [P, 1024], F32))
        psb = s2.enter_context(nc.sbuf_tensor("psb", [P, 1024], F32))

        blocks = [
            (0, 512, 0, 1024),
            (512, 512, 512, 1024),
            (1024, 512, 1024, 1024),
            (1536, 256, 2048, 256),
        ]
        aofree = [None, None]
        kv_use = None
        m_use = None
        pC_free = None
        for bi, (qc0, qw, kc0r, kw) in enumerate(blocks):
            nkc = kw // P
            tk = None
            for g in range(2):
                src = k1d.rearrange("(c p) t -> p c t", p=P)[:, 8 * g:8 * g + 8, kc0r:kc0r + kw]
                dvw = kwin[:].rearrange("p (c t) -> p c t", t=1024)[:, 8 * g:8 * g + 8, 0:kw]
                tk = em.dma("sy", dvw, src,
                            waits=([kv_use] if (kv_use and g == 0) else []))
            src = v1d[kc0r:kc0r + kw, :].rearrange("(r p) d -> p r d", p=P)
            dvw = vwin[:].rearrange("p (r d) -> p r d", d=D)[:, 0:nkc, :]
            tv = em.dma("sy", dvw, src)
            if bi < 3:
                msrc = m1m[qc0:qc0 + qw, :].rearrange("(t p) k -> p t k", p=P)
                mdst = mk1[:].rearrange("p (t k) -> p t k", k=1024)[:, 0:qw // P, :]
            else:
                msrc = m1f.rearrange("(t p) k -> p t k", p=P)
                mdst = mk1[:].rearrange("p (t k) -> p t k", k=1024)[:, 0:2, 0:FRONT]
            tm = em.dma("sy", mdst, msrc, waits=([m_use] if m_use else []))
            kv_use = None
            m_use = None
            for h in range(NH):
                tq = em.dma("sy", qh1[:, 0:qw], q1d[h * P:(h + 1) * P, qc0:qc0 + qw],
                            waits=[em.tok("pT")])
                pt_ev = None
                texp_prev = None
                tpn_prev = None
                sbank_free = None
                for qt in range(qw // P):
                    smm = []
                    for kk in range((kw + 511) // 512):
                        wkk = min(512, kw - kk * 512)
                        pb = pA if kk == 0 else pB
                        wv = []
                        if kk == 0:
                            wv = [tk, tq] + ([sbank_free] if sbank_free else [])
                        smm.append(em.op("te", lambda E, qt=qt, kk=kk, wkk=wkk, pb=pb,
                                         h=h: E.matmul(
                            pb[:, 0:wkk], qh1[:, qt * P:(qt + 1) * P],
                            kwin[:, h * 1024 + kk * 512:h * 1024 + kk * 512 + wkk],
                            start=True, stop=True),
                            waits=wv, inc="pT"))
                    nks = (kw + 511) // 512
                    ta = None
                    for kk in range(nks):
                        wkk = min(512, kw - kk * 512)
                        pb = pA if kk == 0 else pB
                        wv = [smm[kk], tm] + ([texp_prev] if (texp_prev and kk == 0) else [])
                        ta = em.op("ve", lambda E, qt=qt, kk=kk, wkk=wkk, pb=pb: E.tensor_tensor(
                            ssb[:, kk * 512:kk * 512 + wkk], pb[:, 0:wkk],
                            mk1[:, qt * 1024 + kk * 512:qt * 1024 + kk * 512 + wkk],
                            ALU.add), waits=wv,
                            inc=("pV" if kk == nks - 1 else None))
                    sbank_free = ta
                    em.op("ve", lambda E, kw=kw: E.reduce_max(
                        stats[:, 32:33], ssb[:, 0:kw], axis=AX))
                    tneg = em.op("ve", lambda E: E.tensor_scalar_mul(
                        stats[:, 33:34], stats[:, 32:33], -1.0), inc="pV")
                    texp = em.op("sc", lambda E, kw=kw: E.activation(
                        psb[:, 0:kw], ssb[:, 0:kw], AF.Exp, bias=stats[:, 33:34]),
                        waits=[tneg] + ([tpn_prev] if tpn_prev else []), inc="pS")
                    texp_prev = texp
                    em.op("ve", lambda E, kw=kw: E.reduce_sum(
                        stats[:, 34:35], psb[:, 0:kw], axis=AX), waits=[texp])
                    trc = em.op("ve", lambda E: E.reciprocal(
                        stats[:, 35:36], stats[:, 34:35]), inc="pV")
                    tpn = em.op("ve", lambda E, kw=kw: E.tensor_scalar(
                        scr16[:, 0:kw], psb[:, 0:kw], stats[:, 35:36], None, ALU.mult),
                        waits=[trc, em.tok("pT")], inc="pV")
                    tpn_prev = tpn
                    for r0 in range(0, nkc, 8):
                        n = min(8, nkc - r0)
                        last = None
                        for icx in range(n):
                            kcix = r0 + icx
                            wv = [tpn] + ([pt_ev] if (pt_ev and icx == 0) else [])
                            last = em.op("te", lambda E, kcix=kcix, icx=icx: E.transpose(
                                pTb[:, icx * P:(icx + 1) * P],
                                scr16[:, kcix * P:(kcix + 1) * P], identb[:]),
                                waits=wv, inc=("pT" if icx == n - 1 else None))
                        dview = ptb[:].rearrange("p (c t) -> p c t", t=512)[
                            :, r0:r0 + n, qt * P:(qt + 1) * P]
                        sview = pTb[:, 0:n * P].rearrange("p (c t) -> p c t", t=P)
                        pt_ev = em.op("sc", lambda E, d=dview, s=sview: E.copy(d, s),
                                      waits=[last], inc="pS")
                mm = None
                for kcix in range(nkc):
                    wv = []
                    if kcix == 0:
                        wv = [pt_ev, tv] + ([pC_free] if pC_free else [])
                    mm = em.op("te", lambda E, kcix=kcix, h=h, qw=qw, nkc=nkc: E.matmul(
                        pC[:, 0:qw],
                        vwin[:, kcix * D + h * P:kcix * D + (h + 1) * P],
                        ptb[:, kcix * 512:kcix * 512 + qw],
                        start=(kcix == 0), stop=(kcix == nkc - 1)),
                        waits=wv, inc=("pT" if kcix == nkc - 1 else None))
                sl = h % 2
                ob = scr16[:, 1024 + sl * 512:1024 + (sl + 1) * 512]
                cp = em.op("sc", lambda E, ob=ob, qw=qw: E.copy(
                    ob[:, 0:qw], pC[:, 0:qw]),
                    waits=[mm] + ([aofree[sl]] if aofree[sl] else []), inc="pS")
                pC_free = cp
                aofree[sl] = em.dma("sc", aod[h * P:(h + 1) * P, qc0:qc0 + qw],
                                    ob[:, 0:qw], waits=[cp])
                nxt = next(cast_iter, None)
                if nxt is not None:
                    key, cdst, csrc = nxt
                    castk[key] = em.dma("gp", cdst, csrc, waits=[em.tok("pT")])
            kv_use = em.tok("pT")
            m_use = em.tok("pV")
        em.barrier()
        s2.close()

        def wo_residual(wo_ap, ao_dram, ao_sb, res_dram, res_row0, out_dram, rows,
                        wofull, aot, x0t, ysb, x1t, preload_tok=None):
            tw = preload_tok if preload_tok is not None else load_wfull(wofull, wo_ap)
            aofree = None
            x0free = [None] * 2
            pacc_free = {id(pE): None, id(pF): None}
            ysb_free = None
            pTf_free2 = None
            for (t0c, w) in ttiles(rows, 256):
                nsub = w // P
                ta = None
                if ao_dram is not None:
                    src = ao_dram.rearrange("(c p) t -> p c t", p=P)[:, :, t0c:t0c + w]
                    dvw = aot[:].rearrange("p (c t) -> p c t", t=512)[:, :, 0:w]
                    ta = em.dma("sy", dvw, src, waits=([aofree] if aofree else []))
                tx0 = []
                for sub in range(nsub):
                    tx0.append(em.dma(
                        "sy", x0t[:, sub * D:(sub + 1) * D],
                        res_dram[res_row0 + t0c + sub * P:res_row0 + t0c + (sub + 1) * P, :],
                        waits=([x0free[sub]] if x0free[sub] else [])))
                tt_last = None
                for oc in range(16):
                    pb = pE if oc % 2 == 0 else pF
                    mm = None
                    for ic in range(16):
                        wv = []
                        if ic == 0:
                            wv = [tw] + ([ta] if ta else [])
                            if pacc_free[id(pb)]:
                                wv.append(pacc_free[id(pb)])
                        if ao_dram is not None:
                            rhs = aot[:, ic * 512:ic * 512 + w]
                        else:
                            rhs = ao_sb[:, ic * 1024 + t0c:ic * 1024 + t0c + w]
                        mm = em.op("te", lambda E, ic=ic, oc=oc, pb=pb, rhs=rhs, w=w: E.matmul(
                            pb[:, 0:w],
                            wofull[:, ic * D + oc * P:ic * D + (oc + 1) * P],
                            rhs, start=(ic == 0), stop=(ic == 15)),
                            waits=wv, inc=("pT" if ic == 15 else None))
                    cp = em.op("sc", lambda E, pb=pb, w=w: E.copy(
                        ysb[:, 0:w], pb[:, 0:w]),
                        waits=[mm] + ([ysb_free] if ysb_free else []), inc="pS")
                    pacc_free[id(pb)] = cp
                    tlast_tr = None
                    for sub in range(nsub):
                        wv = [cp] + ([pTf_free2] if (pTf_free2 and sub == 0) else [])
                        tlast_tr = em.op("te", lambda E, sub=sub: E.transpose(
                            pTf[:, sub * P:(sub + 1) * P],
                            ysb[:, sub * P:(sub + 1) * P], identf[:]),
                            waits=wv, inc=("pT" if sub == nsub - 1 else None))
                    ysb_free = tlast_tr
                    for sub in range(nsub):
                        wv = [tlast_tr, tx0[sub]]
                        tt_last = em.op("ve", lambda E, sub=sub, oc=oc: E.tensor_tensor(
                            x1t[:, sub * D + oc * P:sub * D + (oc + 1) * P],
                            pTf[:, sub * P:(sub + 1) * P],
                            x0t[:, sub * D + oc * P:sub * D + (oc + 1) * P],
                            ALU.add), waits=wv,
                            inc=("pV" if sub == nsub - 1 else None))
                    pTf_free2 = tt_last
                for sub in range(nsub):
                    st = em.dma("sc", out_dram[t0c + sub * P:t0c + (sub + 1) * P, :],
                                x1t[:, sub * D:(sub + 1) * D], waits=[tt_last])
                    x0free[sub] = st
                if ao_dram is not None:
                    aofree = em.tok("pT")

        s2b = ExitStack()
        aot1 = s2b.enter_context(nc.sbuf_tensor("aot1", [P, 16 * 512], BF16))
        x0t1 = s2b.enter_context(nc.sbuf_tensor("x0t1", [P, 2 * D], F32))
        ysb1 = s2b.enter_context(nc.sbuf_tensor("ysb1", [P, 512], F32))
        x1t1 = s2b.enter_context(nc.sbuf_tensor("x1t1", [P, 2 * D], F32))
        wo_residual(woa, aod, None, x0, 512, X1d, T1, wofl, aot1, x0t1, ysb1, x1t1,
                    preload_tok=tw_wo)
        em.barrier()
        s2b.close()
        sWO.close()

        def emit_ffn(xin_dram, xout_dram, halves, tag, cw_slot0=0):
            sf = ExitStack()
            xln = sf.enter_context(nc.sbuf_tensor("f_xln" + tag, [P, 16 * 1024], BF16))
            X2H = sf.enter_context(nc.sbuf_tensor("f_X2H" + tag, [P, 16 * 1024], F32))
            g_sb = sf.enter_context(nc.sbuf_tensor("f_g" + tag, [P, 8 * 1024], BF16))
            w13r = sf.enter_context(nc.sbuf_tensor("f_w13" + tag, [P, 2 * 4096], BF16))
            we2r = sf.enter_context(nc.sbuf_tensor("f_we2" + tag, [P, 8 * D], BF16))
            cwbc = sf.enter_context(nc.sbuf_tensor("f_cwbc" + tag, [P, 1024], BF16))
            xinb = sf.enter_context(nc.sbuf_tensor("f_xin" + tag, [P, D], F32))
            x32f = sf.enter_context(nc.sbuf_tensor("f_x32f" + tag, [P, 16 * P], F32))
            t1sb = sf.enter_context(nc.sbuf_tensor("f_t1" + tag, [P, 512], F32))
            tmsb = sf.enter_context(nc.sbuf_tensor("f_tm" + tag, [P, 512], F32))
            xout = xinb

            for (h0, H) in halves:
                tiles = ttiles(H)
                xin_free = []
                scr16_war = []
                scr32_war = []
                rtr_war = []
                pTb_free = None
                pTf_free = None
                for t in range(H // P):
                    sl = t % 2
                    xv = xinb[:, 0:D]
                    tin = em.dma("sy", xv, xin_dram[h0 + t * P:h0 + (t + 1) * P, :],
                                 waits=xin_free)
                    toks = ln_tile(xv, scr32[:, 0:D], sl, scr16[:, 0:D], tin,
                                   war=scr16_war, xln32_ap=scr32[:, 0:D],
                                   sq_war=scr32_war, war32=rtr_war)
                    ev0, pTf_free = transpose16(xinb, 0, X2H, 1024, t * P,
                                                tin, pTf, pTf_free, False)
                    xin_free = [toks[0], em.tok("pT")]
                    ev2, pTb_free = transpose16(scr16, 0, xln, 1024, t * P,
                                                toks[0], pTb, pTb_free, True)
                    scr16_war = [em.tok("pT")]
                    ev3, pTf_free = transpose16(scr32, 0, x32f, P, 0,
                                                toks[1], pTf, pTf_free, False)
                    scr32_war = [em.tok("pT")]
                    mm = None
                    for dc in range(16):
                        mm = em.op("te", lambda E, dc=dc: E.matmul(
                            pD[:, 0:16], x32f[:, dc * P:(dc + 1) * P],
                            wrsb[:, dc * 16:(dc + 1) * 16],
                            start=(dc == 0), stop=(dc == 15)),
                            waits=([ev3] if dc == 0 else []),
                            inc=("pT" if dc == 15 else None))
                    em.op("ve", lambda E: E.reduce_max(stats[:, 40:41], pD[:, 0:16], axis=AX),
                          waits=[mm] + rtr_war)
                    tneg = em.op("ve", lambda E: E.tensor_scalar_mul(
                        stats[:, 41:42], stats[:, 40:41], -1.0), inc="pV")
                    texp = em.op("sc", lambda E: E.activation(
                        scr32[:, 0:16], pD[:, 0:16], AF.Exp, bias=stats[:, 41:42]),
                        waits=[tneg] + scr32_war, inc="pS")
                    rtr_war = [texp]
                    em.op("ve", lambda E: E.reduce_sum(stats[:, 42:43], scr32[:, 0:16], axis=AX),
                          waits=[texp])
                    em.op("ve", lambda E: E.reciprocal(stats[:, 43:44], stats[:, 42:43]))
                    em.op("ve", lambda E: E.tensor_scalar(
                        scr32[:, 0:16], scr32[:, 0:16], stats[:, 43:44], None, ALU.mult))
                    em.op("ve", lambda E: E.tensor_copy(scr32[:, 16:32], scr32[:, 0:16]))
                    em.op("ve", lambda E: E.memset(scr32[:, 32:48], 0.0))
                    for r in range(4):
                        em.op("ve", lambda E: E.reduce_max(stats[:, 44:45],
                                                           scr32[:, 16:32], axis=AX))
                        em.op("ve", lambda E: E.tensor_scalar(
                            scr32[:, 48:64], scr32[:, 16:32], stats[:, 44:45], None,
                            ALU.is_ge))
                        em.op("ve", lambda E: E.tensor_mul(
                            scr32[:, 64:80], scr32[:, 48:64], scr32[:, 0:16]))
                        em.op("ve", lambda E: E.tensor_add(
                            scr32[:, 32:48], scr32[:, 32:48], scr32[:, 64:80]))
                        em.op("ve", lambda E: E.scalar_tensor_tensor(
                            scr32[:, 16:32], scr32[:, 48:64], NEG, scr32[:, 16:32],
                            ALU.mult, ALU.add))
                    em.op("ve", lambda E: E.reduce_sum(stats[:, 45:46], scr32[:, 32:48], axis=AX))
                    em.op("ve", lambda E: E.reciprocal(stats[:, 46:47], stats[:, 45:46]))
                    tcw = em.op("ve", lambda E: E.tensor_scalar(
                        scr32[:, 32:48], scr32[:, 32:48], stats[:, 46:47], None, ALU.mult),
                        inc="pV")
                    ttr = em.op("te", lambda E: E.transpose(
                        pTf[0:16, 0:P], scr32[:, 32:48], identf[:]),
                        waits=[tcw, pTf_free], inc="pT")
                    pTf_free = em.op("sc", lambda E, t=t: E.copy(
                        cwT[0:16, t * P:(t + 1) * P], pTf[0:16, 0:P]),
                        waits=[ttr], inc="pS")
                    scr32_war = [toks[-1], tcw, em.tok("pT")]
                if _DEBUG:
                    em.dma("sc", cwdbg[cw_slot0 + halves.index((h0, H)), :, 0:H],
                           cwT[0:16, 0:H])
                em.barrier()

                units = [("sh", su) for su in range(4)] + [("ex", e) for e in range(NE)]
                w13_free = [[], []]
                we2_free = []
                uv_free = {id(pA): None, id(pB): None, id(pC): None, id(pD): None}
                acc_free = {id(pE): None, id(pF): None}
                g_war = []
                cwbc_war = []
                pTf_cw_free = None
                prev_g = None
                for uidx, (kind, ue) in enumerate(units):
                    tcwb = None
                    if kind == "ex":
                        last_cwb = pTf_cw_free
                        for (t0c, w) in tiles:
                            m1 = em.op("te", lambda E, t0c=t0c, w=w, ue=ue: E.matmul(
                                pTf[:, 0:w], selsb[0:16, ue * P:(ue + 1) * P],
                                cwT[0:16, t0c:t0c + w], start=True, stop=True),
                                waits=([last_cwb] if last_cwb else []),
                                inc="pT")
                            tcwb = em.op("sc", lambda E, t0c=t0c, w=w: E.copy(
                                cwbc[:, t0c:t0c + w], pTf[:, 0:w]),
                                waits=[m1] + (cwbc_war if t0c == 0 else []), inc="pS")
                            last_cwb = tcwb
                    glast = None
                    for hc in range(8):
                        sl = hc % 2
                        if kind == "ex":
                            wsrc = w13c[ue, hc].rearrange("u c p h -> p u c h")
                            ctok = castk[("ex", ue)]
                        else:
                            wsrc = ws13c[ue * 8 + hc].rearrange("u c p h -> p u c h")
                            ctok = castk["sh"]
                        wdst = w13r[:, sl * 4096:(sl + 1) * 4096].rearrange(
                            "p (u c h) -> p u c h", u=2, h=P)
                        tw = em.dma("sy" if hc % 2 == 0 else "sc", wdst, wsrc,
                                    waits=list(w13_free[sl]) + [ctok])
                        for ti, (t0c, w) in enumerate(tiles):
                            pu = pA if ti % 2 == 0 else pC
                            pv = pB if ti % 2 == 0 else pD
                            mmu = mmv = None
                            for dc in range(16):
                                wv = []
                                if dc == 0:
                                    wv = [tw] + ([uv_free[id(pu)]] if uv_free[id(pu)] else [])
                                mmu = em.op("te", lambda E, dc=dc, w=w, t0c=t0c, pu=pu,
                                            sl=sl: E.matmul(
                                    pu[:, 0:w],
                                    w13r[:, sl * 4096 + dc * P:sl * 4096 + (dc + 1) * P],
                                    xln[:, dc * 1024 + t0c:dc * 1024 + t0c + w],
                                    start=(dc == 0), stop=(dc == 15)),
                                    waits=wv, inc=("pT" if dc == 15 else None))
                            for dc in range(16):
                                wv = ([uv_free[id(pv)]] if (uv_free[id(pv)] and dc == 0) else [])
                                mmv = em.op("te", lambda E, dc=dc, w=w, t0c=t0c, pv=pv,
                                            sl=sl: E.matmul(
                                    pv[:, 0:w],
                                    w13r[:, sl * 4096 + 2048 + dc * P:sl * 4096 + 2048 + (dc + 1) * P],
                                    xln[:, dc * 1024 + t0c:dc * 1024 + t0c + w],
                                    start=(dc == 0), stop=(dc == 15)),
                                    waits=wv, inc=("pT" if dc == 15 else None))
                            tsl = em.op("sc", lambda E, pu=pu, w=w: E.activation(
                                t1sb[:, 0:w], pu[:, 0:w], AF.Silu),
                                waits=[mmu] + ([prev_g] if prev_g else []) + g_war,
                                inc="pS")
                            uv_free[id(pu)] = tsl
                            if kind == "ex":
                                ttm = em.op("ve", lambda E, pv=pv, w=w, t0c=t0c: E.tensor_tensor(
                                    tmsb[:, 0:w], pv[:, 0:w], cwbc[:, t0c:t0c + w],
                                    ALU.mult),
                                    waits=[mmv, tcwb] + ([prev_g] if prev_g else []),
                                    inc="pV")
                                uv_free[id(pv)] = ttm
                                glast = em.op("gp", lambda E, hc=hc, t0c=t0c, w=w: E.tensor_tensor(
                                    g_sb[:, hc * 1024 + t0c:hc * 1024 + t0c + w],
                                    t1sb[:, 0:w], tmsb[:, 0:w], ALU.mult),
                                    waits=[tsl, ttm], inc="pP")
                            else:
                                glast = em.op("ve", lambda E, hc=hc, t0c=t0c, w=w,
                                              pv=pv: E.tensor_tensor(
                                    g_sb[:, hc * 1024 + t0c:hc * 1024 + t0c + w],
                                    t1sb[:, 0:w], pv[:, 0:w], ALU.mult),
                                    waits=[tsl, mmv], inc="pV")
                                uv_free[id(pv)] = glast
                            prev_g = glast
                        w13_free[sl] = [em.tok("pT")]
                    g_war = []
                    tw2s = []
                    for g2 in range(2):
                        if kind == "ex":
                            src = we2c[ue, g2 * 512:(g2 + 1) * 512, :].rearrange(
                                "(c p) o -> p c o", p=P)
                            ctok2 = castk[("ex2", ue)]
                        else:
                            src = ws2c[ue * 1024 + g2 * 512:ue * 1024 + (g2 + 1) * 512, :].rearrange(
                                "(c p) o -> p c o", p=P)
                            ctok2 = castk["sh2"]
                        dvw = we2r[:].rearrange("p (c o) -> p c o", o=D)[:, g2 * 4:(g2 + 1) * 4, :]
                        tw2s.append(em.dma("sc" if g2 == 0 else "sy", dvw, src,
                                           waits=list(we2_free) + [ctok2]))
                    for (t0c, w) in tiles:
                        for oc in range(16):
                            pb = pE if oc % 2 == 0 else pF
                            mm = None
                            for hc in range(8):
                                wv = []
                                if hc == 0:
                                    wv = tw2s + [glast]
                                    if acc_free[id(pb)]:
                                        wv.append(acc_free[id(pb)])
                                mm = em.op("te", lambda E, hc=hc, oc=oc, t0c=t0c, w=w,
                                           pb=pb: E.matmul(
                                    pb[:, 0:w],
                                    we2r[:, hc * D + oc * P:hc * D + (oc + 1) * P],
                                    g_sb[:, hc * 1024 + t0c:hc * 1024 + t0c + w],
                                    start=(hc == 0), stop=(hc == 7)),
                                    waits=wv, inc=("pT" if hc == 7 else None))
                            if kind == "sh":
                                ev = em.op("ve", lambda E, oc=oc, t0c=t0c, w=w,
                                           pb=pb: E.scalar_tensor_tensor(
                                    X2H[:, oc * 1024 + t0c:oc * 1024 + t0c + w],
                                    pb[:, 0:w], 0.2,
                                    X2H[:, oc * 1024 + t0c:oc * 1024 + t0c + w],
                                    ALU.mult, ALU.add), waits=[mm], inc="pV")
                            else:
                                ev = em.op("ve", lambda E, oc=oc, t0c=t0c, w=w,
                                           pb=pb: E.tensor_tensor(
                                    X2H[:, oc * 1024 + t0c:oc * 1024 + t0c + w],
                                    X2H[:, oc * 1024 + t0c:oc * 1024 + t0c + w],
                                    pb[:, 0:w], ALU.add), waits=[mm], inc="pV")
                            acc_free[id(pb)] = ev
                    g_war = [em.tok("pT")]
                    we2_free = [em.tok("pT")]
                    cwbc_war = [em.tok("pV")]
                    pTf_cw_free = em.tok("pS")
                em.barrier()
                pTf_free = None
                xo_free = []
                for t in range(H // P):
                    last_cp = None
                    for r0 in range(0, 16, 4):
                        last = None
                        for i2 in range(4):
                            oc = r0 + i2
                            wv = []
                            if i2 == 0:
                                wv = ([pTf_free] if pTf_free else []) + \
                                     (xo_free if r0 == 0 else [])
                            last = em.op("te", lambda E, oc=oc, i2=i2, t=t: E.transpose(
                                pTf[:, i2 * P:(i2 + 1) * P],
                                X2H[:, oc * 1024 + t * P:oc * 1024 + (t + 1) * P],
                                identf[:]), waits=wv, inc=("pT" if i2 == 3 else None))
                        last_cp = em.op("sc", lambda E, r0=r0: E.copy(
                            xout[:, r0 * P:(r0 + 4) * P],
                            pTf[:, 0:512]), waits=[last], inc="pS")
                        pTf_free = last_cp
                    st = em.dma("sc", xout_dram[h0 + t * P:h0 + (t + 1) * P, :],
                                xout[:, 0:D], waits=[last_cp])
                    xo_free = [st]
                em.barrier()
            sf.close()

        emit_ffn(X1d, X2d, [(0, 1024), (1024, 768)], "2", cw_slot0=0)

        s4keep = ExitStack()
        s4 = ExitStack()
        s4p = ExitStack()
        xln3 = s4keep.enter_context(nc.sbuf_tensor("xln3", [P, 16 * T1], BF16))
        k3 = s4.enter_context(nc.sbuf_tensor("k3", [P, 4 * T1], BF16))
        q3 = s4.enter_context(nc.sbuf_tensor("q3", [P, 16 * 1024], BF16))
        v3 = s4.enter_context(nc.sbuf_tensor("v3", [P, 14 * 512], BF16))
        tab3 = s4p.enter_context(nc.sbuf_tensor("tab3", [P, 2 * TOWN + 2 * T1], BF16))
        xinb3 = s4p.enter_context(nc.sbuf_tensor("xinb3", [P, 2 * D], F32))
        wsl3 = s4p.enter_context(nc.sbuf_tensor("wsl3", [P, 2 * D], BF16))
        wv3sb = s4p.enter_context(nc.sbuf_tensor("wv3sb", [P, 16 * 512], BF16))
        rsc3 = s4p.enter_context(nc.sbuf_tensor("rsc3", [P, 2 * 512], F32))

        C3Q, S3Q, C3K, S3K = 0, TOWN, 2 * TOWN, 2 * TOWN + T1
        em.dma("gp", tab3[:, C3Q:C3Q + TOWN], qc3)
        em.dma("gp", tab3[:, S3Q:S3Q + TOWN], qs3)
        em.dma("gp", tab3[:, C3K:C3K + T1], kc3)
        em.dma("gp", tab3[:, S3K:S3K + T1], ks3)

        xin_free = [[], []]
        scr16_war = []
        sq_war = []
        pTb_free = None
        for t in range(T1 // P):
            sl = t % 2
            xv = xinb3[:, sl * D:(sl + 1) * D]
            tin = em.dma("sy", xv, X2d[t * P:(t + 1) * P, :], waits=xin_free[sl])
            toks = ln_tile(xv, scr32[:, 0:D], sl, scr16[:, 0:D], tin, war=scr16_war,
                           sq_war=sq_war)
            ev, pTb_free = transpose16(scr16, 0, xln3, T1, t * P,
                                       toks[0], pTb, pTb_free, True)
            xin_free[sl] = [toks[0]]
            scr16_war = [em.tok("pT")]
            sq_war = [toks[-1]]
        em.barrier()

        def rope3(ps, cc0, ss0, tcol, w, out_ap, waits):
            t1 = rsc3[:, 0:512]
            t2 = rsc3[:, 512:1024]
            em.op("ve", lambda E: E.tensor_mul(
                t1[:, 0:w], ps[:, 0:w], tab3[:, cc0 + tcol:cc0 + tcol + w]),
                waits=waits)
            em.op("ve", lambda E: E.tensor_mul(
                t2[0:64, 0:w], ps[64:128, 0:w], tab3[64:128, ss0 + tcol:ss0 + tcol + w]))
            em.op("ve", lambda E: E.tensor_mul(
                t2[64:128, 0:w], ps[0:64, 0:w], tab3[0:64, ss0 + tcol:ss0 + tcol + w]))
            em.op("ve", lambda E: E.tensor_tensor(
                out_ap[0:64, 0:w], t1[0:64, 0:w], t2[0:64, 0:w], ALU.subtract))
            return em.op("ve", lambda E: E.tensor_tensor(
                out_ap[64:128, 0:w], t1[64:128, 0:w], t2[64:128, 0:w], ALU.add),
                inc="pV")

        wfree = [[], []]
        pfree = {id(pA): None, id(pB): None}
        for h in range(NH):
            sl = h % 2
            tw = em.dma("gp", wsl3[:, sl * D:(sl + 1) * D].rearrange(
                "p (c h) -> p c h", h=P), wq3[h].rearrange("c p h -> p c h"),
                waits=wfree[sl])
            for (t0c, w) in ttiles(TOWN):
                pb = pA if (t0c // 512) % 2 == 0 else pB
                mm = None
                for dc in range(16):
                    wv = []
                    if dc == 0:
                        wv = [tw] + ([pfree[id(pb)]] if pfree[id(pb)] else [])
                    mm = em.op("te", lambda E, dc=dc, sl=sl, t0c=t0c, w=w, pb=pb: E.matmul(
                        pb[:, 0:w],
                        wsl3[:, sl * D + dc * P:sl * D + (dc + 1) * P],
                        xln3[:, dc * T1 + 512 + t0c:dc * T1 + 512 + t0c + w],
                        start=(dc == 0), stop=(dc == 15)),
                        waits=wv, inc=("pT" if dc == 15 else None))
                rt = rope3(pb, C3Q, S3Q, t0c, w,
                           q3[:, h * 1024 + t0c:h * 1024 + t0c + w], [mm])
                pfree[id(pb)] = rt
            wfree[sl] = [em.tok("pT")]
        for kv in range(NKV):
            sl = kv % 2
            tw = em.dma("gp", wsl3[:, sl * D:(sl + 1) * D].rearrange(
                "p (c h) -> p c h", h=P), wk3[kv].rearrange("c p h -> p c h"),
                waits=wfree[sl])
            for (t0c, w) in ttiles(T1):
                pb = pA if (t0c // 512) % 2 == 0 else pB
                mm = None
                for dc in range(16):
                    wv = []
                    if dc == 0:
                        wv = [tw] + ([pfree[id(pb)]] if pfree[id(pb)] else [])
                    mm = em.op("te", lambda E, dc=dc, sl=sl, t0c=t0c, w=w, pb=pb: E.matmul(
                        pb[:, 0:w],
                        wsl3[:, sl * D + dc * P:sl * D + (dc + 1) * P],
                        xln3[:, dc * T1 + t0c:dc * T1 + t0c + w],
                        start=(dc == 0), stop=(dc == 15)),
                        waits=wv, inc=("pT" if dc == 15 else None))
                rt = rope3(pb, C3K, S3K, t0c, w,
                           k3[:, kv * T1 + t0c:kv * T1 + t0c + w], [mm])
                pfree[id(pb)] = rt
            wfree[sl] = [em.tok("pT")]
        tv3 = None
        for g2 in range(4):
            src = wv3.rearrange("(c p) v -> p c v", p=P)[:, 4 * g2:4 * g2 + 4, :]
            dvw = wv3sb[:].rearrange("p (c v) -> p c v", v=512)[:, 4 * g2:4 * g2 + 4, :]
            tv3 = em.dma("gp", dvw, src)
        v3done = None
        for rt_i in range(T1 // P):
            pb = pA if rt_i % 2 == 0 else pB
            mm = None
            for dc in range(16):
                wv = []
                if dc == 0:
                    wv = [tv3] + ([pfree[id(pb)]] if pfree[id(pb)] else [])
                mm = em.op("te", lambda E, dc=dc, rt_i=rt_i, pb=pb: E.matmul(
                    pb[:],
                    xln3[:, dc * T1 + rt_i * P:dc * T1 + (rt_i + 1) * P],
                    wv3sb[:, dc * 512:(dc + 1) * 512],
                    start=(dc == 0), stop=(dc == 15)),
                    waits=wv, inc=("pT" if dc == 15 else None))
            cp = em.op("sc", lambda E, rt_i=rt_i, pb=pb: E.copy(
                v3[:, rt_i * 512:(rt_i + 1) * 512], pb[:]), waits=[mm], inc="pS")
            pfree[id(pb)] = cp
            v3done = cp
        em.barrier()
        s4p.close()

        s4c = ExitStack()
        m3sb = s4c.enter_context(nc.sbuf_tensor("m3sb", [P, 4 * T1], F32))
        ssb3 = s4c.enter_context(nc.sbuf_tensor("ssb3", [P, T1], F32))
        psb3 = s4c.enter_context(nc.sbuf_tensor("psb3", [P, T1], F32))
        AOC = 8 * 1024
        pt_ar = xln3
        nkc3 = T1 // P

        m_use3 = None
        pE_free = None
        for qg in range(2):
            msrc = m3.rearrange("(t p) k -> p t k", p=P)[:, qg * 4:(qg + 1) * 4, :]
            mdst = m3sb[:].rearrange("p (t k) -> p t k", k=T1)
            tm3 = em.dma("sy", mdst, msrc, waits=([m_use3] if m_use3 else []))
            for h in range(NH):
                kvh = h // 4
                pt_ev = None
                texp_prev = None
                tpn_prev = None
                sbank_free = None
                for qt in range(4):
                    qcol = qg * 512 + qt * P
                    banks = [pA, pB, pC, pD]
                    smm = []
                    for kk, (k0, kwd) in enumerate(ttiles(T1)):
                        pb = banks[kk]
                        wv = []
                        if kk == 0:
                            wv = [tm3] + ([sbank_free] if sbank_free else [])
                        smm.append(em.op("te", lambda E, k0=k0, kwd=kwd, pb=pb, h=h,
                                         qcol=qcol, kvh=kvh: E.matmul(
                            pb[:, 0:kwd],
                            q3[:, h * 1024 + qcol:h * 1024 + qcol + P],
                            k3[:, kvh * T1 + k0:kvh * T1 + k0 + kwd],
                            start=True, stop=True),
                            waits=wv, inc="pT"))
                    ta = None
                    for kk, (k0, kwd) in enumerate(ttiles(T1)):
                        pb = banks[kk]
                        wv = [smm[kk]] + ([texp_prev] if (texp_prev and kk == 0) else [])
                        ta = em.op("ve", lambda E, k0=k0, kwd=kwd, pb=pb, qt=qt: E.tensor_tensor(
                            ssb3[:, k0:k0 + kwd], pb[:, 0:kwd],
                            m3sb[:, qt * T1 + k0:qt * T1 + k0 + kwd], ALU.add),
                            waits=wv, inc=("pV" if kk == 3 else None))
                    sbank_free = ta
                    em.op("ve", lambda E: E.reduce_max(stats[:, 48:49], ssb3[:], axis=AX))
                    tneg = em.op("ve", lambda E: E.tensor_scalar_mul(
                        stats[:, 49:50], stats[:, 48:49], -1.0), inc="pV")
                    texp = em.op("sc", lambda E: E.activation(
                        psb3[:], ssb3[:], AF.Exp, bias=stats[:, 49:50]),
                        waits=[tneg] + ([tpn_prev] if tpn_prev else []), inc="pS")
                    texp_prev = texp
                    em.op("ve", lambda E: E.reduce_sum(stats[:, 50:51], psb3[:], axis=AX),
                          waits=[texp])
                    trc = em.op("ve", lambda E: E.reciprocal(stats[:, 51:52],
                                                             stats[:, 50:51]), inc="pV")
                    tpn = em.op("ve", lambda E: E.tensor_scalar(
                        scr16[:, 0:T1], psb3[:], stats[:, 51:52], None, ALU.mult),
                        waits=[trc, em.tok("pT")], inc="pV")
                    tpn_prev = tpn
                    for r0 in range(0, nkc3, 8):
                        n = min(8, nkc3 - r0)
                        last = None
                        for icx in range(n):
                            kcix = r0 + icx
                            wv = [tpn] + ([pt_ev] if (pt_ev and icx == 0) else [])
                            last = em.op("te", lambda E, kcix=kcix, icx=icx: E.transpose(
                                pTb[:, icx * P:(icx + 1) * P],
                                scr16[:, kcix * P:(kcix + 1) * P], identb[:]),
                                waits=wv, inc=("pT" if icx == n - 1 else None))
                        dview = pt_ar[:].rearrange("p (c t) -> p c t", t=512)[
                            :, r0:r0 + n, qt * P:(qt + 1) * P]
                        sview = pTb[:, 0:n * P].rearrange("p (c t) -> p c t", t=P)
                        pt_ev = em.op("sc", lambda E, d=dview, s=sview: E.copy(d, s),
                                      waits=[last], inc="pS")
                mm = None
                for kcix in range(nkc3):
                    wv = []
                    if kcix == 0:
                        wv = [pt_ev, v3done] + ([pE_free] if pE_free else [])
                    mm = em.op("te", lambda E, kcix=kcix, kvh=kvh, nkc3=nkc3: E.matmul(
                        pE[:],
                        v3[:, kcix * 512 + kvh * P:kcix * 512 + (kvh + 1) * P],
                        pt_ar[:, kcix * 512:(kcix + 1) * 512],
                        start=(kcix == 0), stop=(kcix == nkc3 - 1)),
                        waits=wv, inc=("pT" if kcix == nkc3 - 1 else None))
                pE_free = em.op("sc", lambda E, h=h, qg=qg: E.copy(
                    pt_ar[:, AOC + h * 1024 + qg * 512:AOC + h * 1024 + qg * 512 + 512],
                    pE[:]), waits=[mm], inc="pS")
            m_use3 = em.tok("pV")
        em.barrier()
        s4c.close()
        s4.close()

        s4d = ExitStack()
        wo3f = s4d.enter_context(nc.sbuf_tensor("wo3f", [P, 16 * D], BF16))
        x0t3 = s4d.enter_context(nc.sbuf_tensor("x0t3", [P, 2 * D], F32))
        ysb3 = s4d.enter_context(nc.sbuf_tensor("ysb3", [P, 512], F32))
        x1t3 = s4d.enter_context(nc.sbuf_tensor("x1t3", [P, 2 * D], F32))
        wo_residual(wo3, None, pt_ar[:, AOC:AOC + 16 * 1024], X2d, 512, X3d, TOWN,
                    wo3f, None, x0t3, ysb3, x1t3)
        em.barrier()
        s4d.close()
        s4keep.close()

        emit_ffn(X3d, y, [(0, 1024)], "4", cw_slot0=2)

        em.wait("sc", em.tok("dA"))
        em.wait("sy", em.tok("dS"))
        em.wait("gp", em.tok("dG"))

        def replay(name):
            def run(E):
                for f in em.q[name]:
                    f(E)
            return run

        with nc.Block() as blk:
            blk.sync(replay("sy"))
            blk.gpsimd(replay("gp"))
            blk.vector(replay("ve"))
            blk.scalar(replay("sc"))
            blk.tensor(replay("te"))
    return nc



def _rope_tables(pos, scale):
    inv = 1.0 / (10000.0 ** (np.arange(0, HD, 2, dtype=np.float64) / HD))
    ang = np.asarray(pos, np.float64)[None, :] * inv[:, None]
    c = np.cos(ang)
    s = np.sin(ang)
    C2 = np.concatenate([c, c], 0) * scale
    S2 = np.concatenate([s, s], 0) * scale
    return np.ascontiguousarray(C2, np.float32), np.ascontiguousarray(S2, np.float32)


def _headperm(w, nh):
    perm = np.r_[np.arange(0, 128, 2), np.arange(1, 128, 2)]
    return np.ascontiguousarray(
        w.reshape(w.shape[0], nh, 128)[:, :, perm].reshape(w.shape[0], -1))


def _prep_inputs(kw):
    f32 = np.float32
    x = np.ascontiguousarray(np.asarray(kw["L0_attn_a_x"], f32))
    Wq, Wk, Wv, Wo = (np.asarray(kw["L0_attn_a_" + n], f32)
                      for n in ("Wq", "Wk", "Wv", "Wo"))
    wq, wk, wv, wo = (np.asarray(kw["L0_attn_b_" + n], f32)
                      for n in ("wq", "wk", "wv", "wo"))
    wrt = np.asarray(kw["L0_ffn_w_router"], f32)
    we1, we3, we2 = (np.asarray(kw["L0_ffn_" + n], f32) for n in ("we1", "we3", "we2"))
    ws1, ws3, ws2 = (np.asarray(kw["L0_ffn_" + n], f32) for n in ("ws1", "ws3", "ws2"))

    shared = {
        "wqa": _headperm(np.ascontiguousarray(Wq.T), NH),
        "wka": _headperm(np.ascontiguousarray(Wk.T), NH),
        "wva": np.ascontiguousarray(Wv.T),
        "woa": np.ascontiguousarray(Wo.T),
        "wq3": np.ascontiguousarray(
            _headperm(wq, NH).reshape(16, P, NH, HD).transpose(2, 0, 1, 3)),
        "wk3": np.ascontiguousarray(
            _headperm(wk, NKV).reshape(16, P, NKV, HD).transpose(2, 0, 1, 3)),
        "wv3": np.ascontiguousarray(wv),
        "wo3": np.ascontiguousarray(wo),
        "wr": np.ascontiguousarray(wrt.reshape(16, P, 16)),
        "w13": np.ascontiguousarray(np.stack(
            [we1.reshape(NE, 16, P, 8, P).transpose(0, 3, 1, 2, 4),
             we3.reshape(NE, 16, P, 8, P).transpose(0, 3, 1, 2, 4)], axis=2)),
        "ws13": np.ascontiguousarray(np.stack(
            [ws1.reshape(16, P, 32, P).transpose(2, 0, 1, 3),
             ws3.reshape(16, P, 32, P).transpose(2, 0, 1, 3)], axis=1)),
        "we2": np.ascontiguousarray(we2),
        "ws2": np.ascontiguousarray(ws2),
    }
    sel = np.zeros((16, D), f32)
    for e in range(16):
        sel[e, e * P:(e + 1) * P] = 0.8
    shared["sel"] = sel

    in_maps = []
    for c in range(NCORES):
        b, j = divmod(c, 4)
        q0 = TOWN * j
        pos0 = np.concatenate([np.arange(q0 - 1024, q0 + 1024), np.arange(FRONT)])
        pos1 = pos0[512:]
        posq = np.arange(q0, q0 + TOWN)
        x0c = np.zeros((T0, D), f32)
        lo = q0 - 1024
        s_lo = max(0, lo)
        x0c[(s_lo - lo):2048] = x[b, s_lo:q0 + 1024]
        x0c[2048:] = x[b, :FRONT]

        sc = 1.0 / np.sqrt(HD)
        qc1_, qs1_ = _rope_tables(np.maximum(pos1, 0), sc)
        kc1_, ks1_ = _rope_tables(np.maximum(pos0, 0), 1.0)
        qc3_, qs3_ = _rope_tables(posq, sc)
        kc3_, ks3_ = _rope_tables(np.maximum(pos1, 0), 1.0)

        m1m = np.full((1536, 1024), NEG, f32)
        for n in range(3):
            qp = pos1[n * 512:(n + 1) * 512][:, None]
            kp = (q0 - 1024 + n * 512 + np.arange(1024))[None, :]
            ok = (kp <= qp) & (kp > qp - WIN) & (kp >= 0)
            m1m[n * 512:(n + 1) * 512] = np.where(ok, 0.0, NEG)
        qf = np.arange(FRONT)[:, None]
        kf = np.arange(FRONT)[None, :]
        m1f = np.where(kf <= qf, 0.0, NEG).astype(f32)

        qp = posq[:, None]
        kp_main = (q0 - 512 + np.arange(1536))[None, :]
        band = (kp_main <= qp) & (kp_main > qp - WIN)
        fok = (kp_main < FRONT) & (kp_main <= qp - WIN)
        main_ok = (band | fok) & (kp_main >= 0)
        kp_f = np.arange(FRONT)[None, :]
        bandf = (kp_f <= qp) & (kp_f > qp - WIN)
        fokf = (kp_f < FRONT) & (kp_f <= qp - WIN)
        front_ok = (bandf | fokf) if j > 0 else np.zeros((TOWN, FRONT), bool)
        m3c = np.concatenate([np.where(main_ok, 0.0, NEG),
                              np.where(front_ok, 0.0, NEG)], axis=1).astype(f32)

        im = dict(shared)
        im.update(x0=x0c, m1m=m1m, m1f=m1f, m3=m3c,
                  qc1=qc1_, qs1=qs1_, kc1=kc1_, ks1=ks1_,
                  qc3=qc3_, qs3=qs3_, kc3=kc3_, ks3=ks3_)
        in_maps.append(im)
    return in_maps


def kernel(**kw):
    in_maps = _prep_inputs(kw)
    nc = build_program()
    res = run_bass_kernel_spmd(nc, in_maps, core_ids=list(range(NCORES)))
    out = np.empty((2, SEQ, D), np.float32)
    for c in range(NCORES):
        b, j = divmod(c, 4)
        out[b, TOWN * j:TOWN * (j + 1)] = res.results[c]["y"]
    return out
